# version 05
# baseline (speedup 1.0000x reference)
import numpy as np
import ml_dtypes
import contextlib
import concourse.bass as bass
import concourse.mybir as mybir
from concourse.bass_utils import run_bass_kernel_spmd

F32 = mybir.dt.float32
BF16 = mybir.dt.bfloat16
AF = mybir.ActivationFunctionType
ALU = mybir.AluOpType
AX = mybir.AxisListType
S8 = float(1.0 / np.sqrt(8.0))
NW = 1024 * 4096


class Ctx:
    def __init__(self, nc):
        self.nc = nc
        self.es = contextlib.ExitStack()
        self._sem_counts = {}
        self._sems = {}

    def sb(self, name, shape, dtype=F32, es=None):
        return (es or self.es).enter_context(self.nc.sbuf_tensor(name, shape, dtype))

    def ps(self, name, shape, dtype=F32):
        return self.es.enter_context(self.nc.psum_tensor(name, shape, dtype))

    def sem(self, name):
        if name not in self._sems:
            self._sems[name] = self.es.enter_context(self.nc.semaphore(name))
            self._sem_counts[name] = 0
        return self._sems[name]

    def bump(self, name, k=1):
        self._sem_counts[name] += k
        return self._sem_counts[name]

    def val(self, name):
        return self._sem_counts.get(name, 0)

    def mk(self, nm):
        return (nm, self.bump(nm))


ENG_SEMS = ("qp", "qv", "qs", "qg")


def wait_marks(eng, ct, marks):
    best = {}
    for m in marks:
        if m is None:
            continue
        nm, v = m
        best[nm] = max(best.get(nm, 0), v)
    for nm, v in best.items():
        if v > 0:
            eng.wait_ge(ct.sem(nm), v)


def all_marks(ct, exclude=()):
    return [(nm, ct.val(nm)) for nm in ENG_SEMS
            if nm not in exclude and ct.val(nm) > 0]


def emit_barrier(ct, blk):
    marks = {nm: ct.val(nm) for nm in ENG_SEMS}

    @blk.tensor
    def _(t):
        wait_marks(t, ct, [(nm, v) for nm, v in marks.items() if nm != "qp"])
        t.drain().then_inc(ct.sem("qp"), 1)
    ct.bump("qp")

    @blk.vector
    def _(v):
        wait_marks(v, ct, [(nm, v2) for nm, v2 in marks.items() if nm != "qv"])
        v.drain().then_inc(ct.sem("qv"), 1)
    ct.bump("qv")

    @blk.scalar
    def _(s):
        wait_marks(s, ct, [(nm, v) for nm, v in marks.items() if nm != "qs"])
        s.drain().then_inc(ct.sem("qs"), 1)
    ct.bump("qs")

    @blk.gpsimd
    def _(g):
        wait_marks(g, ct, [(nm, v) for nm, v in marks.items() if nm != "qg"])
        g.drain().then_inc(ct.sem("qg"), 1)
    ct.bump("qg")


def hi_lo(v):
    hi = v.astype(ml_dtypes.bfloat16).astype(np.float32)
    lo = (v - hi).astype(ml_dtypes.bfloat16).astype(np.float32)
    return hi, lo


def make_consts():
    identf = np.eye(128, dtype=np.float32)
    ones_f = np.ones((65, 128), np.float32)
    ones_bf = np.ones((128, 1), np.float32)
    ones_cf = np.ones((128, 1), np.float32)
    ind_sc = np.zeros((128, 8), np.float32)
    ind_sc[0:64, 0] = 1; ind_sc[64:128, 1] = 1
    ind_sc[0:64, 6] = 1; ind_sc[64:128, 7] = 1
    ind_bc = np.zeros((4, 256), np.float32)
    ind_bc[0, 0:64] = 1; ind_bc[1, 64:128] = 1
    ind_bc[2, 128:192] = 1; ind_bc[3, 192:256] = 1
    sel8 = np.zeros((8, 1024), np.float32)
    for e in range(8):
        sel8[e, e * 128:(e + 1) * 128] = 1.0
    mtri = np.where(np.arange(128)[None, :] <= np.arange(128)[:, None], 0.0,
                    -1e30).astype(np.float32)
    mtriT = np.ascontiguousarray(mtri.T)
    k_ar = np.arange(2048, dtype=np.float32)
    qaug = np.zeros((16, 4, 2048), np.float32)
    kaug = np.zeros((16, 6, 2048), np.float32)
    for h in range(16):
        slope = 2.0 ** (-(h + 1) / 2.0)
        khi, klo = hi_lo(slope * k_ar)
        qhi, qlo = hi_lo(-slope * k_ar)
        qaug[h, 0] = 1.0; qaug[h, 1] = 1.0; qaug[h, 2] = qhi; qaug[h, 3] = qlo
        kaug[h, 0] = 1.0; kaug[h, 1] = 1.0; kaug[h, 2] = khi; kaug[h, 3] = klo
        kaug[h, 4] = 1.0; kaug[h, 5] = 1.0
    return dict(identf=identf, ones_f=ones_f, ones_bf=ones_bf, ones_cf=ones_cf,
                ind_sc=ind_sc, ind_bc=ind_bc, sel8=sel8, mtri=mtri, mtriT=mtriT,
                qaug=qaug.reshape(64, 2048), kaug=kaug.reshape(96, 2048))


def emit_postln(ct, blk, x_fm, xb, cons, pA, pB, sqt, abst, xb_marks):
    mk = ct.mk
    sq_free = None
    sums_marks = []
    for jt in range(4):
        @blk.scalar
        def _(s, jt=jt, ms=list(xb_marks), fr=sq_free):
            wait_marks(s, ct, ms + ([fr] if fr else []))
            for c in range(8):
                s.activation(out=sqt[:, c * 512:(c + 1) * 512],
                             in_=xb[:, c * 2048 + jt * 512: c * 2048 + (jt + 1) * 512],
                             func=AF.Square)
            s.drain().then_inc(ct.sem("qs"), 1)
        sq_mark = mk("qs")

        @blk.tensor
        def _(t, jt=jt, sq_mark=sq_mark, ms=list(xb_marks),
              pre=all_marks(ct, exclude=("qp",)) if jt == 0 else []):
            t.wait_ge(ct.sem("ldg"), 80)
            wait_marks(t, ct, [sq_mark] + ms + pre)
            for c in range(8):
                t.matmul(pA[0:1, 0:512], cons["ones_bf"][:, 0:1],
                         xb[:, c * 2048 + jt * 512: c * 2048 + (jt + 1) * 512],
                         start=(c == 0), stop=(c == 7))
            for c in range(8):
                t.matmul(pB[0:1, 0:512], cons["ones_bf"][:, 0:1],
                         sqt[:, c * 512:(c + 1) * 512],
                         start=(c == 0), stop=(c == 7))
            t.drain().then_inc(ct.sem("qp"), 1)
        sums_mm = mk("qp")

        @blk.vector
        def _(v, jt=jt, sums_mm=sums_mm):
            wait_marks(v, ct, [sums_mm])
            v.tensor_copy(abst[0:1, jt * 512:(jt + 1) * 512], pA[0:1, 0:512])
            v.drain()
            v.tensor_copy(abst[0:1, 2048 + jt * 512: 2048 + (jt + 1) * 512],
                          pB[0:1, 0:512])
            v.drain().then_inc(ct.sem("qv"), 1)
        sq_free = mk("qv")
        sums_marks.append(sq_free)

    @blk.vector
    def _(v, ms=list(sums_marks)):
        wait_marks(v, ct, ms)
        v.tensor_scalar_mul(abst[64:65, 2048:4096], abst[0:1, 0:2048], -1.0 / 1024.0)
        v.tensor_scalar_mul(abst[64:65, 0:2048], abst[0:1, 2048:4096], 1.0 / 1024.0)
        v.drain()
        v.scalar_tensor_tensor(out=pA[0:1, 0:2048], in0=abst[64:65, 2048:4096],
                               scalar=-1.0, in1=abst[64:65, 2048:4096],
                               op0=ALU.mult, op1=ALU.mult)
        v.drain()
        v.tensor_tensor(out=abst[32:33, 0:2048], in0=abst[64:65, 0:2048],
                        in1=pA[0:1, 0:2048], op=ALU.add)
        v.drain()
        v.tensor_scalar_add(abst[32:33, 0:2048], abst[32:33, 0:2048], 1e-5)
        v.drain().then_inc(ct.sem("qv"), 1)
    var_mark = mk("qv")

    @blk.scalar
    def _(s, var_mark=var_mark):
        wait_marks(s, ct, [var_mark])
        s.activation(out=abst[64:65, 0:2048], in_=abst[32:33, 0:2048], func=AF.Sqrt)
        s.drain().then_inc(ct.sem("qs"), 1)
    sqrt_mark = mk("qs")

    @blk.vector
    def _(v, sqrt_mark=sqrt_mark):
        wait_marks(v, ct, [sqrt_mark])
        v.reciprocal(abst[64:65, 0:2048], abst[64:65, 0:2048])
        v.drain()
        v.tensor_tensor(out=abst[64:65, 2048:4096], in0=abst[64:65, 2048:4096],
                        in1=abst[64:65, 0:2048], op=ALU.mult)
        v.drain().then_inc(ct.sem("qv"), 1)
    ab_mark = mk("qv")

    ap_mark = None
    for jt in range(4):
        @blk.tensor
        def _(t, jt=jt, ab_mark=ab_mark, fr=ap_mark,
              fr2=(sums_marks[-1] if jt == 0 else None)):
            wait_marks(t, ct, [ab_mark, fr, fr2])
            t.matmul(pA[:, 0:512], cons["ones_f"][64:65, :],
                     abst[64:65, jt * 512:(jt + 1) * 512], start=True, stop=True)
            t.matmul(pB[:, 0:512], cons["ones_f"][64:65, :],
                     abst[64:65, 2048 + jt * 512:2048 + (jt + 1) * 512],
                     start=True, stop=True)
            t.drain().then_inc(ct.sem("qp"), 1)
        bc_mark = mk("qp")

        @blk.vector
        def _(v, jt=jt, bc_mark=bc_mark):
            wait_marks(v, ct, [bc_mark])
            for c in range(8):
                sl = slice(c * 2048 + jt * 512, c * 2048 + (jt + 1) * 512)
                v.tensor_tensor(out=x_fm[:, sl], in0=x_fm[:, sl], in1=pA[:, 0:512],
                                op=ALU.mult)
            v.drain()
            for c in range(8):
                sl = slice(c * 2048 + jt * 512, c * 2048 + (jt + 1) * 512)
                v.tensor_tensor(out=x_fm[:, sl], in0=x_fm[:, sl], in1=pB[:, 0:512],
                                op=ALU.add)
            v.drain().then_inc(ct.sem("qv"), 1)
        ap_mark = mk("qv")
    return ap_mark


def emit_a0(ct, blk, x_ap, io, cons, x_fm, xb, bank, es, bias, CONSTS_T, LDGC_T):
    nc = ct.nc
    mk = ct.mk
    xtok = ct.sb("a0_xtok", [128, 2048], F32, es=es)
    wq = ct.sb("a0_wq_sb", [128, 2 * 2048], BF16, es=es)
    wk = ct.sb("a0_wk_sb", [128, 2 * 2048], BF16, es=es)
    wv = ct.sb("a0_wv_sb", [128, 2 * 2048], BF16, es=es)
    wo = ct.sb("a0_wo_sb", [128, 2 * 2048], BF16, es=es)
    qr = ct.sb("a0_qr", [128, 1024], BF16, es=es)
    kr = ct.sb("a0_kr", [128, 1024], BF16, es=es)
    vr = ct.sb("a0_vr", [128, 1024], BF16, es=es)
    prod = ct.sb("a0_prod", [128, 1024], BF16, es=es)
    probs = ct.sb("a0_probs", [4, 512], BF16, es=es)
    mrow = ct.sb("a0_mrow", [4, 4], F32, es=es)
    lrow = ct.sb("a0_lrow", [4, 4], F32, es=es)
    linv = ct.sb("a0_linv", [4, 4], F32, es=es)
    wvs = ct.sb("a0_wvs", [128, 1024], BF16, es=es)

    @blk.sync
    def _(sy):
        for i in range(16):
            if i >= 2:
                wait_marks(sy, ct, [("qp", i - 1)])
            sy.dma_start(out=xtok[:, (i % 2) * 1024:(i % 2) * 1024 + 1024],
                         in_=x_ap[i * 128:(i + 1) * 128, :]).then_inc(ct.sem("xld"), 16)
            ct.bump("xld", 16)

    evv = {}
    evs = {}
    for i in range(16):
        @blk.tensor
        def _(t, i=i, ev=evv.get(i - 1), es_=evs.get(i - 1)):
            t.wait_ge(ct.sem("xld"), min(16 * (i + 2), 256))
            t.wait_ge(ct.sem("lds"), CONSTS_T)
            wait_marks(t, ct, [ev, es_])
            for g4 in range(2):
                for c4 in range(4):
                    c = g4 * 4 + c4
                    t.transpose(bank[6 + g4][:, c4 * 128:(c4 + 1) * 128],
                                xtok[:, (i % 2) * 1024 + c * 128:
                                     (i % 2) * 1024 + (c + 1) * 128],
                                cons["identf"][:])
            t.drain().then_inc(ct.sem("qp"), 1)
        td = mk("qp")

        @blk.vector
        def _(v, i=i, td=td):
            wait_marks(v, ct, [td])
            dst = x_fm.rearrange("p (c t) -> p c t", c=8)[:, 0:4, i * 128:(i + 1) * 128]
            v.tensor_copy(dst, bank[6][:].rearrange("p (c q) -> p c q", c=4))
            v.drain()
            dst = x_fm.rearrange("p (c t) -> p c t", c=8)[:, 4:8, i * 128:(i + 1) * 128]
            v.tensor_copy(dst, bank[7][:].rearrange("p (c q) -> p c q", c=4))
            v.drain().then_inc(ct.sem("qv"), 1)
        evv[i] = mk("qv")

        @blk.scalar
        def _(s, i=i, td=td):
            wait_marks(s, ct, [td])
            for g4 in range(2):
                dst = xb.rearrange("p (c t) -> p c t", c=8)[:, g4 * 4:(g4 + 1) * 4,
                                                            i * 128:(i + 1) * 128]
                s.copy(out=dst, in_=bank[6 + g4][:].rearrange("p (c q) -> p c q", c=4))
            s.drain().then_inc(ct.sem("qs"), 1)
        evs[i] = mk("qs")
    x_ready = [evv[15], evs[15]]

    cls_t = {}
    cls_compute_done = {}

    def emit_class_dma(g, r):
        half = (r % 2) * 2048
        sem_nm = "wqa" if r % 2 == 0 else "wqb"
        for w_sb, nm in ((wq, "a0_wq"), (wk, "a0_wk"), (wv, "a0_wv")):
            src = io[nm][:, r * 256:(r + 1) * 256].rearrange("(k p) i -> p k i", p=128)
            dst = w_sb[:, half:half + 2048].rearrange("p (k i) -> p k i", k=8)
            g.dma_start(out=dst, in_=src).then_inc(ct.sem(sem_nm), 16)
            ct.bump(sem_nm, 16)
        src = io["a0_wo"][r * 256:(r + 1) * 256, :].rearrange("(k p) i -> p k i", p=128)
        dst = wo[:, half:half + 2048].rearrange("p (k i) -> p k i", k=2)
        g.dma_start(out=dst, in_=src).then_inc(ct.sem(sem_nm), 16)
        ct.bump(sem_nm, 16)
        cls_t[r] = (sem_nm, ct.val(sem_nm))

    @blk.gpsimd
    def _(g):
        emit_class_dma(g, 0)

    @blk.gpsimd
    def _(g):
        emit_class_dma(g, 1)

    for r in range(4):
        half = (r % 2) * 2048

        @blk.tensor
        def _(t, r=r, half=half, xr=list(x_ready), pre=all_marks(ct, ("qp",)) if r == 0 else []):
            t.wait_ge(ct.sem(cls_t[r][0]), cls_t[r][1])
            t.wait_ge(ct.sem("lds"), CONSTS_T)
            t.wait_ge(ct.sem("ldg"), LDGC_T)
            wait_marks(t, ct, xr + pre + ([cls_compute_done[r - 1]] if r > 0 else []))
            for dst_banks, w_sb in (((0, 1), wq), ((2, 3), wk), ((4, 5), wv)):
                for ic in range(2):
                    for k in range(8):
                        t.matmul(bank[dst_banks[ic]][:],
                                 w_sb[:, half + k * 256 + ic * 128:
                                      half + k * 256 + ic * 128 + 128],
                                 xb[:, k * 2048 + r: (k + 1) * 2048: 4],
                                 start=(k == 0), stop=(k == 7))
            t.drain().then_inc(ct.sem("qp"), 1)
        qkv_mm = mk("qp")

        @blk.vector
        def _(v, r=r, qkv_mm=qkv_mm):
            wait_marks(v, ct, [qkv_mm])
            for ic in range(2):
                v.tensor_scalar(out=qr[:, ic * 512:(ic + 1) * 512], in0=bank[ic][:],
                                scalar1=bias[:, r * 2 + ic: r * 2 + ic + 1], scalar2=None,
                                op0=ALU.add)
                v.tensor_scalar(out=kr[:, ic * 512:(ic + 1) * 512], in0=bank[2 + ic][:],
                                scalar1=bias[:, 8 + r * 2 + ic: 8 + r * 2 + ic + 1],
                                scalar2=None, op0=ALU.add)
                v.tensor_scalar(out=vr[:, ic * 512:(ic + 1) * 512], in0=bank[4 + ic][:],
                                scalar1=bias[:, 16 + r * 2 + ic: 16 + r * 2 + ic + 1],
                                scalar2=None, op0=ALU.add)
            v.drain()
            v.tensor_tensor(out=prod[:], in0=qr[:], in1=kr[:], op=ALU.mult)
            v.drain().then_inc(ct.sem("qv"), 1)
        prod_mark = mk("qv")

        @blk.tensor
        def _(t, prod_mark=prod_mark):
            wait_marks(t, ct, [prod_mark])
            t.matmul(bank[6][0:4, :], cons["ind_sc"][:, 0:4], prod[:, 0:512],
                     start=True, stop=False)
            t.matmul(bank[6][0:4, :], cons["ind_sc"][:, 4:8], prod[:, 512:1024],
                     start=False, stop=True)
            t.drain().then_inc(ct.sem("qp"), 1)
        sc_mark = mk("qp")

        @blk.vector
        def _(v, sc_mark=sc_mark):
            wait_marks(v, ct, [sc_mark])
            for n in range(4):
                v.reduce_max(mrow[:, n:n + 1], bank[6][0:4, n * 128:(n + 1) * 128],
                             axis=AX.X, negate=True)
            v.drain()
            v.tensor_scalar_mul(mrow[:], mrow[:], 0.125)
            v.drain().then_inc(ct.sem("qv"), 1)
        m_mark = mk("qv")

        @blk.scalar
        def _(s, m_mark=m_mark):
            wait_marks(s, ct, [m_mark])
            for n in range(4):
                s.activation(out=probs[:, n * 128:(n + 1) * 128],
                             in_=bank[6][0:4, n * 128:(n + 1) * 128],
                             func=AF.Exp, scale=0.125, bias=mrow[:, n:n + 1],
                             accum_out=lrow[:, n:n + 1])
            s.drain().then_inc(ct.sem("qs"), 1)
        e_mark = mk("qs")

        @blk.vector
        def _(v, e_mark=e_mark):
            wait_marks(v, ct, [e_mark])
            v.reciprocal(linv[:], lrow[:])
            v.drain()
            for n in range(4):
                v.tensor_scalar_mul(probs[:, n * 128:(n + 1) * 128],
                                    probs[:, n * 128:(n + 1) * 128], linv[:, n:n + 1])
            v.drain().then_inc(ct.sem("qv"), 1)
        p_mark = mk("qv")

        @blk.tensor
        def _(t, p_mark=p_mark):
            wait_marks(t, ct, [p_mark])
            t.matmul(bank[0][:], cons["ind_bc"][0:4, 0:128], probs[:], start=True, stop=True)
            t.matmul(bank[1][:], cons["ind_bc"][0:4, 128:256], probs[:], start=True, stop=True)
            t.drain().then_inc(ct.sem("qp"), 1)
        bc_mark = mk("qp")

        @blk.vector
        def _(v, bc_mark=bc_mark):
            wait_marks(v, ct, [bc_mark])
            v.tensor_tensor(out=wvs[:, 0:512], in0=bank[0][:], in1=vr[:, 0:512], op=ALU.mult)
            v.tensor_tensor(out=wvs[:, 512:1024], in0=bank[1][:], in1=vr[:, 512:1024],
                            op=ALU.mult)
            v.drain().then_inc(ct.sem("qv"), 1)
        wv_mark = mk("qv")

        ev_mark = None
        for dpair in range(4):
            d0, d1 = 2 * dpair, 2 * dpair + 1

            @blk.tensor
            def _(t, half=half, d0=d0, d1=d1, wv_mark=wv_mark, fr=ev_mark):
                wait_marks(t, ct, [wv_mark, fr])
                for dd, bk_i in ((d0, 2), (d1, 3)):
                    for k in range(2):
                        t.matmul(bank[bk_i][:],
                                 wo[:, half + k * 1024 + dd * 128:
                                     half + k * 1024 + dd * 128 + 128],
                                 wvs[:, k * 512:(k + 1) * 512],
                                 start=(k == 0), stop=(k == 1))
                t.drain().then_inc(ct.sem("qp"), 1)
            y_mark = mk("qp")

            @blk.vector
            def _(v, r=r, d0=d0, d1=d1, y_mark=y_mark):
                wait_marks(v, ct, [y_mark])
                for dd, bk_i in ((d0, 2), (d1, 3)):
                    sl = slice(dd * 2048 + r, (dd + 1) * 2048, 4)
                    v.scalar_tensor_tensor(out=x_fm[:, sl], in0=bank[bk_i][:],
                                           scalar=bias[:, 24 + dd: 24 + dd + 1],
                                           in1=x_fm[:, sl], op0=ALU.add, op1=ALU.add)
                v.drain().then_inc(ct.sem("qv"), 1)
            ev_mark = mk("qv")
        cls_compute_done[r] = ev_mark

        if r < 2:
            @blk.gpsimd
            def _(g, r=r, gate=ev_mark):
                wait_marks(g, ct, [gate])
                emit_class_dma(g, r + 2)

    @blk.vector
    def _(v, gate=cls_compute_done[3]):
        wait_marks(v, ct, [gate])
        for c in range(8):
            v.tensor_copy(xb[:, c * 2048:(c + 1) * 2048], x_fm[:, c * 2048:(c + 1) * 2048])
        v.drain().then_inc(ct.sem("qv"), 1)
    return [mk("qv")]


def emit_f0(ct, blk, io, cons, x_fm, xb, pA, pB, xb_marks, es, gwf, CONSTS_T, LDGC_T):
    mk = ct.mk
    act = ct.sb("f0_act", [128, 11 * 2048], BF16, es=es)
    xlo = act[:, 0:16384]
    ghi = ct.sb("f0_ghi", [128, 64], BF16, es=es)
    glo = ct.sb("f0_glo", [128, 64], BF16, es=es)
    lgt = ct.sb("f0_lgt", [128, 8], F32, es=es)
    gA = ct.sb("f0_gA", [128, 8], F32, es=es)
    gB = ct.sb("f0_gB", [128, 8], F32, es=es)
    m1 = ct.sb("f0_m1", [128, 1], F32, es=es)
    m2 = ct.sb("f0_m2", [128, 1], F32, es=es)
    dn = ct.sb("f0_dn", [128, 1], F32, es=es)
    wfm = ct.sb("f0_wfm", [8, 2048], F32, es=es)
    wbc = ct.sb("f0_wbc", [128, 2048], BF16, es=es)
    sg = ct.sb("f0_sg", [128, 2048], F32, es=es)
    lgf = sg[0:8, 0:2048]
    wgu = ct.sb("f0_wgu", [128, 4 * 2048], BF16, es=es)
    wdb = ct.sb("f0_wdb", [128, 2 * 1408], BF16, es=es)

    @blk.vector
    def _(v, ms=list(xb_marks)):
        v.wait_ge(ct.sem("ldg"), LDGC_T)
        wait_marks(v, ct, ms)
        v.tensor_copy(ghi[:], gwf[:])
        v.drain()
        v.tensor_tensor(out=glo[:], in0=gwf[:], in1=ghi[:], op=ALU.subtract)
        for c in range(8):
            v.tensor_tensor(out=xlo[:, c * 2048:(c + 1) * 2048],
                            in0=x_fm[:, c * 2048:(c + 1) * 2048],
                            in1=xb[:, c * 2048:(c + 1) * 2048], op=ALU.subtract)
        v.drain().then_inc(ct.sem("qv"), 1)
    gate_prep = mk("qv")

    lg_ev = {}
    for jt in range(4):
        @blk.tensor
        def _(t, jt=jt, gate_prep=gate_prep, pre=all_marks(ct, ("qp",)) if jt == 0 else []):
            wait_marks(t, ct, [gate_prep] + pre + ([lg_ev[jt - 1]] if jt > 0 else []))
            first = True
            for xsrc in (xb, xlo):
                for gsrc in (ghi, glo):
                    for k in range(8):
                        t.matmul(pA[0:8, jt * 512:(jt + 1) * 512],
                                 gsrc[:, k * 8:(k + 1) * 8],
                                 xsrc[:, k * 2048 + jt * 512: k * 2048 + (jt + 1) * 512],
                                 start=first, stop=(xsrc is xlo and gsrc is glo and k == 7))
                        first = False
            t.drain().then_inc(ct.sem("qp"), 1)
        lg_mm = mk("qp")

        @blk.vector
        def _(v, jt=jt, lg_mm=lg_mm):
            wait_marks(v, ct, [lg_mm])
            v.tensor_copy(lgf[:, jt * 512:(jt + 1) * 512], pA[0:8, jt * 512:(jt + 1) * 512])
            v.drain().then_inc(ct.sem("qv"), 1)
        lg_ev[jt] = mk("qv")

    wfm_marks = []
    tp_free = None
    for tt in range(16):
        @blk.tensor
        def _(t, tt=tt, gate=lg_ev[3], fr=tp_free):
            wait_marks(t, ct, [gate, fr])
            t.transpose(pB[0:128, 0:8], lgf[:, tt * 128:(tt + 1) * 128],
                        cons["identf"][0:8, 0:8])
            t.drain().then_inc(ct.sem("qp"), 1)
        tpm = mk("qp")

        @blk.vector
        def _(v, tt=tt, tpm=tpm):
            wait_marks(v, ct, [tpm])
            v.tensor_copy(lgt[:], pB[0:128, 0:8])
            v.drain()
            v.reduce_max(m1[:], lgt[:], axis=AX.X, negate=True)
            v.drain()
            v.tensor_scalar(out=gA[:], in0=lgt[:], scalar1=m1[:], scalar2=0.0,
                            op0=ALU.add, op1=ALU.is_equal)
            v.drain()
            v.scalar_tensor_tensor(out=gB[:], in0=gA[:], scalar=-1e30, in1=lgt[:],
                                   op0=ALU.mult, op1=ALU.add)
            v.drain()
            v.reduce_max(m2[:], gB[:], axis=AX.X, negate=True)
            v.drain().then_inc(ct.sem("qv"), 1)
        m_mark = mk("qv")

        @blk.scalar
        def _(s, m_mark=m_mark):
            wait_marks(s, ct, [m_mark])
            s.activation(out=gA[:], in_=lgt[:], func=AF.Exp, bias=m1[:])
            s.drain().then_inc(ct.sem("qs"), 1)
        e_mark = mk("qs")

        @blk.vector
        def _(v, e_mark=e_mark):
            wait_marks(v, ct, [e_mark])
            v.tensor_scalar(out=gB[:], in0=lgt[:], scalar1=m2[:], scalar2=0.0,
                            op0=ALU.add, op1=ALU.is_ge)
            v.drain()
            v.tensor_tensor(out=gA[:], in0=gA[:], in1=gB[:], op=ALU.mult)
            v.drain()
            v.reduce_sum(dn[:], gA[:], axis=AX.X)
            v.drain()
            v.reciprocal(dn[:], dn[:])
            v.drain()
            v.tensor_scalar_mul(gA[:], gA[:], dn[:])
            v.drain().then_inc(ct.sem("qv"), 1)
        w_mark = mk("qv")

        @blk.tensor
        def _(t, w_mark=w_mark):
            wait_marks(t, ct, [w_mark])
            t.transpose(pB[0:8, 0:128], gA[:], cons["identf"][:])
            t.drain().then_inc(ct.sem("qp"), 1)
        wtm = mk("qp")

        @blk.vector
        def _(v, tt=tt, wtm=wtm):
            wait_marks(v, ct, [wtm])
            v.tensor_copy(wfm[:, tt * 128:(tt + 1) * 128], pB[0:8, 0:128])
            v.drain().then_inc(ct.sem("qv"), 1)
        tp_free = mk("qv")
        wfm_marks.append(tp_free)
    wfm_mark = wfm_marks[-1]

    def pair_list(n):
        out = []
        i = 0
        while i < n:
            out.append((i, min(2, n - i)))
            i += 2
        return out

    slab_gate = [wfm_mark]

    def emit_hg_hu_act(n_ic, wg_name, wu_name, wrow0, scale, wcol0=0):
        pairs = pair_list(n_ic)
        dma_t = {}
        ic_done = {}

        def emit_wdma(g, pi, gates):
            ic0, cnt = pairs[pi]
            sem_nm = "wqa" if pi % 2 == 0 else "wqb"
            wait_marks(g, ct, gates)
            for j, nm in ((0, wg_name), (1, wu_name)):
                slot = (pi % 2) + 2 * j
                for o in range(cnt):
                    ic = ic0 + o
                    src = io[nm][wrow0:wrow0 + 1024, wcol0 + ic * 128:wcol0 + (ic + 1) * 128]
                    src = src.rearrange("(k p) i -> p k i", p=128)
                    dst = wgu[:, slot * 2048 + o * 1024: slot * 2048 + (o + 1) * 1024]
                    dst = dst.rearrange("p (k i) -> p k i", k=8)
                    g.dma_start(out=dst, in_=src).then_inc(ct.sem(sem_nm), 16)
                    ct.bump(sem_nm, 16)
            dma_t[pi] = (sem_nm, ct.val(sem_nm))

        phase_gate = list(slab_gate)

        @blk.gpsimd
        def _(g):
            emit_wdma(g, 0, phase_gate)

        @blk.gpsimd
        def _(g):
            emit_wdma(g, 1, phase_gate)

        sil_mark = None
        act_mark = None
        for ic in range(n_ic):
            pi, o = ic // 2, ic % 2
            slot_g, slot_u = (pi % 2), 2 + (pi % 2)

            @blk.tensor
            def _(t, ic=ic, pi=pi, o=o, slot_g=slot_g, slot_u=slot_u,
                  fr_sil=sil_mark, fr_act=act_mark):
                t.wait_ge(ct.sem(dma_t[pi][0]), dma_t[pi][1])
                wait_marks(t, ct, [fr_sil, fr_act])
                for ps, slot in ((pA, slot_g), (pB, slot_u)):
                    for k in range(8):
                        for jt in range(4):
                            t.matmul(ps[:, jt * 512:(jt + 1) * 512],
                                     wgu[:, slot * 2048 + o * 1024 + k * 128:
                                         slot * 2048 + o * 1024 + k * 128 + 128],
                                     xb[:, k * 2048 + jt * 512: k * 2048 + (jt + 1) * 512],
                                     start=(k == 0), stop=(k == 7))
                t.drain().then_inc(ct.sem("qp"), 1)
            mm_mark = mk("qp")

            @blk.scalar
            def _(s, mm_mark=mm_mark):
                wait_marks(s, ct, [mm_mark])
                s.activation(out=sg[:], in_=pA[:], func=AF.Silu)
                s.drain().then_inc(ct.sem("qs"), 1)
            sil_mark = mk("qs")

            @blk.vector
            def _(v, ic=ic, sil_mark=sil_mark, scale=scale):
                wait_marks(v, ct, [sil_mark])
                aslot = act[:, ic * 2048:(ic + 1) * 2048]
                v.tensor_tensor(out=aslot, in0=sg[:], in1=pB[:], op=ALU.mult)
                if scale:
                    v.drain()
                    v.tensor_tensor(out=aslot, in0=aslot, in1=wbc[:], op=ALU.mult)
                v.drain().then_inc(ct.sem("qv"), 1)
            act_mark = mk("qv")
            ic_done[ic] = act_mark

            if pi + 2 < len(pairs) and (o == pairs[pi][1] - 1):
                @blk.gpsimd
                def _(g, pi=pi, gate=ic_done[ic]):
                    emit_wdma(g, pi + 2, [gate])
        slab_gate[0] = ic_done[n_ic - 1]
        return ic_done

    wd_gate = [wfm_mark]

    def emit_y_phase(n_ic, wd_name, wdrow0, act_ready):
        dma_t = {}
        ev = {}
        phase_gate = list(wd_gate)

        def emit_wd_dma(g, d, gates):
            sem_nm = "wda" if d % 2 == 0 else "wdb"
            wait_marks(g, ct, gates)
            src = io[wd_name][wdrow0:wdrow0 + n_ic * 128, d * 128:(d + 1) * 128]
            src = src.rearrange("(k p) i -> p k i", p=128)
            dst = wdb[:, (d % 2) * 1408: (d % 2) * 1408 + n_ic * 128]
            dst = dst.rearrange("p (k i) -> p k i", k=n_ic)
            g.dma_start(out=dst, in_=src).then_inc(ct.sem(sem_nm), 16)
            ct.bump(sem_nm, 16)
            dma_t[d] = (sem_nm, ct.val(sem_nm))

        @blk.gpsimd
        def _(g):
            emit_wd_dma(g, 0, phase_gate)

        @blk.gpsimd
        def _(g):
            emit_wd_dma(g, 1, phase_gate)

        for d in range(8):
            ps = pA if d % 2 == 0 else pB

            @blk.tensor
            def _(t, d=d, ps=ps, act_ready=act_ready, fr=ev.get(d - 2)):
                t.wait_ge(ct.sem(dma_t[d][0]), dma_t[d][1])
                wait_marks(t, ct, [act_ready, fr])
                for k in range(n_ic):
                    for jt in range(4):
                        t.matmul(ps[:, jt * 512:(jt + 1) * 512],
                                 wdb[:, (d % 2) * 1408 + k * 128:
                                     (d % 2) * 1408 + k * 128 + 128],
                                 act[:, k * 2048 + jt * 512: k * 2048 + (jt + 1) * 512],
                                 start=(k == 0), stop=(k == n_ic - 1))
                t.drain().then_inc(ct.sem("qp"), 1)
            y_mark = mk("qp")

            @blk.vector
            def _(v, d=d, ps=ps, y_mark=y_mark):
                wait_marks(v, ct, [y_mark])
                v.tensor_tensor(out=x_fm[:, d * 2048:(d + 1) * 2048],
                                in0=ps[:], in1=x_fm[:, d * 2048:(d + 1) * 2048],
                                op=ALU.add)
                v.drain().then_inc(ct.sem("qv"), 1)
            ev[d] = mk("qv")

            if d + 2 < 8:
                @blk.gpsimd
                def _(g, d=d, gate=ev[d]):
                    emit_wd_dma(g, d + 2, [gate])
        wd_gate[0] = ev[7]
        slab_gate[0] = ev[7]
        return ev[7]

    for grp in range(2):
        ic_done = emit_hg_hu_act(11, "f0_swg", "f0_swu", 0, False, wcol0=grp * 1408)
        emit_y_phase(11, "f0_swd", grp * 1408, ic_done[10])

    wbc_ev = None
    for e in range(8):
        @blk.tensor
        def _(t, e=e, gate=wd_gate[0], wfm_mark=wfm_mark, fr=wbc_ev):
            wait_marks(t, ct, [gate, wfm_mark, fr])
            for jt in range(4):
                t.matmul(pA[:, jt * 512:(jt + 1) * 512],
                         cons["sel8"][:, e * 128:(e + 1) * 128],
                         wfm[0:8, jt * 512:(jt + 1) * 512], start=True, stop=True)
            t.drain().then_inc(ct.sem("qp"), 1)
        wbc_mm = mk("qp")

        @blk.vector
        def _(v, wbc_mm=wbc_mm):
            wait_marks(v, ct, [wbc_mm])
            v.tensor_copy(wbc[:], pA[:])
            v.drain().then_inc(ct.sem("qv"), 1)
        wbc_ev = mk("qv")
        slab_gate[0] = wbc_ev

        ic_done = emit_hg_hu_act(11, "f0_wg", "f0_wu", e * 1024, True)
        wbc_ev = emit_y_phase(11, "f0_wd", e * 1408, ic_done[10])

    @blk.vector
    def _(v, gate=wd_gate[0]):
        wait_marks(v, ct, [gate])
        for c in range(8):
            v.tensor_copy(xb[:, c * 2048:(c + 1) * 2048], x_fm[:, c * 2048:(c + 1) * 2048])
        v.drain().then_inc(ct.sem("qv"), 1)
    return [mk("qv")], act


def emit_a1(ct, blk, io, cons, x2d, x3d, u_dram, xb, pA, pB, xb_marks, es,
            CONSTS_T, LDGC_T):
    mk = ct.mk
    QA = ct.sb("a1_QA", [70, 16384], BF16, es=es)
    KA = ct.sb("a1_KA", [70, 16384], BF16, es=es)
    V = ct.sb("a1_V", [128, 8192], BF16, es=es)
    attn = ct.sb("a1_attn", [128, 16384], BF16, es=es)
    xjt = ct.sb("a1_xjt", [128, 4096], F32, es=es)
    wstr = ct.sb("a1_wstr", [128, 4096], BF16, es=es)
    dump = ct.sb("a1_dump", [128, 2048], BF16, es=es)
    ptb = ct.sb("a1_ptb", [128, 3072], BF16, es=es)
    mcols = ct.sb("a1_mcols", [128, 32], F32, es=es)
    lcols = ct.sb("a1_lcols", [128, 32], F32, es=es)
    ccols = ct.sb("a1_ccols", [128, 16], F32, es=es)
    t16 = ct.sb("a1_t16", [128, 16], F32, es=es)
    chl = ct.sb("a1_chl", [128, 32], F32, es=es)
    chb = ct.sb("a1_chb", [128, 16], BF16, es=es)
    abst = ct.sb("a1_abst", [65, 4096], F32, es=es)
    bias8 = ct.sb("a1_bias8", [128, 16], F32, es=es)
    bvrow = ct.sb("a1_bvrow_sb", [1, 512], F32, es=es)
    pS = [pB[:, 0:512], pB[:, 512:1024]]
    pACC = pB[:, 1024:1536]
    pTf = pB[:, 1536:2048]
    biases = io["a1_biases_sb"]



    @blk.vector
    def _(v, ms=list(xb_marks)):
        v.wait_ge(ct.sem("lds"), CONSTS_T)
        wait_marks(v, ct, ms)
        v.tensor_scalar_mul(bias8[:], biases[:, 0:16], S8)
        v.drain().then_inc(ct.sem("qv"), 1)
    bias8_done = mk("qv")

    attn_done = []
    for g in range(2):
        gc = g * 512

        @blk.gpsimd
        def _(gp, g=g, pre=all_marks(ct, ("qg",))):
            wait_marks(gp, ct, pre)
            qa = io["c_qaug"].rearrange("(h r) n -> h r n", r=4)
            ka = io["c_kaug"].rearrange("(h r) n -> h r n", r=6)
            for hh in range(8):
                h = g * 8 + hh
                gp.dma_start(out=QA[66:70, hh * 2048:(hh + 1) * 2048],
                             in_=qa[h]).then_inc(ct.sem("aug"), 16)
                gp.dma_start(out=KA[64:70, hh * 2048:(hh + 1) * 2048],
                             in_=ka[h]).then_inc(ct.sem("aug"), 16)
                ct.bump("aug", 32)
            gp.memset(QA[64:66, :], 0.0)
            gp.drain().then_inc(ct.sem("qg"), 1)
        aug_dma = ct.val("aug")
        aug_mark = mk("qg")

        @blk.sync
        def _(sy, gc=gc, pre=all_marks(ct)):
            wait_marks(sy, ct, pre)
            sy.dma_start(out=bvrow[:], in_=io["a1_bvrow"][0:1, gc:gc + 512]
                         ).then_inc(ct.sem("lds"), 16)
            ct.bump("lds", 16)
        BV_T = ct.val("lds")

        ps_free = {0: None, 1: None}
        for wnm, dst, bcol0 in (("a1_wq", QA, 0), ("a1_wk", KA, 8)):
            @blk.gpsimd
            def _(gp, wnm=wnm, gc=gc, pre=all_marks(ct, ("qg",))):
                wait_marks(gp, ct, pre)
                src = io[wnm][:, gc:gc + 512].rearrange("(k p) i -> p k i", p=128)
                gp.dma_start(out=wstr[:].rearrange("p (k i) -> p k i", k=8),
                             in_=src).then_inc(ct.sem("wqs"), 16)
                ct.bump("wqs", 16)
            w_t = ct.val("wqs")

            pa_ev = ps_free.get("pa")
            for ic in range(4):
                @blk.tensor
                def _(t, ic=ic, w_t=w_t, fr=pa_ev):
                    t.wait_ge(ct.sem("wqs"), w_t)
                    wait_marks(t, ct, [fr])
                    for k in range(8):
                        for jt in range(4):
                            t.matmul(pA[:, jt * 512:(jt + 1) * 512],
                                     wstr[:, k * 512 + ic * 128: k * 512 + ic * 128 + 128],
                                     xb[:, k * 2048 + jt * 512: k * 2048 + (jt + 1) * 512],
                                     start=(k == 0), stop=(k == 7))
                    t.drain().then_inc(ct.sem("qp"), 1)
                mm_mark = mk("qp")

                @blk.vector
                def _(v, ic=ic, mm_mark=mm_mark, dst=dst,
                      bcol0=bcol0, g=g, bias8_done=bias8_done):
                    wait_marks(v, ct, [mm_mark, bias8_done])
                    bcol = bcol0 + g * 4 + ic
                    for jt in range(4):
                        for half in range(2):
                            hh = 2 * ic + half
                            v.tensor_scalar(
                                out=dst[0:64, hh * 2048 + jt * 512:
                                        hh * 2048 + (jt + 1) * 512],
                                in0=pA[half * 64:(half + 1) * 64, jt * 512:(jt + 1) * 512],
                                scalar1=S8,
                                scalar2=bias8[half * 64:(half + 1) * 64, bcol:bcol + 1],
                                op0=ALU.mult, op1=ALU.add)
                    v.drain().then_inc(ct.sem("qv"), 1)
                pa_ev = mk("qv")
            ps_free["pa"] = pa_ev

        @blk.gpsimd
        def _(gp, gc=gc, pre=all_marks(ct, ("qg",))):
            wait_marks(gp, ct, pre)
            src = io["a1_wv"][:, gc:gc + 512].rearrange("(k p) i -> p k i", p=128)
            gp.dma_start(out=wstr[:].rearrange("p (k i) -> p k i", k=8),
                         in_=src).then_inc(ct.sem("wqs"), 16)
            ct.bump("wqs", 16)
        wv_t = ct.val("wqs")

        for kt in range(16):
            sl = kt % 2

            @blk.tensor
            def _(t, kt=kt, sl=sl, wv_t=wv_t, gc=gc, fr=ps_free[sl], BV_T=BV_T):
                t.wait_ge(ct.sem("lds"), BV_T)
                t.wait_ge(ct.sem("wqs"), wv_t)
                wait_marks(t, ct, [fr])
                for k in range(8):
                    t.matmul(pS[sl][:],
                             xb[:, k * 2048 + kt * 128: k * 2048 + kt * 128 + 128],
                             wstr[:, k * 512:(k + 1) * 512],
                             start=(k == 0), stop=False)
                t.matmul(pS[sl][:], cons["ones_f"][0:1, :],
                         bvrow[0:1, 0:512], start=False, stop=True)
                t.drain().then_inc(ct.sem("qp"), 1)
            vm_mark = mk("qp")

            @blk.vector
            def _(v, kt=kt, sl=sl, vm_mark=vm_mark):
                wait_marks(v, ct, [vm_mark])
                v.tensor_copy(V[:, kt * 512:(kt + 1) * 512], pS[sl][:])
                v.drain().then_inc(ct.sem("qv"), 1)
            ps_free[sl] = mk("qv")

        pairs_done = attn_done[-1] if attn_done else None
        first_of_group = True
        pair_evacs = []
        for jp in range(4):
            hA, hB = g * 8 + 2 * jp, g * 8 + 2 * jp + 1
            heads = [(hA, (2 * jp) * 2048, pA, 0), (hB, (2 * jp + 1) * 2048, pB, 16)]


            mred = {hA: [], hB: []}
            exp_marks = {hA: None, hB: None}
            for qc in range(16):
                kend = (qc + 1) * 128
                red_marks = {}
                for h_, hc_, ps_, off_ in heads:
                    @blk.tensor
                    def _(t, qc=qc, kend=kend, hc=hc_, ps=ps_, fr=exp_marks[h_],
                          first=first_of_group, aug_dma=aug_dma, aug_mark=aug_mark,
                          pre=all_marks(ct, ("qp",)) if first_of_group else [],
                          pd=pairs_done):
                        t.wait_ge(ct.sem("aug"), aug_dma)
                        if first:
                            wait_marks(t, ct, pre)
                        else:
                            wait_marks(t, ct, [fr, aug_mark, pd])
                        j0 = 0
                        while j0 < kend:
                            jl = min(512, kend - j0)
                            t.matmul(ps[:, j0:j0 + jl],
                                     QA[0:70, hc + qc * 128: hc + (qc + 1) * 128],
                                     KA[0:70, hc + j0: hc + j0 + jl],
                                     start=True, stop=True)
                            j0 += jl
                        t.drain().then_inc(ct.sem("qp"), 1)
                    smm_mark = mk("qp")
                    first_of_group = False

                    @blk.vector
                    def _(v, qc=qc, kend=kend, ps=ps_, off=off_, smm_mark=smm_mark):
                        wait_marks(v, ct, [smm_mark])
                        v.tensor_tensor(out=ps[:, qc * 128:(qc + 1) * 128],
                                        in0=ps[:, qc * 128:(qc + 1) * 128],
                                        in1=cons["mtri"][:], op=ALU.add)
                        v.drain()
                        v.reduce_max(mcols[:, off + qc:off + qc + 1], ps[:, 0:kend],
                                     axis=AX.X, negate=True)
                        v.drain().then_inc(ct.sem("qv"), 1)
                    mm_ = mk("qv")
                    mred[h_].append(mm_)
                    red_marks[h_] = mm_

                @blk.scalar
                def _(s, qc=qc, kend=kend, rm=dict(red_marks), hds=list(heads)):
                    wait_marks(s, ct, list(rm.values()))
                    for h_, hc_, ps_, off_ in hds:
                        s.activation(out=dump[:, 0:kend], in_=ps_[:, 0:kend],
                                     func=AF.Exp,
                                     bias=mcols[:, off_ + qc:off_ + qc + 1],
                                     accum_out=lcols[:, off_ + qc:off_ + qc + 1])
                        s.drain()
                    s.drain().then_inc(ct.sem("qs"), 1)
                em2 = mk("qs")
                exp_marks[hA] = em2
                exp_marks[hB] = em2

            mrow_marks = {}
            prev_tp = None
            prev_chl = None
            for h_, hc_, ps_, off_ in heads:
                @blk.scalar
                def _(s, off=off_, ex=exp_marks[h_], pc=prev_chl):
                    wait_marks(s, ct, [ex, pc])
                    s.activation(out=t16[:], in_=lcols[:, off:off + 16], func=AF.Ln)
                    s.drain().then_inc(ct.sem("qs"), 1)
                lnl_mark = mk("qs")

                @blk.vector
                def _(v, off=off_, lnl_mark=lnl_mark, mm=list(mred[h_]), ptp=prev_tp):
                    wait_marks(v, ct, [lnl_mark, ptp] + mm)
                    v.tensor_tensor(out=ccols[:], in0=mcols[:, off:off + 16], in1=t16[:],
                                    op=ALU.subtract)
                    v.drain()
                    v.tensor_copy(chb[:], ccols[:])
                    v.drain()
                    v.tensor_copy(chl[:, 0:32:2], chb[:])
                    v.drain()
                    v.tensor_tensor(out=t16[:], in0=ccols[:], in1=chl[:, 0:32:2],
                                    op=ALU.subtract)
                    v.drain()
                    v.tensor_copy(chb[:], t16[:])
                    v.drain()
                    v.tensor_copy(chl[:, 1:32:2], chb[:])
                    v.drain().then_inc(ct.sem("qv"), 1)
                chl_mark = mk("qv")
                prev_chl = chl_mark

                @blk.tensor
                def _(t, ps=ps_, chl_mark=chl_mark, fr=exp_marks[h_]):
                    wait_marks(t, ct, [chl_mark, fr])
                    for qc in range(16):
                        t.transpose(ps[0:2, qc * 128:(qc + 1) * 128],
                                    chl[:, 2 * qc:2 * qc + 2], cons["identf"][:])
                    t.drain().then_inc(ct.sem("qp"), 1)
                tp_mark = mk("qp")

                @blk.vector
                def _(v, hc=hc_, ps=ps_, tp_mark=tp_mark):
                    wait_marks(v, ct, [tp_mark])
                    v.tensor_copy(QA[64:66, hc:hc + 2048], ps[0:2, 0:2048])
                    v.drain().then_inc(ct.sem("qv"), 1)
                mrow_marks[h_] = mk("qv")
                prev_tp = ("qp", ct.val("qp"))

            def head_groups(h_, hc_, ps_, ptb0, mrowm):
                pacc = ps_[:, 1536:2048]
                slots = [ps_[:, 0:512], ps_[:, 512:1024], ps_[:, 1024:1536]]
                state = {"exp": mrowm, "pv": None, "expprev": None}
                out = []
                for qt in range(4):
                    nkc = 4 * (qt + 1)
                    kcs = list(range(nkc))
                    gsets = [kcs[i:i + 3] for i in range(0, nkc, 3)]
                    for gi, kset in enumerate(gsets):
                        out.append((qt, gi, kset, len(gsets), nkc))
                return out, pacc, slots, state

            emitted = {}
            for h_, hc_, ps_, off_ in heads:
                emitted[h_] = head_groups(h_, hc_, ps_,
                                          0 if h_ == hA else 1536, mrow_marks[h_])

            maxg = max(len(emitted[h_][0]) for h_, _, _, _ in heads)
            pv_track = {hA: {}, hB: {}}
            exp_track = {hA: {}, hB: {}}
            evac_last = {hA: None, hB: None}
            for step in range(maxg):
                for h_, hc_, ps_, off_ in heads:
                    groups, pacc, slots, state = emitted[h_]
                    if step >= len(groups):
                        continue
                    qt, gi, kset, ngroups, nkc = groups[step]
                    ptb0 = 0 if h_ == hA else 1536
                    prev_exp = state["exp"]
                    prev_pv = state["pv"]

                    @blk.tensor
                    def _(t, qt=qt, kset=kset, hc=hc_, slots=slots,
                          fr=prev_exp, ev=evac_last[h_]):
                        wait_marks(t, ct, [fr, ev])
                        for i, kc in enumerate(kset):
                            kc_rel = kc - 4 * qt
                            j0 = max(0, kc_rel) * 128
                            t.matmul(slots[i][:, j0:512],
                                     KA[0:70, hc + kc * 128: hc + (kc + 1) * 128],
                                     QA[0:70, hc + qt * 512 + j0: hc + (qt + 1) * 512],
                                     start=True, stop=True)
                        t.drain().then_inc(ct.sem("qp"), 1)
                    smm = mk("qp")

                    ndiag = sum(1 for kc in kset if kc - 4 * qt >= 0)
                    if ndiag:
                        @blk.vector
                        def _(v, qt=qt, kset=kset, slots=slots, smm=smm):
                            wait_marks(v, ct, [smm])
                            for i, kc in enumerate(kset):
                                kc_rel = kc - 4 * qt
                                if kc_rel < 0:
                                    continue
                                j0 = kc_rel * 128
                                v.tensor_tensor(out=slots[i][:, j0:j0 + 128],
                                                in0=slots[i][:, j0:j0 + 128],
                                                in1=cons["mtriT"][:], op=ALU.add)
                                if j0 > 0:
                                    v.drain()
                                    v.memset(slots[i][:, 0:j0], -1e30)
                                v.drain()
                            v.drain().then_inc(ct.sem("qv"), 1)
                        gate = mk("qv")
                    else:
                        gate = smm

                    @blk.scalar
                    def _(s, kset=kset, ps=ps_, ptb0=ptb0, gate=gate, pv2=prev_pv):
                        wait_marks(s, ct, [gate, pv2])
                        n = len(kset)
                        s.activation(out=ptb[:, ptb0:ptb0 + n * 512],
                                     in_=ps[:, 0:n * 512], func=AF.Exp)
                        s.drain().then_inc(ct.sem("qs"), 1)
                    em = mk("qs")
                    state["exp"] = em

                    @blk.tensor
                    def _(t, qt=qt, kset=kset, hh2=h_ % 8, pacc=pacc, ptb0=ptb0,
                          em=em, nkc=nkc):
                        wait_marks(t, ct, [em])
                        for i, kc in enumerate(kset):
                            t.matmul(pacc[0:64, :],
                                     V[:, kc * 512 + hh2 * 64: kc * 512 + hh2 * 64 + 64],
                                     ptb[:, ptb0 + i * 512: ptb0 + (i + 1) * 512],
                                     start=(kc == 0), stop=(kc == nkc - 1))
                        t.drain().then_inc(ct.sem("qp"), 1)
                    state["pv"] = mk("qp")

                    if kset[-1] == nkc - 1:
                        @blk.vector
                        def _(v, h2=h_, qt=qt, pacc=pacc, m=state["pv"]):
                            wait_marks(v, ct, [m])
                            chunk, half = h2 // 2, h2 % 2
                            v.tensor_copy(attn[half * 64:half * 64 + 64,
                                               chunk * 2048 + qt * 512:
                                               chunk * 2048 + (qt + 1) * 512],
                                          pacc[0:64, :])
                            v.drain().then_inc(ct.sem("qv"), 1)
                        evac_last[h_] = mk("qv")
            pairs_done = evac_last[hB]
            pair_evacs.extend([evac_last[hA], evac_last[hB]])
        pacc_free = pairs_done
        attn_done.append(pair_evacs[-1])

    wo_mm = {}
    stt_marks = []
    ust_t = ct.val("ust")
    for jt in range(4):
        @blk.sync
        def _(sy, jt=jt, ust_t=ust_t, pre=all_marks(ct) if jt == 0 else []):
            if jt >= 1:
                sy.wait_ge(ct.sem("ust"), ust_t)
            else:
                sy.wait_ge(ct.sem("spl"), ct.val("spl"))
                wait_marks(sy, ct, pre)
            for c in range(8):
                sy.dma_start(out=xjt[:, c * 512:(c + 1) * 512],
                             in_=x2d[c * 128:(c + 1) * 128, jt * 512:(jt + 1) * 512]
                             ).then_inc(ct.sem("uld"), 16)
                ct.bump("uld", 16)
        xjt_t = ct.val("uld")

        jt_stt = []
        for d in range(8):
            wsem = "woa" if d % 2 == 0 else "wob"

            @blk.gpsimd
            def _(gp, d=d, jt=jt, wsem=wsem,
                  prev=wo_mm.get((jt, d - 2)) or wo_mm.get((jt - 1, d + 6)),
                  att=list(attn_done)):
                wait_marks(gp, ct, [prev] + (att if (jt == 0 and d < 2) else []))
                src = io["a1_wo"][:, d * 128:(d + 1) * 128].rearrange("(k p) i -> p k i",
                                                                     p=128)
                gp.dma_start(out=wstr[:, (d % 2) * 1024:(d % 2) * 1024 + 1024]
                             .rearrange("p (k i) -> p k i", k=8),
                             in_=src).then_inc(ct.sem(wsem), 16)
                ct.bump(wsem, 16)
            wo_t = ct.val(wsem)

            @blk.tensor
            def _(t, d=d, jt=jt, wo_t=wo_t, wsem=wsem,
                  fr=(stt_marks[-2] if len(stt_marks) >= 2 else None),
                  att=list(attn_done)):
                t.wait_ge(ct.sem(wsem), wo_t)
                wait_marks(t, ct, [fr] + att)
                for k in range(8):
                    t.matmul(pS[d % 2][:],
                             wstr[:, (d % 2) * 1024 + k * 128:
                                  (d % 2) * 1024 + k * 128 + 128],
                             attn[:, k * 2048 + jt * 512: k * 2048 + (jt + 1) * 512],
                             start=(k == 0), stop=(k == 7))
                t.drain().then_inc(ct.sem("qp"), 1)
            wo_mm[(jt, d)] = mk("qp")

            @blk.vector
            def _(v, d=d, jt=jt, xjt_t=xjt_t, m=wo_mm[(jt, d)]):
                v.wait_ge(ct.sem("uld"), xjt_t)
                wait_marks(v, ct, [m])
                v.scalar_tensor_tensor(out=xjt[:, d * 512:(d + 1) * 512],
                                       in0=pS[d % 2][:],
                                       scalar=biases[:, 24 + d:24 + d + 1],
                                       in1=xjt[:, d * 512:(d + 1) * 512],
                                       op0=ALU.add, op1=ALU.add)
                v.drain()
                v.tensor_copy(xb[:, d * 2048 + jt * 512: d * 2048 + (jt + 1) * 512],
                              xjt[:, d * 512:(d + 1) * 512])
                v.drain().then_inc(ct.sem("qv"), 1)
            m2 = mk("qv")
            stt_marks.append(m2)
            jt_stt.append(m2)

        @blk.sync
        def _(sy, jt=jt, ms=list(jt_stt)):
            wait_marks(sy, ct, ms)
            for c in range(8):
                sy.dma_start(out=u_dram[c * 128:(c + 1) * 128, jt * 512:(jt + 1) * 512],
                             in_=xjt[:, c * 512:(c + 1) * 512]).then_inc(ct.sem("ust"), 16)
                ct.bump("ust", 16)
        ust_t = ct.val("ust")
    u_all = ct.val("ust")

    sq_free = None
    sums_marks = []
    for jt in range(4):
        @blk.scalar
        def _(s, jt=jt, ms=list(stt_marks), fr=sq_free):
            wait_marks(s, ct, ms + ([fr] if fr else []))
            for c in range(8):
                sq_out = (dump[:, c * 512:(c + 1) * 512] if c < 4
                          else ptb[:, (c - 4) * 512:(c - 3) * 512])
                s.activation(out=sq_out,
                             in_=xb[:, c * 2048 + jt * 512: c * 2048 + (jt + 1) * 512],
                             func=AF.Square)
            s.drain().then_inc(ct.sem("qs"), 1)
        sq_mark = mk("qs")

        @blk.tensor
        def _(t, jt=jt, sq_mark=sq_mark,
              fr=(stt_marks[-1] if jt == 0 else None)):
            wait_marks(t, ct, [sq_mark, fr])
            for c in range(8):
                t.matmul(pS[0][0:1, :], cons["ones_bf"][:, 0:1],
                         xb[:, c * 2048 + jt * 512: c * 2048 + (jt + 1) * 512],
                         start=(c == 0), stop=(c == 7))
            for c in range(8):
                sq_in = (dump[:, c * 512:(c + 1) * 512] if c < 4
                         else ptb[:, (c - 4) * 512:(c - 3) * 512])
                t.matmul(pS[1][0:1, :], cons["ones_bf"][:, 0:1], sq_in,
                         start=(c == 0), stop=(c == 7))
            t.drain().then_inc(ct.sem("qp"), 1)
        sums_mm = mk("qp")

        @blk.vector
        def _(v, jt=jt, sums_mm=sums_mm):
            wait_marks(v, ct, [sums_mm])
            v.tensor_copy(abst[0:1, jt * 512:(jt + 1) * 512], pS[0][0:1, :])
            v.drain()
            v.tensor_copy(abst[0:1, 2048 + jt * 512: 2048 + (jt + 1) * 512],
                          pS[1][0:1, :])
            v.drain().then_inc(ct.sem("qv"), 1)
        sq_free = mk("qv")
        sums_marks.append(sq_free)

    @blk.vector
    def _(v, ms=list(sums_marks)):
        wait_marks(v, ct, ms)
        v.tensor_scalar_mul(abst[64:65, 2048:4096], abst[0:1, 0:2048], -1.0 / 1024.0)
        v.tensor_scalar_mul(abst[64:65, 0:2048], abst[0:1, 2048:4096], 1.0 / 1024.0)
        v.drain()
        v.scalar_tensor_tensor(out=pA[0:1, 0:2048], in0=abst[64:65, 2048:4096],
                               scalar=-1.0, in1=abst[64:65, 2048:4096],
                               op0=ALU.mult, op1=ALU.mult)
        v.drain()
        v.tensor_tensor(out=abst[32:33, 0:2048], in0=abst[64:65, 0:2048],
                        in1=pA[0:1, 0:2048], op=ALU.add)
        v.drain()
        v.tensor_scalar_add(abst[32:33, 0:2048], abst[32:33, 0:2048], 1e-5)
        v.drain().then_inc(ct.sem("qv"), 1)
    var_mark = mk("qv")

    @blk.scalar
    def _(s, var_mark=var_mark):
        wait_marks(s, ct, [var_mark])
        s.activation(out=abst[64:65, 0:2048], in_=abst[32:33, 0:2048], func=AF.Sqrt)
        s.drain().then_inc(ct.sem("qs"), 1)
    sqrt_mark = mk("qs")

    @blk.vector
    def _(v, sqrt_mark=sqrt_mark):
        wait_marks(v, ct, [sqrt_mark])
        v.reciprocal(abst[64:65, 0:2048], abst[64:65, 0:2048])
        v.drain()
        v.tensor_tensor(out=abst[64:65, 2048:4096], in0=abst[64:65, 2048:4096],
                        in1=abst[64:65, 0:2048], op=ALU.mult)
        v.drain().then_inc(ct.sem("qv"), 1)
    ab_mark = mk("qv")

    ap_mark = None
    st_t = ct.val("x3s")
    for jt in range(4):
        @blk.tensor
        def _(t, jt=jt, ab_mark=ab_mark, fr=ap_mark,
              fr2=(sums_marks[-1] if jt == 0 else None)):
            wait_marks(t, ct, [ab_mark, fr, fr2])
            t.matmul(pS[0][:], cons["ones_f"][64:65, :],
                     abst[64:65, jt * 512:(jt + 1) * 512], start=True, stop=True)
            t.matmul(pS[1][:], cons["ones_f"][64:65, :],
                     abst[64:65, 2048 + jt * 512:2048 + (jt + 1) * 512],
                     start=True, stop=True)
            t.drain().then_inc(ct.sem("qp"), 1)
        bc_mark = mk("qp")

        @blk.sync
        def _(sy, jt=jt, st_t=st_t, u_all=u_all):
            sy.wait_ge(ct.sem("ust"), u_all)
            sy.wait_ge(ct.sem("x3s"), st_t)
            for c in range(8):
                sy.dma_start(out=xjt[:, c * 512:(c + 1) * 512],
                             in_=u_dram[c * 128:(c + 1) * 128, jt * 512:(jt + 1) * 512]
                             ).then_inc(ct.sem("uld"), 16)
                ct.bump("uld", 16)
        u_t = ct.val("uld")

        @blk.vector
        def _(v, jt=jt, bc_mark=bc_mark, u_t=u_t):
            wait_marks(v, ct, [bc_mark])
            v.wait_ge(ct.sem("uld"), u_t)
            for c in range(8):
                v.tensor_tensor(out=xjt[:, c * 512:(c + 1) * 512],
                                in0=xjt[:, c * 512:(c + 1) * 512],
                                in1=pS[0][:], op=ALU.mult)
            v.drain()
            for c in range(8):
                v.tensor_tensor(out=xjt[:, c * 512:(c + 1) * 512],
                                in0=xjt[:, c * 512:(c + 1) * 512],
                                in1=pS[1][:], op=ALU.add)
            v.drain().then_inc(ct.sem("qv"), 1)
        ap_mark = mk("qv")

        @blk.sync
        def _(sy, jt=jt, ap_mark=ap_mark):
            wait_marks(sy, ct, [ap_mark])
            for c in range(8):
                sy.dma_start(out=x3d[c * 128:(c + 1) * 128, jt * 512:(jt + 1) * 512],
                             in_=xjt[:, c * 512:(c + 1) * 512]).then_inc(ct.sem("x3s"), 16)
                ct.bump("x3s", 16)
        st_t = ct.val("x3s")
    return ct.val("x3s")


def emit_f1(ct, blk, io, cons, x3d, y_ap, u_dram, wpq_dram, xb, pA, pB, x3_all, es):
    mk = ct.mk
    wfc = ct.sb("f1_wfc", [128, 8 * 4096], BF16, es=es)
    hbuf = ct.sb("f1_hbuf", [128, 32 * 512], BF16, es=es)
    wpt = ct.sb("f1_wpt", [128, 2 * 4096], BF16, es=es)
    slab = ct.sb("f1_slab", [128, 2 * 1024], F32, es=es)
    qtmp = ct.sb("f1_qtmp", [128, 1024], BF16, es=es)
    sgn = ct.sb("f1_sgn", [128, 1024], BF16, es=es)
    absf = ct.sb("f1_absf", [128, 1024], F32, es=es)
    xjt = ct.sb("f1_xjt", [128, 4096], F32, es=es)
    tmp5 = ct.sb("f1_tmp5", [128, 512], F32, es=es)
    dump = ct.sb("f1_dump", [128, 1024], BF16, es=es)
    abst = ct.sb("f1_abst", [65, 4096], F32, es=es)
    gcol = ct.sb("f1_gcol", [128, 4], F32, es=es)
    g11 = ct.sb("f1_g11", [1, 4], F32, es=es)
    lnc = io["f1_lnc_sb"]
    wfcT, wprojT = io["f1_wfcT"], io["f1_wprojT"]

    cast_mark = None
    for p in range(4):
        @blk.sync
        def _(sy, p=p, cast_mark=cast_mark, pre=all_marks(ct) if p == 0 else []):
            sy.wait_ge(ct.sem("x3s"), x3_all)
            if cast_mark is not None:
                wait_marks(sy, ct, [cast_mark])
            else:
                wait_marks(sy, ct, pre)
            for cc in range(2):
                c = 2 * p + cc
                sy.dma_start(out=xjt[:, cc * 2048:(cc + 1) * 2048],
                             in_=x3d[c * 128:(c + 1) * 128, :]).then_inc(ct.sem("uld"), 16)
                ct.bump("uld", 16)
        lds_t = ct.val("uld")

        @blk.vector
        def _(v, p=p, lds_t=lds_t):
            v.wait_ge(ct.sem("uld"), lds_t)
            for cc in range(2):
                c = 2 * p + cc
                v.tensor_copy(xb[:, c * 2048:(c + 1) * 2048],
                              xjt[:, cc * 2048:(cc + 1) * 2048])
            v.drain().then_inc(ct.sem("qv"), 1)
        cast_mark = mk("qv")
    xb_done = cast_mark

    def slab_src(mat, i):
        if mat == "fc":
            r, cpc = i // 4, i % 4
            return wfcT[r * 128:(r + 1) * 128, cpc * 1024:(cpc + 1) * 1024]
        return wprojT[i * 128:(i + 1) * 128, :]

    @blk.vector
    def _(v):
        v.memset(gcol[:, 0:2], 0.0)
        v.drain().then_inc(ct.sem("qv"), 1)
    ct.bump("qv")

    gm_marks = []
    slab_carry = [None, None]
    for mat, col, sems in (("fc", 0, ("wfa", "wfb")), ("wp", 1, ("wpa", "wpb"))):
        sl_done = {-2: slab_carry[0], -1: slab_carry[1]}
        for i in range(32):
            sem_nm = sems[i % 2]

            @blk.gpsimd
            def _(g, mat=mat, i=i, sem_nm=sem_nm, prev=sl_done.get(i - 2),
                  pre=all_marks(ct, ("qg",)) if (mat == "fc" and i < 2) else []):
                wait_marks(g, ct, ([prev] if prev else []) + pre)
                g.dma_start(out=slab[:, (i % 2) * 1024:(i % 2) * 1024 + 1024],
                            in_=slab_src(mat, i)).then_inc(ct.sem(sem_nm), 16)
                ct.bump(sem_nm, 16)
            dma_t = ct.val(sem_nm)

            @blk.scalar
            def _(s, i=i, sem_nm=sem_nm, dma_t=dma_t, prev=sl_done.get(i - 1)):
                s.wait_ge(ct.sem(sem_nm), dma_t)
                wait_marks(s, ct, [prev] if prev else [])
                s.activation(out=dump[:, 0:1024],
                             in_=slab[:, (i % 2) * 1024:(i % 2) * 1024 + 1024],
                             func=AF.Abs, accum_out=gcol[:, 2:3])
                s.drain().then_inc(ct.sem("qs"), 1)
            ab_mark = mk("qs")

            @blk.vector
            def _(v, col=col, ab_mark=ab_mark):
                wait_marks(v, ct, [ab_mark])
                v.tensor_tensor(out=gcol[:, col:col + 1], in0=gcol[:, col:col + 1],
                                in1=gcol[:, 2:3], op=ALU.add)
                v.drain().then_inc(ct.sem("qv"), 1)
            sl_done[i] = mk("qv")
        gm_marks.append(sl_done[31])
        slab_carry = [sl_done[30], sl_done[31]]

    @blk.tensor
    def _(t, ms=list(gm_marks), pre=all_marks(ct, ("qp",))):
        wait_marks(t, ct, ms + pre)
        t.matmul(pA[0:1, 0:1], gcol[:, 0:1], cons["ones_cf"][:], start=True, stop=True)
        t.matmul(pA[0:1, 1:2], gcol[:, 1:2], cons["ones_cf"][:], start=True, stop=True)
        t.drain().then_inc(ct.sem("qp"), 1)
    gsum_mark = mk("qp")

    @blk.vector
    def _(v, gsum_mark=gsum_mark):
        wait_marks(v, ct, [gsum_mark])
        v.tensor_scalar(out=g11[0:1, 0:2], in0=pA[0:1, 0:2],
                        scalar1=1.0 / NW, scalar2=1e-6, op0=ALU.mult, op1=ALU.add)
        v.drain()
        v.tensor_scalar_mul(g11[0:1, 0:2], g11[0:1, 0:2], 0.5)
        v.drain().then_inc(ct.sem("qv"), 1)
    ghalf_mark = mk("qv")

    gbc_marks = []
    for c in range(2):
        @blk.tensor
        def _(t, c=c, ghalf_mark=ghalf_mark, fr=(gbc_marks[-1] if gbc_marks else None)):
            wait_marks(t, ct, [ghalf_mark, fr])
            t.matmul(pB[:, 0:1], cons["ones_f"][0:1, :], g11[0:1, c:c + 1],
                     start=True, stop=True)
            t.drain().then_inc(ct.sem("qp"), 1)
        m = mk("qp")

        @blk.vector
        def _(v, c=c, m=m):
            wait_marks(v, ct, [m])
            v.tensor_copy(gcol[:, 2 + c:3 + c], pB[:, 0:1])
            v.drain().then_inc(ct.sem("qv"), 1)
        gbc_marks.append(mk("qv"))
    gbc_mark = gbc_marks[-1]

    slab_carry2 = list(slab_carry)

    def emit_quant(mat, col, sems, out_cb):
        sl_done = {-2: slab_carry2[0], -1: slab_carry2[1]}
        for i in range(32):
            sem_nm = sems[i % 2]

            @blk.gpsimd
            def _(g, mat=mat, i=i, sem_nm=sem_nm, prev=sl_done.get(i - 2),
                  gbc_mark=gbc_mark):
                wait_marks(g, ct, ([prev] if prev else []) + [gbc_mark])
                g.dma_start(out=slab[:, (i % 2) * 1024:(i % 2) * 1024 + 1024],
                            in_=slab_src(mat, i)).then_inc(ct.sem(sem_nm), 16)
                ct.bump(sem_nm, 16)
            dma_t = ct.val(sem_nm)

            @blk.vector
            def _(v, i=i, col=col, sem_nm=sem_nm, dma_t=dma_t,
                  prevq=sl_done.get(i - 1)):
                v.wait_ge(ct.sem(sem_nm), dma_t)
                wait_marks(v, ct, [prevq] if prevq else [])
                sl_ = slab[:, (i % 2) * 1024:(i % 2) * 1024 + 1024]
                v.tensor_scalar(out=sgn[:], in0=sl_, scalar1=0.0, scalar2=None,
                                op0=ALU.is_gt)
                v.tensor_scalar(out=qtmp[:], in0=sl_, scalar1=0.0, scalar2=None,
                                op0=ALU.is_lt)
                v.drain()
                v.tensor_tensor(out=sgn[:], in0=sgn[:], in1=qtmp[:],
                                op=ALU.subtract)
                v.tensor_tensor(out=absf[:], in0=sl_, in1=sgn[:], op=ALU.mult)
                v.drain()
                v.tensor_scalar(out=qtmp[:], in0=absf[:],
                                scalar1=gcol[:, 2 + col:3 + col], scalar2=None,
                                op0=ALU.is_gt)
                v.drain()
                v.tensor_tensor(out=qtmp[:], in0=qtmp[:], in1=sgn[:], op=ALU.mult)
                v.drain().then_inc(ct.sem("qv"), 1)
            q_mark = mk("qv")
            sl_done[i] = out_cb(i, q_mark)
        slab_carry2[0], slab_carry2[1] = sl_done[30], sl_done[31]

    def fc_out(i, q_mark):
        r, cpc = i // 4, i % 4

        @blk.vector
        def _(v, r=r, cpc=cpc):
            v.tensor_copy(wfc[:, r * 4096 + cpc * 1024: r * 4096 + (cpc + 1) * 1024],
                          qtmp[:])
            v.drain().then_inc(ct.sem("qv"), 1)
        return ("qv", ct.bump("qv"))

    def wp_out(i, q_mark):
        @blk.sync
        def _(sy, i=i, q_mark=q_mark):
            wait_marks(sy, ct, [q_mark])
            sy.dma_start(out=wpq_dram[i * 128:(i + 1) * 128, :],
                         in_=qtmp[:]).then_inc(ct.sem("wps"), 16)
        ct.bump("wps", 16)
        u_t = ct.val("wps")

        @blk.vector
        def _(v, u_t=u_t):
            v.wait_ge(ct.sem("wps"), u_t)
            v.drain().then_inc(ct.sem("qv"), 1)
        return ("qv", ct.bump("qv"))

    emit_quant("fc", 0, ("wfa", "wfb"), fc_out)
    emit_quant("wp", 1, ("wpa", "wpb"), wp_out)
    wpq_all = ct.val("wps")
    quant_done = ("qv", ct.val("qv"))

    ust_y = ct.val("ust")
    stt_marks = []
    wp_mm = {}
    for tg in range(4):
        @blk.sync
        def _(sy, tg=tg, fr=(stt_marks[-1] if stt_marks else xb_done), u_prev=ust_y):
            wait_marks(sy, ct, [fr])
            if tg >= 1:
                sy.wait_ge(ct.sem("ust"), u_prev)
            for c in range(8):
                sy.dma_start(out=xjt[:, c * 512:(c + 1) * 512],
                             in_=x3d[c * 128:(c + 1) * 128, tg * 512:(tg + 1) * 512]
                             ).then_inc(ct.sem("uld"), 16)
                ct.bump("uld", 16)
        xjt_t = ct.val("uld")

        ge_marks = {}
        prev_tg_mark = stt_marks[-1] if stt_marks else None
        for hc in range(32):
            q = hc % 4

            @blk.tensor
            def _(t, tg=tg, hc=hc, q=q,
                  fr=ge_marks.get(hc - 4) or prev_tg_mark,
                  qd=quant_done, xb_done=xb_done):
                wait_marks(t, ct, [fr, qd, xb_done])
                for k in range(8):
                    t.matmul(pA[:, q * 512:(q + 1) * 512],
                             wfc[:, k * 4096 + hc * 128: k * 4096 + hc * 128 + 128],
                             xb[:, k * 2048 + tg * 512: k * 2048 + (tg + 1) * 512],
                             start=(k == 0), stop=(k == 7))
                t.drain().then_inc(ct.sem("qp"), 1)
            mmk = mk("qp")

            @blk.scalar
            def _(s, hc=hc, q=q, mmk=mmk):
                wait_marks(s, ct, [mmk])
                s.activation(out=hbuf[:, hc * 512:(hc + 1) * 512],
                             in_=pA[:, q * 512:(q + 1) * 512], func=AF.Gelu)
                s.drain().then_inc(ct.sem("qs"), 1)
            ge_marks[hc] = mk("qs")

        @blk.tensor
        def _(t, tg=tg, ms=[ge_marks[31]]):
            t.wait_ge(ct.sem("ldg"), 80)
            wait_marks(t, ct, ms)
            for hc in range(32):
                t.matmul(pB[0:1, 0:512], cons["ones_bf"][:, 0:1],
                         hbuf[:, hc * 512:(hc + 1) * 512],
                         start=(hc == 0), stop=(hc == 31))
            t.drain().then_inc(ct.sem("qp"), 1)
        sum_mark = mk("qp")

        sq_marks = {}
        for hc in range(32):
            @blk.vector
            def _(v, hc=hc, fr=sq_marks.get(hc - 2), ms=[ge_marks[31]]):
                wait_marks(v, ct, [fr] if fr else ms)
                v.tensor_tensor(out=dump[:, (hc % 2) * 512:(hc % 2) * 512 + 512],
                                in0=hbuf[:, hc * 512:(hc + 1) * 512],
                                in1=hbuf[:, hc * 512:(hc + 1) * 512], op=ALU.mult)
                v.drain().then_inc(ct.sem("qv"), 1)
            sqm = mk("qv")

            @blk.tensor
            def _(t, hc=hc, sqm=sqm):
                wait_marks(t, ct, [sqm])
                t.matmul(pB[0:1, 512:1024], cons["ones_bf"][:, 0:1],
                         dump[:, (hc % 2) * 512:(hc % 2) * 512 + 512],
                         start=(hc == 0), stop=(hc == 31))
                t.drain().then_inc(ct.sem("qp"), 1)
            sq_marks[hc] = mk("qp")
        sumsq_mark = sq_marks[31]

        @blk.vector
        def _(v, sum_mark=sum_mark, sumsq_mark=sumsq_mark):
            wait_marks(v, ct, [sum_mark, sumsq_mark])
            v.tensor_scalar_mul(abst[64:65, 512:1024], pB[0:1, 0:512], -1.0 / 4096.0)
            v.tensor_scalar_mul(abst[64:65, 0:512], pB[0:1, 512:1024], 1.0 / 4096.0)
            v.drain()
            v.scalar_tensor_tensor(out=pA[0:1, 0:512], in0=abst[64:65, 512:1024],
                                   scalar=-1.0, in1=abst[64:65, 512:1024],
                                   op0=ALU.mult, op1=ALU.mult)
            v.drain()
            v.tensor_tensor(out=abst[32:33, 0:512], in0=abst[64:65, 0:512],
                            in1=pA[0:1, 0:512], op=ALU.add)
            v.drain()
            v.tensor_scalar_add(abst[32:33, 0:512], abst[32:33, 0:512], 1e-5)
            v.drain().then_inc(ct.sem("qv"), 1)
        var_mark = mk("qv")

        @blk.scalar
        def _(s, var_mark=var_mark):
            wait_marks(s, ct, [var_mark])
            s.activation(out=abst[64:65, 0:512], in_=abst[32:33, 0:512], func=AF.Sqrt)
            s.drain().then_inc(ct.sem("qs"), 1)
        sqrt_mark = mk("qs")

        @blk.vector
        def _(v, sqrt_mark=sqrt_mark):
            wait_marks(v, ct, [sqrt_mark])
            v.reciprocal(abst[64:65, 0:512], abst[64:65, 0:512])
            v.drain()
            v.tensor_tensor(out=abst[64:65, 512:1024], in0=abst[64:65, 512:1024],
                            in1=abst[64:65, 0:512], op=ALU.mult)
            v.drain().then_inc(ct.sem("qv"), 1)
        ab_mark = mk("qv")

        @blk.tensor
        def _(t, ab_mark=ab_mark):
            wait_marks(t, ct, [ab_mark])
            t.matmul(pA[:, 0:512], cons["ones_f"][64:65, :],
                     abst[64:65, 0:512], start=True, stop=True)
            t.matmul(pA[:, 512:1024], cons["ones_f"][64:65, :],
                     abst[64:65, 512:1024], start=True, stop=True)
            t.drain().then_inc(ct.sem("qp"), 1)
        bc_mark = mk("qp")

        apl_mark = None
        for hc in range(32):
            @blk.vector
            def _(v, hc=hc, bc_mark=bc_mark):
                wait_marks(v, ct, [bc_mark])
                v.tensor_tensor(out=tmp5[:], in0=hbuf[:, hc * 512:(hc + 1) * 512],
                                in1=pA[:, 0:512], op=ALU.mult)
                v.drain()
                v.tensor_tensor(out=tmp5[:], in0=tmp5[:], in1=pA[:, 512:1024],
                                op=ALU.add)
                v.drain()
                v.tensor_scalar(out=hbuf[:, hc * 512:(hc + 1) * 512], in0=tmp5[:],
                                scalar1=lnc[:, hc:hc + 1],
                                scalar2=lnc[:, 32 + hc:33 + hc],
                                op0=ALU.mult, op1=ALU.add)
                v.drain().then_inc(ct.sem("qv"), 1)
            apl_mark = mk("qv")

        jt_stt = []
        for d in range(8):
            sem_nm = "wta" if d % 2 == 0 else "wtb"

            @blk.sync
            def _(sy, d=d, tg=tg, sem_nm=sem_nm,
                  prev=wp_mm.get((tg, d - 2)) or wp_mm.get((tg - 1, d + 6)),
                  wpq_all=wpq_all):
                sy.wait_ge(ct.sem("wps"), wpq_all)
                wait_marks(sy, ct, [prev] if prev else [])
                src = wpq_dram[:, d * 128:(d + 1) * 128].rearrange("(k p) i -> p k i",
                                                                  p=128)
                sy.dma_start(out=wpt[:, (d % 2) * 4096:(d % 2) * 4096 + 4096]
                             .rearrange("p (k i) -> p k i", k=32),
                             in_=src).then_inc(ct.sem(sem_nm), 16)
                ct.bump(sem_nm, 16)
            wp_t = ct.val(sem_nm)

            @blk.tensor
            def _(t, d=d, tg=tg, wp_t=wp_t, sem_nm=sem_nm, apl_mark=apl_mark,
                  fr=(stt_marks[-2] if len(stt_marks) >= 2 else None)):
                t.wait_ge(ct.sem(sem_nm), wp_t)
                wait_marks(t, ct, [apl_mark, fr])
                for k in range(32):
                    t.matmul(pB[:, (d % 2) * 512:(d % 2) * 512 + 512],
                             wpt[:, (d % 2) * 4096 + k * 128:
                                 (d % 2) * 4096 + k * 128 + 128],
                             hbuf[:, k * 512:(k + 1) * 512],
                             start=(k == 0), stop=(k == 31))
                t.drain().then_inc(ct.sem("qp"), 1)
            wp_mm[(tg, d)] = mk("qp")

            @blk.vector
            def _(v, d=d, tg=tg, m=wp_mm[(tg, d)], xjt_t=xjt_t):
                v.wait_ge(ct.sem("uld"), xjt_t)
                wait_marks(v, ct, [m])
                v.tensor_tensor(out=xjt[:, d * 512:(d + 1) * 512],
                                in0=pB[:, (d % 2) * 512:(d % 2) * 512 + 512],
                                in1=xjt[:, d * 512:(d + 1) * 512], op=ALU.add)
                v.drain()
                v.tensor_copy(xb[:, d * 2048 + tg * 512: d * 2048 + (tg + 1) * 512],
                              xjt[:, d * 512:(d + 1) * 512])
                v.drain().then_inc(ct.sem("qv"), 1)
            m2 = mk("qv")
            stt_marks.append(m2)
            jt_stt.append(m2)

        @blk.sync
        def _(sy, tg=tg, ms=list(jt_stt)):
            wait_marks(sy, ct, ms)
            for c in range(8):
                sy.dma_start(out=u_dram[c * 128:(c + 1) * 128, tg * 512:(tg + 1) * 512],
                             in_=xjt[:, c * 512:(c + 1) * 512]).then_inc(ct.sem("ust"), 16)
                ct.bump("ust", 16)
        ust_y = ct.val("ust")
    u_all = ct.val("ust")

    sq_free = None
    sums_marks = []
    for jt in range(4):
        @blk.scalar
        def _(s, jt=jt, ms=list(stt_marks), fr=sq_free):
            wait_marks(s, ct, ms + ([fr] if fr else []))
            for c in range(8):
                s.activation(out=hbuf[:, c * 512:(c + 1) * 512],
                             in_=xb[:, c * 2048 + jt * 512: c * 2048 + (jt + 1) * 512],
                             func=AF.Square)
            s.drain().then_inc(ct.sem("qs"), 1)
        sq_mark = mk("qs")

        @blk.tensor
        def _(t, jt=jt, sq_mark=sq_mark,
              fr=(stt_marks[-1] if jt == 0 else None)):
            wait_marks(t, ct, [sq_mark, fr])
            for c in range(8):
                t.matmul(pA[0:1, 0:512], cons["ones_bf"][:, 0:1],
                         xb[:, c * 2048 + jt * 512: c * 2048 + (jt + 1) * 512],
                         start=(c == 0), stop=(c == 7))
            for c in range(8):
                t.matmul(pB[0:1, 0:512], cons["ones_bf"][:, 0:1],
                         hbuf[:, c * 512:(c + 1) * 512],
                         start=(c == 0), stop=(c == 7))
            t.drain().then_inc(ct.sem("qp"), 1)
        sums_mm = mk("qp")

        @blk.vector
        def _(v, jt=jt, sums_mm=sums_mm):
            wait_marks(v, ct, [sums_mm])
            v.tensor_copy(abst[0:1, jt * 512:(jt + 1) * 512], pA[0:1, 0:512])
            v.drain()
            v.tensor_copy(abst[0:1, 2048 + jt * 512: 2048 + (jt + 1) * 512],
                          pB[0:1, 0:512])
            v.drain().then_inc(ct.sem("qv"), 1)
        sq_free = mk("qv")
        sums_marks.append(sq_free)

    @blk.vector
    def _(v, ms=list(sums_marks)):
        wait_marks(v, ct, ms)
        v.tensor_scalar_mul(abst[64:65, 2048:4096], abst[0:1, 0:2048], -1.0 / 1024.0)
        v.tensor_scalar_mul(abst[64:65, 0:2048], abst[0:1, 2048:4096], 1.0 / 1024.0)
        v.drain()
        v.scalar_tensor_tensor(out=pA[0:1, 0:2048], in0=abst[64:65, 2048:4096],
                               scalar=-1.0, in1=abst[64:65, 2048:4096],
                               op0=ALU.mult, op1=ALU.mult)
        v.drain()
        v.tensor_tensor(out=abst[32:33, 0:2048], in0=abst[64:65, 0:2048],
                        in1=pA[0:1, 0:2048], op=ALU.add)
        v.drain()
        v.tensor_scalar_add(abst[32:33, 0:2048], abst[32:33, 0:2048], 1e-5)
        v.drain().then_inc(ct.sem("qv"), 1)
    var_mark = mk("qv")

    @blk.scalar
    def _(s, var_mark=var_mark):
        wait_marks(s, ct, [var_mark])
        s.activation(out=abst[64:65, 0:2048], in_=abst[32:33, 0:2048], func=AF.Sqrt)
        s.drain().then_inc(ct.sem("qs"), 1)
    sqrt_mark = mk("qs")

    @blk.vector
    def _(v, sqrt_mark=sqrt_mark):
        wait_marks(v, ct, [sqrt_mark])
        v.reciprocal(abst[64:65, 0:2048], abst[64:65, 0:2048])
        v.drain()
        v.tensor_tensor(out=abst[64:65, 2048:4096], in0=abst[64:65, 2048:4096],
                        in1=abst[64:65, 0:2048], op=ALU.mult)
        v.drain().then_inc(ct.sem("qv"), 1)
    ab_mark2 = mk("qv")

    ap_mark = None
    st_t = ct.val("yst")
    for jt in range(4):
        @blk.tensor
        def _(t, jt=jt, ab_mark2=ab_mark2, fr=ap_mark,
              fr2=(sums_marks[-1] if jt == 0 else None)):
            wait_marks(t, ct, [ab_mark2, fr, fr2])
            t.matmul(pA[:, 0:512], cons["ones_f"][64:65, :],
                     abst[64:65, jt * 512:(jt + 1) * 512], start=True, stop=True)
            t.matmul(pB[:, 0:512], cons["ones_f"][64:65, :],
                     abst[64:65, 2048 + jt * 512:2048 + (jt + 1) * 512],
                     start=True, stop=True)
            t.drain().then_inc(ct.sem("qp"), 1)
        bc2 = mk("qp")

        @blk.sync
        def _(sy, jt=jt, st_t=st_t, u_all=u_all):
            sy.wait_ge(ct.sem("ust"), u_all)
            sy.wait_ge(ct.sem("yst"), st_t)
            for c in range(8):
                sy.dma_start(out=xjt[:, c * 512:(c + 1) * 512],
                             in_=u_dram[c * 128:(c + 1) * 128, jt * 512:(jt + 1) * 512]
                             ).then_inc(ct.sem("uld"), 16)
                ct.bump("uld", 16)
        u_t = ct.val("uld")

        @blk.vector
        def _(v, jt=jt, bc2=bc2, u_t=u_t):
            wait_marks(v, ct, [bc2])
            v.wait_ge(ct.sem("uld"), u_t)
            for c in range(8):
                v.tensor_tensor(out=xjt[:, c * 512:(c + 1) * 512],
                                in0=xjt[:, c * 512:(c + 1) * 512],
                                in1=pA[:, 0:512], op=ALU.mult)
            v.drain()
            for c in range(8):
                v.tensor_tensor(out=xjt[:, c * 512:(c + 1) * 512],
                                in0=xjt[:, c * 512:(c + 1) * 512],
                                in1=pB[:, 0:512], op=ALU.add)
            v.drain().then_inc(ct.sem("qv"), 1)
        ap_mark = mk("qv")

        @blk.sync
        def _(sy, jt=jt, ap_mark=ap_mark):
            wait_marks(sy, ct, [ap_mark])
            for c in range(8):
                sy.dma_start(out=y_ap[c * 128:(c + 1) * 128, jt * 512:(jt + 1) * 512],
                             in_=xjt[:, c * 512:(c + 1) * 512]).then_inc(ct.sem("yst"), 16)
                ct.bump("yst", 16)
        st_t = ct.val("yst")

    @blk.vector
    def _(v):
        v.wait_ge(ct.sem("yst"), ct.val("yst"))
        v.drain().then_inc(ct.sem("qv"), 1)
    ct.bump("qv")


def build_fused():
    nc = bass.Bass("TRN2", target_bir_lowering=False)
    dr = lambda nm, shp, dt=F32, kind="ExternalInput": nc.dram_tensor(nm, shp, dt, kind=kind).ap()
    x_ap = dr("x", [2048, 1024])
    io = {}
    for nm, shp in (("a0_wq", [1024, 1024]), ("a0_wk", [1024, 1024]),
                    ("a0_wv", [1024, 1024]), ("a0_wo", [1024, 1024]),
                    ("a0_biases", [128, 32]),
                    ("f0_gate_wT", [1024, 8]), ("f0_wg", [8192, 1408]),
                    ("f0_wu", [8192, 1408]), ("f0_wd", [11264, 1024]),
                    ("f0_swg", [1024, 2816]), ("f0_swu", [1024, 2816]),
                    ("f0_swd", [2816, 1024]),
                    ("a1_wq", [1024, 1024]), ("a1_wk", [1024, 1024]),
                    ("a1_wv", [1024, 1024]), ("a1_wo", [1024, 1024]),
                    ("a1_biases", [128, 32]), ("a1_bvrow", [1, 1024]),
                    ("f1_wfcT", [1024, 4096]), ("f1_wprojT", [4096, 1024]),
                    ("f1_lncols", [128, 64])):
        io[nm] = dr(nm, shp)
    cs = make_consts()
    for k, v in cs.items():
        io["c_" + k] = dr("c_" + k, list(v.shape))
    y_ap = dr("y_fm", [1024, 2048], F32, "ExternalOutput")
    x2d = dr("x2_scr", [1024, 2048], F32, "ExternalOutput")
    x3d = dr("x3_scr", [1024, 2048], F32, "ExternalOutput")
    u_dram = dr("u_scr", [1024, 2048], F32, "ExternalOutput")
    wpq_dram = dr("wpq_scr", [4096, 1024], BF16, "ExternalOutput")

    ct = Ctx(nc)
    with ct.es:
        xb = ct.sb("xb", [128, 16384], BF16)
        cons = dict(
            identf=ct.sb("s_identf", [128, 128], F32),
            identb=ct.sb("s_identb", [128, 128], BF16),
            ones_f=ct.sb("s_ones_f", [65, 128], F32),
            ones_bf=ct.sb("s_ones_bf", [128, 1], BF16),
            ones_cf=ct.sb("s_ones_cf", [128, 1], F32),
            ind_sc=ct.sb("s_ind_sc", [128, 8], BF16),
            ind_bc=ct.sb("s_ind_bc", [4, 256], BF16),
            sel8=ct.sb("s_sel8", [8, 1024], F32),
            mtri=ct.sb("s_mtri", [128, 128], F32),
            mtriT=ct.sb("s_mtriT", [128, 128], F32),
        )
        a0_bias = ct.sb("a0_bias_sb", [128, 32], F32)
        a1_bias = ct.sb("a1_bias_sb", [128, 32], F32)
        f1_lnc = ct.sb("f1_lnc_sb", [128, 64], F32)
        f0_gwf = ct.sb("f0_gwf_sb", [128, 64], F32)
        io["a1_biases_sb"] = a1_bias
        io["f1_lnc_sb"] = f1_lnc
        pA = ct.ps("pA", [128, 2048], F32)
        pB = ct.ps("pB", [128, 2048], F32)
        bank = [pA[:, i * 512:(i + 1) * 512] for i in range(4)] + \
               [pB[:, i * 512:(i + 1) * 512] for i in range(4)]
        blk = ct.es.enter_context(nc.Block())
        for s in ("lds", "ldg", "xld", "wqa", "wqb", "wda", "wdb", "aug",
                  "wqs", "woa", "wob", "spl", "x3s", "uld", "ust", "yst",
                  "wfa", "wfb", "wpa", "wpb", "wta", "wtb", "wps",
                  "qp", "qv", "qs", "qg"):
            ct.sem(s)

        @blk.sync
        def _(sy):
            for t_sb, src in ((cons["ones_f"], io["c_ones_f"]),
                              (cons["ones_cf"], io["c_ones_cf"]),
                              (cons["identf"], io["c_identf"]),
                              (cons["mtri"], io["c_mtri"]),
                              (cons["mtriT"], io["c_mtriT"]),
                              (cons["sel8"], io["c_sel8"]),
                              (a0_bias, io["a0_biases"]),
                              (a1_bias, io["a1_biases"]),
                              (f1_lnc, io["f1_lncols"])):
                sy.dma_start(out=t_sb[:], in_=src).then_inc(ct.sem("lds"), 16)
                ct.bump("lds", 16)

        @blk.gpsimd
        def _(g):
            for t_sb, src in ((cons["identb"], io["c_identf"]),
                              (cons["ones_bf"], io["c_ones_bf"]),
                              (cons["ind_sc"], io["c_ind_sc"]),
                              (cons["ind_bc"], io["c_ind_bc"])):
                g.dma_start(out=t_sb[:], in_=src).then_inc(ct.sem("ldg"), 16)
                ct.bump("ldg", 16)
            src = io["f0_gate_wT"].rearrange("(k p) e -> p k e", p=128)
            g.dma_start(out=f0_gwf[:].rearrange("p (k e) -> p k e", k=8),
                        in_=src).then_inc(ct.sem("ldg"), 16)
            ct.bump("ldg", 16)
        CONSTS_T = ct.val("lds")
        LDGC_T = ct.val("ldg")

        es_A = contextlib.ExitStack()
        with es_A:
            x_fm = ct.sb("x_fm", [128, 16384], F32, es=es_A)
            abst = ct.sb("ab_abst", [65, 4096], F32, es=es_A)
            es_a0 = contextlib.ExitStack()
            with es_a0:
                sqt0 = ct.sb("a0_sqt", [128, 4096], BF16, es=es_a0)
                m1 = emit_a0(ct, blk, x_ap, io, cons, x_fm, xb, bank, es_a0,
                             a0_bias, CONSTS_T, LDGC_T)
                ap0 = emit_postln(ct, blk, x_fm, xb, cons, pA, pB, sqt0, abst, m1)
                @blk.vector
                def _(v, ap0=ap0):
                    wait_marks(v, ct, [ap0])
                    for c in range(8):
                        v.tensor_copy(xb[:, c * 2048:(c + 1) * 2048],
                                      x_fm[:, c * 2048:(c + 1) * 2048])
                    v.drain().then_inc(ct.sem("qv"), 1)
                m2 = [ct.mk("qv")]
            emit_barrier_dma(ct, blk, [])
            m3, act_f0 = emit_f0(ct, blk, io, cons, x_fm, xb, pA, pB, m2, es_A,
                                 f0_gwf, CONSTS_T, LDGC_T)
            ap1 = emit_postln(ct, blk, x_fm, xb, cons, pA, pB,
                              act_f0[:, 0:4096], abst, m3)

            @blk.vector
            def _(v, ap1=ap1):
                wait_marks(v, ct, [ap1])
                for c in range(8):
                    v.tensor_copy(xb[:, c * 2048:(c + 1) * 2048],
                                  x_fm[:, c * 2048:(c + 1) * 2048])
                v.drain().then_inc(ct.sem("qv"), 1)
            m4 = [ct.mk("qv")]

            @blk.sync
            def _(sy, ap1=ap1):
                wait_marks(sy, ct, [ap1])
                for c in range(8):
                    sy.dma_start(out=x2d[c * 128:(c + 1) * 128, :],
                                 in_=x_fm[:, c * 2048:(c + 1) * 2048]
                                 ).then_inc(ct.sem("spl"), 16)
                    ct.bump("spl", 16)

        emit_barrier_dma(ct, blk, [("spl", ct.val("spl"))])

        es_B = contextlib.ExitStack()
        with es_B:
            x3_all = emit_a1(ct, blk, io, cons, x2d, x3d, u_dram, xb, pA, pB,
                             m4, es_B, CONSTS_T, LDGC_T)

        emit_barrier_dma(ct, blk, [("x3s", ct.val("x3s")), ("ust", ct.val("ust"))])

        es_C = contextlib.ExitStack()
        with es_C:
            emit_f1(ct, blk, io, cons, x3d, y_ap, u_dram, wpq_dram, xb, pA, pB,
                    ct.val("x3s"), es_C)
    return nc


def emit_barrier_dma(ct, blk, dma_marks):
    marks = {nm: ct.val(nm) for nm in ENG_SEMS}

    def waits(eng, own):
        for nm, v in marks.items():
            if nm != own and v > 0:
                eng.wait_ge(ct.sem(nm), v)
        for nm, v in dma_marks:
            if v > 0:
                eng.wait_ge(ct.sem(nm), v)

    @blk.tensor
    def _(t):
        waits(t, "qp")
        t.drain().then_inc(ct.sem("qp"), 1)
    ct.bump("qp")

    @blk.vector
    def _(v):
        waits(v, "qv")
        v.drain().then_inc(ct.sem("qv"), 1)
    ct.bump("qv")

    @blk.scalar
    def _(s):
        waits(s, "qs")
        s.drain().then_inc(ct.sem("qs"), 1)
    ct.bump("qs")

    @blk.gpsimd
    def _(g):
        waits(g, "qg")
        g.drain().then_inc(ct.sem("qg"), 1)
    ct.bump("qg")


def host_inputs(inp):
    cs = make_consts()
    perm = np.concatenate([np.arange(64) + h * 64 for r in range(4)
                           for h in (r, r + 4, r + 8, r + 12)])
    a0_biases = np.zeros((128, 32), np.float32)
    for j, nm in enumerate(("bq", "bk", "bv")):
        a0_biases[:, j * 8:(j + 1) * 8] = np.asarray(inp["L0_attn_" + nm])[perm].reshape(8, 128).T
    a0_biases[:, 24:32] = np.asarray(inp["L0_attn_bo"]).reshape(8, 128).T
    a1_biases = np.zeros((128, 32), np.float32)
    a1_biases[:, 0:8] = np.asarray(inp["L1_attn_bq"]).reshape(8, 128).T
    a1_biases[:, 8:16] = np.asarray(inp["L1_attn_bk"]).reshape(8, 128).T
    a1_biases[:, 24:32] = np.asarray(inp["L1_attn_bo"]).reshape(8, 128).T
    lncols = np.zeros((128, 64), np.float32)
    lncols[:, 0:32] = np.asarray(inp["L1_ffn_ln_w"]).reshape(32, 128).T
    lncols[:, 32:64] = np.asarray(inp["L1_ffn_ln_b"]).reshape(32, 128).T
    base = dict(
        a0_wq=np.ascontiguousarray(np.asarray(inp["L0_attn_Wq"])[:, perm]),
        a0_wk=np.ascontiguousarray(np.asarray(inp["L0_attn_Wk"])[:, perm]),
        a0_wv=np.ascontiguousarray(np.asarray(inp["L0_attn_Wv"])[:, perm]),
        a0_wo=np.ascontiguousarray(np.asarray(inp["L0_attn_Wo"])[perm, :]),
        a0_biases=a0_biases,
        f0_gate_wT=np.ascontiguousarray(np.asarray(inp["L0_ffn_gate_w"]).T),
        f0_wg=np.ascontiguousarray(np.asarray(inp["L0_ffn_wg"]).reshape(8192, 1408)),
        f0_wu=np.ascontiguousarray(np.asarray(inp["L0_ffn_wu"]).reshape(8192, 1408)),
        f0_wd=np.ascontiguousarray(np.asarray(inp["L0_ffn_wd"]).reshape(11264, 1024)),
        f0_swg=np.ascontiguousarray(np.asarray(inp["L0_ffn_swg"])),
        f0_swu=np.ascontiguousarray(np.asarray(inp["L0_ffn_swu"])),
        f0_swd=np.ascontiguousarray(np.asarray(inp["L0_ffn_swd"])),
        a1_wq=np.ascontiguousarray(np.asarray(inp["L1_attn_Wq"])),
        a1_wk=np.ascontiguousarray(np.asarray(inp["L1_attn_Wk"])),
        a1_wv=np.ascontiguousarray(np.asarray(inp["L1_attn_Wv"])),
        a1_wo=np.ascontiguousarray(np.asarray(inp["L1_attn_Wo"])),
        a1_biases=a1_biases,
        a1_bvrow=np.ascontiguousarray(np.asarray(inp["L1_attn_bv"]).reshape(1, 1024)),
        f1_wfcT=np.ascontiguousarray(np.asarray(inp["L1_ffn_W_fc"]).T),
        f1_wprojT=np.ascontiguousarray(np.asarray(inp["L1_ffn_W_proj"]).T),
        f1_lncols=lncols,
        **{"c_" + k: np.ascontiguousarray(v) for k, v in cs.items()})
    x = np.asarray(inp["L0_attn_x"])
    return [dict(base, x=np.ascontiguousarray(x[b])) for b in range(8)]




_PROG = []


def kernel(**inputs):
    inp = {k: np.asarray(v) for k, v in inputs.items()}
    if not _PROG:
        _PROG.append(build_fused())
    nc = _PROG[0]
    in_maps = host_inputs(inp)
    res = run_bass_kernel_spmd(nc, in_maps, core_ids=list(range(8)))
    return np.stack([r["y_fm"].T for r in res.results]).astype(np.float32)


# version 06
# speedup vs baseline: 1.2345x; 1.2345x over previous
import numpy as np
import ml_dtypes
import contextlib
import concourse.bass as bass
import concourse.mybir as mybir
from concourse.bass_utils import run_bass_kernel_spmd

F32 = mybir.dt.float32
BF16 = mybir.dt.bfloat16
AF = mybir.ActivationFunctionType
ALU = mybir.AluOpType
AX = mybir.AxisListType
S8 = float(1.0 / np.sqrt(8.0))
NW = 1024 * 4096


class Ctx:
    def __init__(self, nc):
        self.nc = nc
        self.es = contextlib.ExitStack()
        self._sem_counts = {}
        self._sems = {}

    def sb(self, name, shape, dtype=F32, es=None):
        return (es or self.es).enter_context(self.nc.sbuf_tensor(name, shape, dtype))

    def ps(self, name, shape, dtype=F32):
        return self.es.enter_context(self.nc.psum_tensor(name, shape, dtype))

    def sem(self, name):
        if name not in self._sems:
            self._sems[name] = self.es.enter_context(self.nc.semaphore(name))
            self._sem_counts[name] = 0
        return self._sems[name]

    def bump(self, name, k=1):
        self._sem_counts[name] += k
        return self._sem_counts[name]

    def val(self, name):
        return self._sem_counts.get(name, 0)

    def mk(self, nm):
        return (nm, self.bump(nm))


ENG_SEMS = ("qp", "qv", "qs", "qg")


def wait_marks(eng, ct, marks):
    best = {}
    for m in marks:
        if m is None:
            continue
        nm, v = m
        best[nm] = max(best.get(nm, 0), v)
    for nm, v in best.items():
        if v > 0:
            eng.wait_ge(ct.sem(nm), v)


def all_marks(ct, exclude=()):
    return [(nm, ct.val(nm)) for nm in ENG_SEMS
            if nm not in exclude and ct.val(nm) > 0]


def emit_barrier(ct, blk):
    marks = {nm: ct.val(nm) for nm in ENG_SEMS}

    @blk.tensor
    def _(t):
        wait_marks(t, ct, [(nm, v) for nm, v in marks.items() if nm != "qp"])
        t.drain().then_inc(ct.sem("qp"), 1)
    ct.bump("qp")

    @blk.vector
    def _(v):
        wait_marks(v, ct, [(nm, v2) for nm, v2 in marks.items() if nm != "qv"])
        v.drain().then_inc(ct.sem("qv"), 1)
    ct.bump("qv")

    @blk.scalar
    def _(s):
        wait_marks(s, ct, [(nm, v) for nm, v in marks.items() if nm != "qs"])
        s.drain().then_inc(ct.sem("qs"), 1)
    ct.bump("qs")

    @blk.gpsimd
    def _(g):
        wait_marks(g, ct, [(nm, v) for nm, v in marks.items() if nm != "qg"])
        g.drain().then_inc(ct.sem("qg"), 1)
    ct.bump("qg")


def hi_lo(v):
    hi = v.astype(ml_dtypes.bfloat16).astype(np.float32)
    lo = (v - hi).astype(ml_dtypes.bfloat16).astype(np.float32)
    return hi, lo


def make_consts():
    identf = np.eye(128, dtype=np.float32)
    ones_f = np.ones((65, 128), np.float32)
    ones_bf = np.ones((128, 1), np.float32)
    ones_cf = np.ones((128, 1), np.float32)
    ind_sc = np.zeros((128, 8), np.float32)
    ind_sc[0:64, 0] = 1; ind_sc[64:128, 1] = 1
    ind_sc[0:64, 6] = 1; ind_sc[64:128, 7] = 1
    ind_bc = np.zeros((4, 256), np.float32)
    ind_bc[0, 0:64] = 1; ind_bc[1, 64:128] = 1
    ind_bc[2, 128:192] = 1; ind_bc[3, 192:256] = 1
    sel8 = np.zeros((8, 1024), np.float32)
    for e in range(8):
        sel8[e, e * 128:(e + 1) * 128] = 1.0
    mtri = np.where(np.arange(128)[None, :] <= np.arange(128)[:, None], 0.0,
                    -1e30).astype(np.float32)
    mtriT = np.ascontiguousarray(mtri.T)
    k_ar = np.arange(2048, dtype=np.float32)
    qaug = np.zeros((16, 4, 2048), np.float32)
    kaug = np.zeros((16, 6, 2048), np.float32)
    for h in range(16):
        slope = 2.0 ** (-(h + 1) / 2.0)
        khi, klo = hi_lo(slope * k_ar)
        qhi, qlo = hi_lo(-slope * k_ar)
        qaug[h, 0] = 1.0; qaug[h, 1] = 1.0; qaug[h, 2] = qhi; qaug[h, 3] = qlo
        kaug[h, 0] = 1.0; kaug[h, 1] = 1.0; kaug[h, 2] = khi; kaug[h, 3] = klo
        kaug[h, 4] = 1.0; kaug[h, 5] = 1.0
    return dict(identf=identf, ones_f=ones_f, ones_bf=ones_bf, ones_cf=ones_cf,
                ind_sc=ind_sc, ind_bc=ind_bc, sel8=sel8, mtri=mtri, mtriT=mtriT,
                qaug=qaug.reshape(64, 2048), kaug=kaug.reshape(96, 2048))


def emit_postln(ct, blk, x_fm, xb, cons, pA, pB, sqt, abst, xb_marks):
    mk = ct.mk
    sq_free = None
    sums_marks = []
    for jt in range(4):
        @blk.scalar
        def _(s, jt=jt, ms=list(xb_marks), fr=sq_free):
            wait_marks(s, ct, ms + ([fr] if fr else []))
            for c in range(8):
                s.activation(out=sqt[:, c * 512:(c + 1) * 512],
                             in_=xb[:, c * 2048 + jt * 512: c * 2048 + (jt + 1) * 512],
                             func=AF.Square)
            s.drain().then_inc(ct.sem("qs"), 1)
        sq_mark = mk("qs")

        @blk.tensor
        def _(t, jt=jt, sq_mark=sq_mark, ms=list(xb_marks),
              pre=all_marks(ct, exclude=("qp",)) if jt == 0 else []):
            t.wait_ge(ct.sem("ldg"), 80)
            wait_marks(t, ct, [sq_mark] + ms + pre)
            for c in range(8):
                t.matmul(pA[0:1, 0:512], cons["ones_bf"][:, 0:1],
                         xb[:, c * 2048 + jt * 512: c * 2048 + (jt + 1) * 512],
                         start=(c == 0), stop=(c == 7))
            for c in range(8):
                t.matmul(pB[0:1, 0:512], cons["ones_bf"][:, 0:1],
                         sqt[:, c * 512:(c + 1) * 512],
                         start=(c == 0), stop=(c == 7))
            t.drain().then_inc(ct.sem("qp"), 1)
        sums_mm = mk("qp")

        @blk.vector
        def _(v, jt=jt, sums_mm=sums_mm):
            wait_marks(v, ct, [sums_mm])
            v.tensor_copy(abst[0:1, jt * 512:(jt + 1) * 512], pA[0:1, 0:512])
            v.drain()
            v.tensor_copy(abst[0:1, 2048 + jt * 512: 2048 + (jt + 1) * 512],
                          pB[0:1, 0:512])
            v.drain().then_inc(ct.sem("qv"), 1)
        sq_free = mk("qv")
        sums_marks.append(sq_free)

    @blk.vector
    def _(v, ms=list(sums_marks)):
        wait_marks(v, ct, ms)
        v.tensor_scalar_mul(abst[64:65, 2048:4096], abst[0:1, 0:2048], -1.0 / 1024.0)
        v.tensor_scalar_mul(abst[64:65, 0:2048], abst[0:1, 2048:4096], 1.0 / 1024.0)
        v.drain()
        v.scalar_tensor_tensor(out=pA[0:1, 0:2048], in0=abst[64:65, 2048:4096],
                               scalar=-1.0, in1=abst[64:65, 2048:4096],
                               op0=ALU.mult, op1=ALU.mult)
        v.drain()
        v.tensor_tensor(out=abst[32:33, 0:2048], in0=abst[64:65, 0:2048],
                        in1=pA[0:1, 0:2048], op=ALU.add)
        v.drain()
        v.tensor_scalar_add(abst[32:33, 0:2048], abst[32:33, 0:2048], 1e-5)
        v.drain().then_inc(ct.sem("qv"), 1)
    var_mark = mk("qv")

    @blk.scalar
    def _(s, var_mark=var_mark):
        wait_marks(s, ct, [var_mark])
        s.activation(out=abst[64:65, 0:2048], in_=abst[32:33, 0:2048], func=AF.Sqrt)
        s.drain().then_inc(ct.sem("qs"), 1)
    sqrt_mark = mk("qs")

    @blk.vector
    def _(v, sqrt_mark=sqrt_mark):
        wait_marks(v, ct, [sqrt_mark])
        v.reciprocal(abst[64:65, 0:2048], abst[64:65, 0:2048])
        v.drain()
        v.tensor_tensor(out=abst[64:65, 2048:4096], in0=abst[64:65, 2048:4096],
                        in1=abst[64:65, 0:2048], op=ALU.mult)
        v.drain().then_inc(ct.sem("qv"), 1)
    ab_mark = mk("qv")

    ap_mark = None
    for jt in range(4):
        @blk.tensor
        def _(t, jt=jt, ab_mark=ab_mark, fr=ap_mark,
              fr2=(sums_marks[-1] if jt == 0 else None)):
            wait_marks(t, ct, [ab_mark, fr, fr2])
            t.matmul(pA[:, 0:512], cons["ones_f"][64:65, :],
                     abst[64:65, jt * 512:(jt + 1) * 512], start=True, stop=True)
            t.matmul(pB[:, 0:512], cons["ones_f"][64:65, :],
                     abst[64:65, 2048 + jt * 512:2048 + (jt + 1) * 512],
                     start=True, stop=True)
            t.drain().then_inc(ct.sem("qp"), 1)
        bc_mark = mk("qp")

        @blk.vector
        def _(v, jt=jt, bc_mark=bc_mark):
            wait_marks(v, ct, [bc_mark])
            for c in range(8):
                sl = slice(c * 2048 + jt * 512, c * 2048 + (jt + 1) * 512)
                v.tensor_tensor(out=x_fm[:, sl], in0=x_fm[:, sl], in1=pA[:, 0:512],
                                op=ALU.mult)
            v.drain()
            for c in range(8):
                sl = slice(c * 2048 + jt * 512, c * 2048 + (jt + 1) * 512)
                v.tensor_tensor(out=x_fm[:, sl], in0=x_fm[:, sl], in1=pB[:, 0:512],
                                op=ALU.add)
            v.drain().then_inc(ct.sem("qv"), 1)
        ap_mark = mk("qv")
    return ap_mark


def emit_a0(ct, blk, x_ap, io, cons, x_fm, xb, bank, es, bias, CONSTS_T, LDGC_T):
    nc = ct.nc
    mk = ct.mk
    xtok = ct.sb("a0_xtok", [128, 2048], F32, es=es)
    wq = ct.sb("a0_wq_sb", [128, 2 * 2048], BF16, es=es)
    wk = ct.sb("a0_wk_sb", [128, 2 * 2048], BF16, es=es)
    wv = ct.sb("a0_wv_sb", [128, 2 * 2048], BF16, es=es)
    wo = ct.sb("a0_wo_sb", [128, 2 * 2048], BF16, es=es)
    qr = ct.sb("a0_qr", [128, 1024], BF16, es=es)
    kr = ct.sb("a0_kr", [128, 1024], BF16, es=es)
    vr = ct.sb("a0_vr", [128, 1024], BF16, es=es)
    prod = ct.sb("a0_prod", [128, 1024], BF16, es=es)
    probs = ct.sb("a0_probs", [4, 512], BF16, es=es)
    mrow = ct.sb("a0_mrow", [4, 4], F32, es=es)
    lrow = ct.sb("a0_lrow", [4, 4], F32, es=es)
    linv = ct.sb("a0_linv", [4, 4], F32, es=es)
    wvs = ct.sb("a0_wvs", [128, 1024], BF16, es=es)

    @blk.sync
    def _(sy):
        for i in range(16):
            if i >= 2:
                wait_marks(sy, ct, [("qp", i - 1)])
            sy.dma_start(out=xtok[:, (i % 2) * 1024:(i % 2) * 1024 + 1024],
                         in_=x_ap[i * 128:(i + 1) * 128, :]).then_inc(ct.sem("xld"), 16)
            ct.bump("xld", 16)

    evv = {}
    evs = {}
    for i in range(16):
        @blk.tensor
        def _(t, i=i, ev=evv.get(i - 1), es_=evs.get(i - 1)):
            t.wait_ge(ct.sem("xld"), min(16 * (i + 2), 256))
            t.wait_ge(ct.sem("lds"), CONSTS_T)
            wait_marks(t, ct, [ev, es_])
            for g4 in range(2):
                for c4 in range(4):
                    c = g4 * 4 + c4
                    t.transpose(bank[6 + g4][:, c4 * 128:(c4 + 1) * 128],
                                xtok[:, (i % 2) * 1024 + c * 128:
                                     (i % 2) * 1024 + (c + 1) * 128],
                                cons["identf"][:])
            t.drain().then_inc(ct.sem("qp"), 1)
        td = mk("qp")

        @blk.vector
        def _(v, i=i, td=td):
            wait_marks(v, ct, [td])
            dst = x_fm.rearrange("p (c t) -> p c t", c=8)[:, 0:4, i * 128:(i + 1) * 128]
            v.tensor_copy(dst, bank[6][:].rearrange("p (c q) -> p c q", c=4))
            v.drain()
            dst = x_fm.rearrange("p (c t) -> p c t", c=8)[:, 4:8, i * 128:(i + 1) * 128]
            v.tensor_copy(dst, bank[7][:].rearrange("p (c q) -> p c q", c=4))
            v.drain().then_inc(ct.sem("qv"), 1)
        evv[i] = mk("qv")

        @blk.scalar
        def _(s, i=i, td=td):
            wait_marks(s, ct, [td])
            for g4 in range(2):
                dst = xb.rearrange("p (c t) -> p c t", c=8)[:, g4 * 4:(g4 + 1) * 4,
                                                            i * 128:(i + 1) * 128]
                s.copy(out=dst, in_=bank[6 + g4][:].rearrange("p (c q) -> p c q", c=4))
            s.drain().then_inc(ct.sem("qs"), 1)
        evs[i] = mk("qs")
    x_ready = [evv[15], evs[15]]

    cls_t = {}
    cls_compute_done = {}

    def emit_class_dma(g, r):
        half = (r % 2) * 2048
        sem_nm = "wqa" if r % 2 == 0 else "wqb"
        for w_sb, nm in ((wq, "a0_wq"), (wk, "a0_wk"), (wv, "a0_wv")):
            src = io[nm][:, r * 256:(r + 1) * 256].rearrange("(k p) i -> p k i", p=128)
            dst = w_sb[:, half:half + 2048].rearrange("p (k i) -> p k i", k=8)
            g.dma_start(out=dst, in_=src).then_inc(ct.sem(sem_nm), 16)
            ct.bump(sem_nm, 16)
        src = io["a0_wo"][r * 256:(r + 1) * 256, :].rearrange("(k p) i -> p k i", p=128)
        dst = wo[:, half:half + 2048].rearrange("p (k i) -> p k i", k=2)
        g.dma_start(out=dst, in_=src).then_inc(ct.sem(sem_nm), 16)
        ct.bump(sem_nm, 16)
        cls_t[r] = (sem_nm, ct.val(sem_nm))

    @blk.gpsimd
    def _(g):
        emit_class_dma(g, 0)

    @blk.gpsimd
    def _(g):
        emit_class_dma(g, 1)

    for r in range(4):
        half = (r % 2) * 2048

        @blk.tensor
        def _(t, r=r, half=half, xr=list(x_ready), pre=all_marks(ct, ("qp",)) if r == 0 else []):
            t.wait_ge(ct.sem(cls_t[r][0]), cls_t[r][1])
            t.wait_ge(ct.sem("lds"), CONSTS_T)
            t.wait_ge(ct.sem("ldg"), LDGC_T)
            wait_marks(t, ct, xr + pre + ([cls_compute_done[r - 1]] if r > 0 else []))
            for dst_banks, w_sb in (((0, 1), wq), ((2, 3), wk), ((4, 5), wv)):
                for ic in range(2):
                    for k in range(8):
                        t.matmul(bank[dst_banks[ic]][:],
                                 w_sb[:, half + k * 256 + ic * 128:
                                      half + k * 256 + ic * 128 + 128],
                                 xb[:, k * 2048 + r: (k + 1) * 2048: 4],
                                 start=(k == 0), stop=(k == 7))
            t.drain().then_inc(ct.sem("qp"), 1)
        qkv_mm = mk("qp")

        @blk.vector
        def _(v, r=r, qkv_mm=qkv_mm):
            wait_marks(v, ct, [qkv_mm])
            for ic in range(2):
                v.tensor_scalar(out=qr[:, ic * 512:(ic + 1) * 512], in0=bank[ic][:],
                                scalar1=bias[:, r * 2 + ic: r * 2 + ic + 1], scalar2=None,
                                op0=ALU.add)
                v.tensor_scalar(out=kr[:, ic * 512:(ic + 1) * 512], in0=bank[2 + ic][:],
                                scalar1=bias[:, 8 + r * 2 + ic: 8 + r * 2 + ic + 1],
                                scalar2=None, op0=ALU.add)
                v.tensor_scalar(out=vr[:, ic * 512:(ic + 1) * 512], in0=bank[4 + ic][:],
                                scalar1=bias[:, 16 + r * 2 + ic: 16 + r * 2 + ic + 1],
                                scalar2=None, op0=ALU.add)
            v.drain()
            v.tensor_tensor(out=prod[:], in0=qr[:], in1=kr[:], op=ALU.mult)
            v.drain().then_inc(ct.sem("qv"), 1)
        prod_mark = mk("qv")

        @blk.tensor
        def _(t, prod_mark=prod_mark):
            wait_marks(t, ct, [prod_mark])
            t.matmul(bank[6][0:4, :], cons["ind_sc"][:, 0:4], prod[:, 0:512],
                     start=True, stop=False)
            t.matmul(bank[6][0:4, :], cons["ind_sc"][:, 4:8], prod[:, 512:1024],
                     start=False, stop=True)
            t.drain().then_inc(ct.sem("qp"), 1)
        sc_mark = mk("qp")

        @blk.vector
        def _(v, sc_mark=sc_mark):
            wait_marks(v, ct, [sc_mark])
            for n in range(4):
                v.reduce_max(mrow[:, n:n + 1], bank[6][0:4, n * 128:(n + 1) * 128],
                             axis=AX.X, negate=True)
            v.drain()
            v.tensor_scalar_mul(mrow[:], mrow[:], 0.125)
            v.drain().then_inc(ct.sem("qv"), 1)
        m_mark = mk("qv")

        @blk.scalar
        def _(s, m_mark=m_mark):
            wait_marks(s, ct, [m_mark])
            for n in range(4):
                s.activation(out=probs[:, n * 128:(n + 1) * 128],
                             in_=bank[6][0:4, n * 128:(n + 1) * 128],
                             func=AF.Exp, scale=0.125, bias=mrow[:, n:n + 1],
                             accum_out=lrow[:, n:n + 1])
            s.drain().then_inc(ct.sem("qs"), 1)
        e_mark = mk("qs")

        @blk.vector
        def _(v, e_mark=e_mark):
            wait_marks(v, ct, [e_mark])
            v.reciprocal(linv[:], lrow[:])
            v.drain()
            for n in range(4):
                v.tensor_scalar_mul(probs[:, n * 128:(n + 1) * 128],
                                    probs[:, n * 128:(n + 1) * 128], linv[:, n:n + 1])
            v.drain().then_inc(ct.sem("qv"), 1)
        p_mark = mk("qv")

        @blk.tensor
        def _(t, p_mark=p_mark):
            wait_marks(t, ct, [p_mark])
            t.matmul(bank[0][:], cons["ind_bc"][0:4, 0:128], probs[:], start=True, stop=True)
            t.matmul(bank[1][:], cons["ind_bc"][0:4, 128:256], probs[:], start=True, stop=True)
            t.drain().then_inc(ct.sem("qp"), 1)
        bc_mark = mk("qp")

        @blk.vector
        def _(v, bc_mark=bc_mark):
            wait_marks(v, ct, [bc_mark])
            v.tensor_tensor(out=wvs[:, 0:512], in0=bank[0][:], in1=vr[:, 0:512], op=ALU.mult)
            v.tensor_tensor(out=wvs[:, 512:1024], in0=bank[1][:], in1=vr[:, 512:1024],
                            op=ALU.mult)
            v.drain().then_inc(ct.sem("qv"), 1)
        wv_mark = mk("qv")

        ev_mark = None
        for dpair in range(4):
            d0, d1 = 2 * dpair, 2 * dpair + 1

            @blk.tensor
            def _(t, half=half, d0=d0, d1=d1, wv_mark=wv_mark, fr=ev_mark):
                wait_marks(t, ct, [wv_mark, fr])
                for dd, bk_i in ((d0, 2), (d1, 3)):
                    for k in range(2):
                        t.matmul(bank[bk_i][:],
                                 wo[:, half + k * 1024 + dd * 128:
                                     half + k * 1024 + dd * 128 + 128],
                                 wvs[:, k * 512:(k + 1) * 512],
                                 start=(k == 0), stop=(k == 1))
                t.drain().then_inc(ct.sem("qp"), 1)
            y_mark = mk("qp")

            @blk.vector
            def _(v, r=r, d0=d0, d1=d1, y_mark=y_mark):
                wait_marks(v, ct, [y_mark])
                for dd, bk_i in ((d0, 2), (d1, 3)):
                    sl = slice(dd * 2048 + r, (dd + 1) * 2048, 4)
                    v.scalar_tensor_tensor(out=x_fm[:, sl], in0=bank[bk_i][:],
                                           scalar=bias[:, 24 + dd: 24 + dd + 1],
                                           in1=x_fm[:, sl], op0=ALU.add, op1=ALU.add)
                v.drain().then_inc(ct.sem("qv"), 1)
            ev_mark = mk("qv")
        cls_compute_done[r] = ev_mark

        if r < 2:
            @blk.gpsimd
            def _(g, r=r, gate=ev_mark):
                wait_marks(g, ct, [gate])
                emit_class_dma(g, r + 2)

    @blk.vector
    def _(v, gate=cls_compute_done[3]):
        wait_marks(v, ct, [gate])
        for c in range(8):
            v.tensor_copy(xb[:, c * 2048:(c + 1) * 2048], x_fm[:, c * 2048:(c + 1) * 2048])
        v.drain().then_inc(ct.sem("qv"), 1)
    return [mk("qv")]


def emit_f0(ct, blk, io, cons, x_fm, xb, pA, pB, xb_marks, es, gwf, CONSTS_T, LDGC_T):
    mk = ct.mk
    act = ct.sb("f0_act", [128, 11 * 2048], BF16, es=es)
    xlo = act[:, 0:16384]
    ghi = ct.sb("f0_ghi", [128, 64], BF16, es=es)
    glo = ct.sb("f0_glo", [128, 64], BF16, es=es)
    lgt = ct.sb("f0_lgt", [128, 8], F32, es=es)
    gA = ct.sb("f0_gA", [128, 8], F32, es=es)
    gB = ct.sb("f0_gB", [128, 8], F32, es=es)
    m1 = ct.sb("f0_m1", [128, 1], F32, es=es)
    m2 = ct.sb("f0_m2", [128, 1], F32, es=es)
    dn = ct.sb("f0_dn", [128, 1], F32, es=es)
    wfm = ct.sb("f0_wfm", [8, 2048], F32, es=es)
    wbc = ct.sb("f0_wbc", [128, 2048], BF16, es=es)
    sg = ct.sb("f0_sg", [128, 2048], F32, es=es)
    lgf = sg[0:8, 0:2048]
    wgu = ct.sb("f0_wgu", [128, 4 * 2048], BF16, es=es)
    wdb = ct.sb("f0_wdb", [128, 2 * 1408], BF16, es=es)

    @blk.vector
    def _(v, ms=list(xb_marks)):
        v.wait_ge(ct.sem("ldg"), LDGC_T)
        wait_marks(v, ct, ms)
        v.tensor_copy(ghi[:], gwf[:])
        v.drain()
        v.tensor_tensor(out=glo[:], in0=gwf[:], in1=ghi[:], op=ALU.subtract)
        for c in range(8):
            v.tensor_tensor(out=xlo[:, c * 2048:(c + 1) * 2048],
                            in0=x_fm[:, c * 2048:(c + 1) * 2048],
                            in1=xb[:, c * 2048:(c + 1) * 2048], op=ALU.subtract)
        v.drain().then_inc(ct.sem("qv"), 1)
    gate_prep = mk("qv")

    lg_ev = {}
    for jt in range(4):
        @blk.tensor
        def _(t, jt=jt, gate_prep=gate_prep, pre=all_marks(ct, ("qp",)) if jt == 0 else []):
            wait_marks(t, ct, [gate_prep] + pre + ([lg_ev[jt - 1]] if jt > 0 else []))
            first = True
            for xsrc in (xb, xlo):
                for gsrc in (ghi, glo):
                    for k in range(8):
                        t.matmul(pA[0:8, jt * 512:(jt + 1) * 512],
                                 gsrc[:, k * 8:(k + 1) * 8],
                                 xsrc[:, k * 2048 + jt * 512: k * 2048 + (jt + 1) * 512],
                                 start=first, stop=(xsrc is xlo and gsrc is glo and k == 7))
                        first = False
            t.drain().then_inc(ct.sem("qp"), 1)
        lg_mm = mk("qp")

        @blk.vector
        def _(v, jt=jt, lg_mm=lg_mm):
            wait_marks(v, ct, [lg_mm])
            v.tensor_copy(lgf[:, jt * 512:(jt + 1) * 512], pA[0:8, jt * 512:(jt + 1) * 512])
            v.drain().then_inc(ct.sem("qv"), 1)
        lg_ev[jt] = mk("qv")

    wfm_marks = []
    tp_free = None
    for tt in range(16):
        @blk.tensor
        def _(t, tt=tt, gate=lg_ev[3], fr=tp_free):
            wait_marks(t, ct, [gate, fr])
            t.transpose(pB[0:128, 0:8], lgf[:, tt * 128:(tt + 1) * 128],
                        cons["identf"][0:8, 0:8])
            t.drain().then_inc(ct.sem("qp"), 1)
        tpm = mk("qp")

        @blk.vector
        def _(v, tt=tt, tpm=tpm):
            wait_marks(v, ct, [tpm])
            v.tensor_copy(lgt[:], pB[0:128, 0:8])
            v.drain()
            v.reduce_max(m1[:], lgt[:], axis=AX.X, negate=True)
            v.drain()
            v.tensor_scalar(out=gA[:], in0=lgt[:], scalar1=m1[:], scalar2=0.0,
                            op0=ALU.add, op1=ALU.is_equal)
            v.drain()
            v.scalar_tensor_tensor(out=gB[:], in0=gA[:], scalar=-1e30, in1=lgt[:],
                                   op0=ALU.mult, op1=ALU.add)
            v.drain()
            v.reduce_max(m2[:], gB[:], axis=AX.X, negate=True)
            v.drain().then_inc(ct.sem("qv"), 1)
        m_mark = mk("qv")

        @blk.scalar
        def _(s, m_mark=m_mark):
            wait_marks(s, ct, [m_mark])
            s.activation(out=gA[:], in_=lgt[:], func=AF.Exp, bias=m1[:])
            s.drain().then_inc(ct.sem("qs"), 1)
        e_mark = mk("qs")

        @blk.vector
        def _(v, e_mark=e_mark):
            wait_marks(v, ct, [e_mark])
            v.tensor_scalar(out=gB[:], in0=lgt[:], scalar1=m2[:], scalar2=0.0,
                            op0=ALU.add, op1=ALU.is_ge)
            v.drain()
            v.tensor_tensor(out=gA[:], in0=gA[:], in1=gB[:], op=ALU.mult)
            v.drain()
            v.reduce_sum(dn[:], gA[:], axis=AX.X)
            v.drain()
            v.reciprocal(dn[:], dn[:])
            v.drain()
            v.tensor_scalar_mul(gA[:], gA[:], dn[:])
            v.drain().then_inc(ct.sem("qv"), 1)
        w_mark = mk("qv")

        @blk.tensor
        def _(t, w_mark=w_mark):
            wait_marks(t, ct, [w_mark])
            t.transpose(pB[0:8, 0:128], gA[:], cons["identf"][:])
            t.drain().then_inc(ct.sem("qp"), 1)
        wtm = mk("qp")

        @blk.vector
        def _(v, tt=tt, wtm=wtm):
            wait_marks(v, ct, [wtm])
            v.tensor_copy(wfm[:, tt * 128:(tt + 1) * 128], pB[0:8, 0:128])
            v.drain().then_inc(ct.sem("qv"), 1)
        tp_free = mk("qv")
        wfm_marks.append(tp_free)
    wfm_mark = wfm_marks[-1]

    def pair_list(n):
        out = []
        i = 0
        while i < n:
            out.append((i, min(2, n - i)))
            i += 2
        return out

    slab_gate = [wfm_mark]

    def emit_hg_hu_act(n_ic, wg_name, wu_name, wrow0, scale, wcol0=0):
        pairs = pair_list(n_ic)
        dma_t = {}
        ic_done = {}

        def emit_wdma(g, pi, gates):
            ic0, cnt = pairs[pi]
            sem_nm = "wqa" if pi % 2 == 0 else "wqb"
            wait_marks(g, ct, gates)
            for j, nm in ((0, wg_name), (1, wu_name)):
                slot = (pi % 2) + 2 * j
                for o in range(cnt):
                    ic = ic0 + o
                    src = io[nm][wrow0:wrow0 + 1024, wcol0 + ic * 128:wcol0 + (ic + 1) * 128]
                    src = src.rearrange("(k p) i -> p k i", p=128)
                    dst = wgu[:, slot * 2048 + o * 1024: slot * 2048 + (o + 1) * 1024]
                    dst = dst.rearrange("p (k i) -> p k i", k=8)
                    g.dma_start(out=dst, in_=src).then_inc(ct.sem(sem_nm), 16)
                    ct.bump(sem_nm, 16)
            dma_t[pi] = (sem_nm, ct.val(sem_nm))

        phase_gate = list(slab_gate)

        @blk.gpsimd
        def _(g):
            emit_wdma(g, 0, phase_gate)

        @blk.gpsimd
        def _(g):
            emit_wdma(g, 1, phase_gate)

        sil_mark = None
        act_mark = None
        for ic in range(n_ic):
            pi, o = ic // 2, ic % 2
            slot_g, slot_u = (pi % 2), 2 + (pi % 2)

            @blk.tensor
            def _(t, ic=ic, pi=pi, o=o, slot_g=slot_g, slot_u=slot_u,
                  fr_sil=sil_mark, fr_act=act_mark):
                t.wait_ge(ct.sem(dma_t[pi][0]), dma_t[pi][1])
                wait_marks(t, ct, [fr_sil, fr_act])
                for ps, slot in ((pA, slot_g), (pB, slot_u)):
                    for k in range(8):
                        for jt in range(4):
                            t.matmul(ps[:, jt * 512:(jt + 1) * 512],
                                     wgu[:, slot * 2048 + o * 1024 + k * 128:
                                         slot * 2048 + o * 1024 + k * 128 + 128],
                                     xb[:, k * 2048 + jt * 512: k * 2048 + (jt + 1) * 512],
                                     start=(k == 0), stop=(k == 7))
                t.drain().then_inc(ct.sem("qp"), 1)
            mm_mark = mk("qp")

            @blk.scalar
            def _(s, mm_mark=mm_mark):
                wait_marks(s, ct, [mm_mark])
                s.activation(out=sg[:], in_=pA[:], func=AF.Silu)
                s.drain().then_inc(ct.sem("qs"), 1)
            sil_mark = mk("qs")

            @blk.vector
            def _(v, ic=ic, sil_mark=sil_mark, scale=scale):
                wait_marks(v, ct, [sil_mark])
                aslot = act[:, ic * 2048:(ic + 1) * 2048]
                v.tensor_tensor(out=aslot, in0=sg[:], in1=pB[:], op=ALU.mult)
                if scale:
                    v.drain()
                    v.tensor_tensor(out=aslot, in0=aslot, in1=wbc[:], op=ALU.mult)
                v.drain().then_inc(ct.sem("qv"), 1)
            act_mark = mk("qv")
            ic_done[ic] = act_mark

            if pi + 2 < len(pairs) and (o == pairs[pi][1] - 1):
                @blk.gpsimd
                def _(g, pi=pi, gate=ic_done[ic]):
                    emit_wdma(g, pi + 2, [gate])
        slab_gate[0] = ic_done[n_ic - 1]
        return ic_done

    wd_gate = [wfm_mark]

    def emit_y_phase(n_ic, wd_name, wdrow0, act_ready):
        dma_t = {}
        ev = {}
        phase_gate = list(wd_gate)

        def emit_wd_dma(g, d, gates):
            sem_nm = "wda" if d % 2 == 0 else "wdb"
            wait_marks(g, ct, gates)
            src = io[wd_name][wdrow0:wdrow0 + n_ic * 128, d * 128:(d + 1) * 128]
            src = src.rearrange("(k p) i -> p k i", p=128)
            dst = wdb[:, (d % 2) * 1408: (d % 2) * 1408 + n_ic * 128]
            dst = dst.rearrange("p (k i) -> p k i", k=n_ic)
            g.dma_start(out=dst, in_=src).then_inc(ct.sem(sem_nm), 16)
            ct.bump(sem_nm, 16)
            dma_t[d] = (sem_nm, ct.val(sem_nm))

        @blk.gpsimd
        def _(g):
            emit_wd_dma(g, 0, phase_gate)

        @blk.gpsimd
        def _(g):
            emit_wd_dma(g, 1, phase_gate)

        for d in range(8):
            ps = pA if d % 2 == 0 else pB

            @blk.tensor
            def _(t, d=d, ps=ps, act_ready=act_ready, fr=ev.get(d - 2)):
                t.wait_ge(ct.sem(dma_t[d][0]), dma_t[d][1])
                wait_marks(t, ct, [act_ready, fr])
                for k in range(n_ic):
                    for jt in range(4):
                        t.matmul(ps[:, jt * 512:(jt + 1) * 512],
                                 wdb[:, (d % 2) * 1408 + k * 128:
                                     (d % 2) * 1408 + k * 128 + 128],
                                 act[:, k * 2048 + jt * 512: k * 2048 + (jt + 1) * 512],
                                 start=(k == 0), stop=(k == n_ic - 1))
                t.drain().then_inc(ct.sem("qp"), 1)
            y_mark = mk("qp")

            @blk.vector
            def _(v, d=d, ps=ps, y_mark=y_mark):
                wait_marks(v, ct, [y_mark])
                v.tensor_tensor(out=x_fm[:, d * 2048:(d + 1) * 2048],
                                in0=ps[:], in1=x_fm[:, d * 2048:(d + 1) * 2048],
                                op=ALU.add)
                v.drain().then_inc(ct.sem("qv"), 1)
            ev[d] = mk("qv")

            if d + 2 < 8:
                @blk.gpsimd
                def _(g, d=d, gate=ev[d]):
                    emit_wd_dma(g, d + 2, [gate])
        wd_gate[0] = ev[7]
        slab_gate[0] = ev[7]
        return ev[7]

    for grp in range(2):
        ic_done = emit_hg_hu_act(11, "f0_swg", "f0_swu", 0, False, wcol0=grp * 1408)
        emit_y_phase(11, "f0_swd", grp * 1408, ic_done[10])

    wbc_ev = None
    for e in range(8):
        @blk.tensor
        def _(t, e=e, gate=wd_gate[0], wfm_mark=wfm_mark, fr=wbc_ev):
            wait_marks(t, ct, [gate, wfm_mark, fr])
            for jt in range(4):
                t.matmul(pA[:, jt * 512:(jt + 1) * 512],
                         cons["sel8"][:, e * 128:(e + 1) * 128],
                         wfm[0:8, jt * 512:(jt + 1) * 512], start=True, stop=True)
            t.drain().then_inc(ct.sem("qp"), 1)
        wbc_mm = mk("qp")

        @blk.vector
        def _(v, wbc_mm=wbc_mm):
            wait_marks(v, ct, [wbc_mm])
            v.tensor_copy(wbc[:], pA[:])
            v.drain().then_inc(ct.sem("qv"), 1)
        wbc_ev = mk("qv")
        slab_gate[0] = wbc_ev

        ic_done = emit_hg_hu_act(11, "f0_wg", "f0_wu", e * 1024, True)
        wbc_ev = emit_y_phase(11, "f0_wd", e * 1408, ic_done[10])

    @blk.vector
    def _(v, gate=wd_gate[0]):
        wait_marks(v, ct, [gate])
        for c in range(8):
            v.tensor_copy(xb[:, c * 2048:(c + 1) * 2048], x_fm[:, c * 2048:(c + 1) * 2048])
        v.drain().then_inc(ct.sem("qv"), 1)
    return [mk("qv")], act


def emit_a1(ct, blk, io, cons, x2d, x3d, u_dram, xb, pA, pB, xb_marks, es,
            CONSTS_T, LDGC_T):
    mk = ct.mk
    QA = ct.sb("a1_QA", [70, 16384], BF16, es=es)
    KA = ct.sb("a1_KA", [70, 16384], BF16, es=es)
    V = ct.sb("a1_V", [128, 8192], BF16, es=es)
    attn = ct.sb("a1_attn", [128, 16384], BF16, es=es)
    xjt = ct.sb("a1_xjt", [128, 4096], F32, es=es)
    wstr = ct.sb("a1_wstr", [128, 4096], BF16, es=es)
    dump = ct.sb("a1_dump", [128, 2048], BF16, es=es)
    ptb = ct.sb("a1_ptb", [128, 3072], BF16, es=es)
    mcols = ct.sb("a1_mcols", [128, 32], F32, es=es)
    lcols = ct.sb("a1_lcols", [128, 32], F32, es=es)
    ccols = ct.sb("a1_ccols", [128, 16], F32, es=es)
    t16 = ct.sb("a1_t16", [128, 16], F32, es=es)
    chl = ct.sb("a1_chl", [128, 32], F32, es=es)
    chb = ct.sb("a1_chb", [128, 16], BF16, es=es)
    abst = ct.sb("a1_abst", [65, 4096], F32, es=es)
    bias8 = ct.sb("a1_bias8", [128, 16], F32, es=es)
    bvrow = ct.sb("a1_bvrow_sb", [1, 512], F32, es=es)
    pS = [pB[:, 0:512], pB[:, 512:1024]]
    pACC = pB[:, 1024:1536]
    pTf = pB[:, 1536:2048]
    biases = io["a1_biases_sb"]



    @blk.vector
    def _(v, ms=list(xb_marks)):
        v.wait_ge(ct.sem("lds"), CONSTS_T)
        wait_marks(v, ct, ms)
        v.tensor_scalar_mul(bias8[:], biases[:, 0:16], S8)
        v.drain().then_inc(ct.sem("qv"), 1)
    bias8_done = mk("qv")

    attn_done = []
    for g in range(2):
        gc = g * 512

        @blk.gpsimd
        def _(gp, g=g, pre=all_marks(ct, ("qg",))):
            wait_marks(gp, ct, pre)
            qa = io["c_qaug"].rearrange("(h r) n -> h r n", r=4)
            ka = io["c_kaug"].rearrange("(h r) n -> h r n", r=6)
            for hh in range(8):
                h = g * 8 + hh
                gp.dma_start(out=QA[66:70, hh * 2048:(hh + 1) * 2048],
                             in_=qa[h]).then_inc(ct.sem("aug"), 16)
                gp.dma_start(out=KA[64:70, hh * 2048:(hh + 1) * 2048],
                             in_=ka[h]).then_inc(ct.sem("aug"), 16)
                ct.bump("aug", 32)
            gp.memset(QA[64:66, :], 0.0)
            gp.drain().then_inc(ct.sem("qg"), 1)
        aug_dma = ct.val("aug")
        aug_mark = mk("qg")

        @blk.sync
        def _(sy, gc=gc, pre=all_marks(ct)):
            wait_marks(sy, ct, pre)
            sy.dma_start(out=bvrow[:], in_=io["a1_bvrow"][0:1, gc:gc + 512]
                         ).then_inc(ct.sem("lds"), 16)
            ct.bump("lds", 16)
        BV_T = ct.val("lds")

        ps_free = {0: None, 1: None}
        for wnm, dst, bcol0 in (("a1_wq", QA, 0), ("a1_wk", KA, 8)):
            @blk.gpsimd
            def _(gp, wnm=wnm, gc=gc, pre=all_marks(ct, ("qg",))):
                wait_marks(gp, ct, pre)
                src = io[wnm][:, gc:gc + 512].rearrange("(k p) i -> p k i", p=128)
                gp.dma_start(out=wstr[:].rearrange("p (k i) -> p k i", k=8),
                             in_=src).then_inc(ct.sem("wqs"), 16)
                ct.bump("wqs", 16)
            w_t = ct.val("wqs")

            pa_ev = ps_free.get("pa")
            for ic in range(4):
                @blk.tensor
                def _(t, ic=ic, w_t=w_t, fr=pa_ev):
                    t.wait_ge(ct.sem("wqs"), w_t)
                    wait_marks(t, ct, [fr])
                    for k in range(8):
                        for jt in range(4):
                            t.matmul(pA[:, jt * 512:(jt + 1) * 512],
                                     wstr[:, k * 512 + ic * 128: k * 512 + ic * 128 + 128],
                                     xb[:, k * 2048 + jt * 512: k * 2048 + (jt + 1) * 512],
                                     start=(k == 0), stop=(k == 7))
                    t.drain().then_inc(ct.sem("qp"), 1)
                mm_mark = mk("qp")

                @blk.vector
                def _(v, ic=ic, mm_mark=mm_mark, dst=dst,
                      bcol0=bcol0, g=g, bias8_done=bias8_done):
                    wait_marks(v, ct, [mm_mark, bias8_done])
                    bcol = bcol0 + g * 4 + ic
                    for jt in range(4):
                        for half in range(2):
                            hh = 2 * ic + half
                            v.tensor_scalar(
                                out=dst[0:64, hh * 2048 + jt * 512:
                                        hh * 2048 + (jt + 1) * 512],
                                in0=pA[half * 64:(half + 1) * 64, jt * 512:(jt + 1) * 512],
                                scalar1=S8,
                                scalar2=bias8[half * 64:(half + 1) * 64, bcol:bcol + 1],
                                op0=ALU.mult, op1=ALU.add)
                    v.drain().then_inc(ct.sem("qv"), 1)
                pa_ev = mk("qv")
            ps_free["pa"] = pa_ev

        @blk.gpsimd
        def _(gp, gc=gc, pre=all_marks(ct, ("qg",))):
            wait_marks(gp, ct, pre)
            src = io["a1_wv"][:, gc:gc + 512].rearrange("(k p) i -> p k i", p=128)
            gp.dma_start(out=wstr[:].rearrange("p (k i) -> p k i", k=8),
                         in_=src).then_inc(ct.sem("wqs"), 16)
            ct.bump("wqs", 16)
        wv_t = ct.val("wqs")

        for kt in range(16):
            sl = kt % 2

            @blk.tensor
            def _(t, kt=kt, sl=sl, wv_t=wv_t, gc=gc, fr=ps_free[sl], BV_T=BV_T):
                t.wait_ge(ct.sem("lds"), BV_T)
                t.wait_ge(ct.sem("wqs"), wv_t)
                wait_marks(t, ct, [fr])
                for k in range(8):
                    t.matmul(pS[sl][:],
                             xb[:, k * 2048 + kt * 128: k * 2048 + kt * 128 + 128],
                             wstr[:, k * 512:(k + 1) * 512],
                             start=(k == 0), stop=False)
                t.matmul(pS[sl][:], cons["ones_f"][0:1, :],
                         bvrow[0:1, 0:512], start=False, stop=True)
                t.drain().then_inc(ct.sem("qp"), 1)
            vm_mark = mk("qp")

            @blk.vector
            def _(v, kt=kt, sl=sl, vm_mark=vm_mark):
                wait_marks(v, ct, [vm_mark])
                v.tensor_copy(V[:, kt * 512:(kt + 1) * 512], pS[sl][:])
                v.drain().then_inc(ct.sem("qv"), 1)
            ps_free[sl] = mk("qv")

        pairs_done = attn_done[-1] if attn_done else None
        first_of_group = True
        pair_evacs = []
        for jp in range(4):
            hA, hB = g * 8 + 2 * jp, g * 8 + 2 * jp + 1
            heads = [(hA, (2 * jp) * 2048, pA, 0), (hB, (2 * jp + 1) * 2048, pB, 16)]


            mred = {hA: [], hB: []}
            exp_marks = {hA: None, hB: None}
            for qc in range(16):
                kend = (qc + 1) * 128
                @blk.tensor
                def _(t, qc=qc, kend=kend, hds=list(heads), fr=exp_marks[hA],
                      first=first_of_group, aug_dma=aug_dma, aug_mark=aug_mark,
                      pre=all_marks(ct, ("qp",)) if first_of_group else [],
                      pd=pairs_done):
                    t.wait_ge(ct.sem("aug"), aug_dma)
                    if first:
                        wait_marks(t, ct, pre)
                    else:
                        wait_marks(t, ct, [fr, aug_mark, pd])
                    for h_, hc, ps, off in hds:
                        j0 = 0
                        while j0 < kend:
                            jl = min(512, kend - j0)
                            t.matmul(ps[:, j0:j0 + jl],
                                     QA[0:70, hc + qc * 128: hc + (qc + 1) * 128],
                                     KA[0:70, hc + j0: hc + j0 + jl],
                                     start=True, stop=True)
                            j0 += jl
                    t.drain().then_inc(ct.sem("qp"), 1)
                smm_mark = mk("qp")
                first_of_group = False

                @blk.vector
                def _(v, qc=qc, kend=kend, hds=list(heads), smm_mark=smm_mark):
                    wait_marks(v, ct, [smm_mark])
                    for h_, hc, ps, off in hds:
                        v.tensor_tensor(out=ps[:, qc * 128:(qc + 1) * 128],
                                        in0=ps[:, qc * 128:(qc + 1) * 128],
                                        in1=cons["mtri"][:], op=ALU.add)
                    v.drain()
                    for h_, hc, ps, off in hds:
                        v.reduce_max(mcols[:, off + qc:off + qc + 1], ps[:, 0:kend],
                                     axis=AX.X, negate=True)
                    v.drain().then_inc(ct.sem("qv"), 1)
                mm_ = mk("qv")
                mred[hA].append(mm_)
                mred[hB].append(mm_)

                @blk.scalar
                def _(s, qc=qc, kend=kend, mm_=mm_, hds=list(heads)):
                    wait_marks(s, ct, [mm_])
                    for h_, hc_, ps_, off_ in hds:
                        s.activation(out=dump[:, 0:kend], in_=ps_[:, 0:kend],
                                     func=AF.Exp,
                                     bias=mcols[:, off_ + qc:off_ + qc + 1],
                                     accum_out=lcols[:, off_ + qc:off_ + qc + 1])
                        s.drain()
                    s.drain().then_inc(ct.sem("qs"), 1)
                em2 = mk("qs")
                exp_marks[hA] = em2
                exp_marks[hB] = em2

            mrow_marks = {}
            prev_tp = None
            prev_chl = None
            for h_, hc_, ps_, off_ in heads:
                @blk.scalar
                def _(s, off=off_, ex=exp_marks[h_], pc=prev_chl):
                    wait_marks(s, ct, [ex, pc])
                    s.activation(out=t16[:], in_=lcols[:, off:off + 16], func=AF.Ln)
                    s.drain().then_inc(ct.sem("qs"), 1)
                lnl_mark = mk("qs")

                @blk.vector
                def _(v, off=off_, lnl_mark=lnl_mark, mm=list(mred[h_]), ptp=prev_tp):
                    wait_marks(v, ct, [lnl_mark, ptp] + mm)
                    v.tensor_tensor(out=ccols[:], in0=mcols[:, off:off + 16], in1=t16[:],
                                    op=ALU.subtract)
                    v.drain()
                    v.tensor_copy(chb[:], ccols[:])
                    v.drain()
                    v.tensor_copy(chl[:, 0:32:2], chb[:])
                    v.drain()
                    v.tensor_tensor(out=t16[:], in0=ccols[:], in1=chl[:, 0:32:2],
                                    op=ALU.subtract)
                    v.drain()
                    v.tensor_copy(chb[:], t16[:])
                    v.drain()
                    v.tensor_copy(chl[:, 1:32:2], chb[:])
                    v.drain().then_inc(ct.sem("qv"), 1)
                chl_mark = mk("qv")
                prev_chl = chl_mark

                @blk.tensor
                def _(t, ps=ps_, chl_mark=chl_mark, fr=exp_marks[h_]):
                    wait_marks(t, ct, [chl_mark, fr])
                    for qc in range(16):
                        t.transpose(ps[0:2, qc * 128:(qc + 1) * 128],
                                    chl[:, 2 * qc:2 * qc + 2], cons["identf"][:])
                    t.drain().then_inc(ct.sem("qp"), 1)
                tp_mark = mk("qp")

                @blk.vector
                def _(v, hc=hc_, ps=ps_, tp_mark=tp_mark):
                    wait_marks(v, ct, [tp_mark])
                    v.tensor_copy(QA[64:66, hc:hc + 2048], ps[0:2, 0:2048])
                    v.drain().then_inc(ct.sem("qv"), 1)
                mrow_marks[h_] = mk("qv")
                prev_tp = ("qp", ct.val("qp"))

            def head_groups(h_, hc_, ps_, ptb0, mrowm):
                pacc = ps_[:, 1536:2048]
                slots = [ps_[:, 0:512], ps_[:, 512:1024], ps_[:, 1024:1536]]
                state = {"exp": mrowm, "pv": None, "expprev": None}
                out = []
                for qt in range(4):
                    nkc = 4 * (qt + 1)
                    kcs = list(range(nkc))
                    gsets = [kcs[i:i + 3] for i in range(0, nkc, 3)]
                    for gi, kset in enumerate(gsets):
                        out.append((qt, gi, kset, len(gsets), nkc))
                return out, pacc, slots, state

            emitted = {}
            for h_, hc_, ps_, off_ in heads:
                emitted[h_] = head_groups(h_, hc_, ps_,
                                          0 if h_ == hA else 1536, mrow_marks[h_])

            maxg = max(len(emitted[h_][0]) for h_, _, _, _ in heads)
            evac_last = {hA: None, hB: None}
            prev_exp_ab = None
            prev_pv_ab = None
            for step in range(maxg):
                work = []
                for h_, hc_, ps_, off_ in heads:
                    groups, pacc, slots, state = emitted[h_]
                    if step < len(groups):
                        work.append((h_, hc_, ps_, groups[step], pacc, slots,
                                     0 if h_ == hA else 1536))

                @blk.tensor
                def _(t, work=list(work), fr=prev_exp_ab,
                      ev=[evac_last[hA], evac_last[hB]],
                      mr=[mrow_marks[hA], mrow_marks[hB]]):
                    wait_marks(t, ct, [fr] + ev + mr)
                    for h_, hc, ps, (qt, gi, kset, ng, nkc), pacc, slots, ptb0 in work:
                        for i, kc in enumerate(kset):
                            kc_rel = kc - 4 * qt
                            j0 = max(0, kc_rel) * 128
                            t.matmul(slots[i][:, j0:512],
                                     KA[0:70, hc + kc * 128: hc + (kc + 1) * 128],
                                     QA[0:70, hc + qt * 512 + j0: hc + (qt + 1) * 512],
                                     start=True, stop=True)
                    t.drain().then_inc(ct.sem("qp"), 1)
                smm = mk("qp")

                ndiag = sum(1 for h_, hc, ps, (qt, gi, kset, ng, nkc), pacc, slots,
                            ptb0 in work for kc in kset if kc - 4 * qt >= 0)
                if ndiag:
                    @blk.vector
                    def _(v, work=list(work), smm=smm):
                        wait_marks(v, ct, [smm])
                        for h_, hc, ps, (qt, gi, kset, ng, nkc), pacc, slots, ptb0 in work:
                            for i, kc in enumerate(kset):
                                kc_rel = kc - 4 * qt
                                if kc_rel < 0:
                                    continue
                                j0 = kc_rel * 128
                                v.tensor_tensor(out=slots[i][:, j0:j0 + 128],
                                                in0=slots[i][:, j0:j0 + 128],
                                                in1=cons["mtriT"][:], op=ALU.add)
                                if j0 > 0:
                                    v.memset(slots[i][:, 0:j0], -1e30)
                        v.drain().then_inc(ct.sem("qv"), 1)
                    gate = mk("qv")
                else:
                    gate = smm

                @blk.scalar
                def _(s, work=list(work), gate=gate, pv2=prev_pv_ab):
                    wait_marks(s, ct, [gate, pv2])
                    for h_, hc, ps, (qt, gi, kset, ng, nkc), pacc, slots, ptb0 in work:
                        n = len(kset)
                        s.activation(out=ptb[:, ptb0:ptb0 + n * 512],
                                     in_=ps[:, 0:n * 512], func=AF.Exp)
                    s.drain().then_inc(ct.sem("qs"), 1)
                em = mk("qs")
                prev_exp_ab = em

                @blk.tensor
                def _(t, work=list(work), em=em):
                    wait_marks(t, ct, [em])
                    for h_, hc, ps, (qt, gi, kset, ng, nkc), pacc, slots, ptb0 in work:
                        hh2 = h_ % 8
                        for i, kc in enumerate(kset):
                            t.matmul(pacc[0:64, :],
                                     V[:, kc * 512 + hh2 * 64: kc * 512 + hh2 * 64 + 64],
                                     ptb[:, ptb0 + i * 512: ptb0 + (i + 1) * 512],
                                     start=(kc == 0), stop=(kc == nkc - 1))
                    t.drain().then_inc(ct.sem("qp"), 1)
                prev_pv_ab = mk("qp")

                evs = [(h_, qt, pacc) for h_, hc, ps, (qt, gi, kset, ng, nkc),
                       pacc, slots, ptb0 in work if kset[-1] == nkc - 1]
                if evs:
                    @blk.vector
                    def _(v, evs=list(evs), m=prev_pv_ab):
                        wait_marks(v, ct, [m])
                        for h2, qt, pacc in evs:
                            chunk, half = h2 // 2, h2 % 2
                            v.tensor_copy(attn[half * 64:half * 64 + 64,
                                               chunk * 2048 + qt * 512:
                                               chunk * 2048 + (qt + 1) * 512],
                                          pacc[0:64, :])
                        v.drain().then_inc(ct.sem("qv"), 1)
                    evm = mk("qv")
                    for h2, qt, pacc in evs:
                        evac_last[h2] = evm
            pairs_done = evac_last[hB]
            pair_evacs.extend([evac_last[hA], evac_last[hB]])
        pacc_free = pairs_done
        attn_done.append(pair_evacs[-1])

    wo_mm = {}
    stt_marks = []
    ust_t = ct.val("ust")
    for jt in range(4):
        @blk.sync
        def _(sy, jt=jt, ust_t=ust_t, pre=all_marks(ct) if jt == 0 else []):
            if jt >= 1:
                sy.wait_ge(ct.sem("ust"), ust_t)
            else:
                sy.wait_ge(ct.sem("spl"), ct.val("spl"))
                wait_marks(sy, ct, pre)
            for c in range(8):
                sy.dma_start(out=xjt[:, c * 512:(c + 1) * 512],
                             in_=x2d[c * 128:(c + 1) * 128, jt * 512:(jt + 1) * 512]
                             ).then_inc(ct.sem("uld"), 16)
                ct.bump("uld", 16)
        xjt_t = ct.val("uld")

        jt_stt = []
        for d in range(8):
            wsem = "woa" if d % 2 == 0 else "wob"

            @blk.gpsimd
            def _(gp, d=d, jt=jt, wsem=wsem,
                  prev=wo_mm.get((jt, d - 2)) or wo_mm.get((jt - 1, d + 6)),
                  att=list(attn_done)):
                wait_marks(gp, ct, [prev] + (att if (jt == 0 and d < 2) else []))
                src = io["a1_wo"][:, d * 128:(d + 1) * 128].rearrange("(k p) i -> p k i",
                                                                     p=128)
                gp.dma_start(out=wstr[:, (d % 2) * 1024:(d % 2) * 1024 + 1024]
                             .rearrange("p (k i) -> p k i", k=8),
                             in_=src).then_inc(ct.sem(wsem), 16)
                ct.bump(wsem, 16)
            wo_t = ct.val(wsem)

            @blk.tensor
            def _(t, d=d, jt=jt, wo_t=wo_t, wsem=wsem,
                  fr=(stt_marks[-2] if len(stt_marks) >= 2 else None),
                  att=list(attn_done)):
                t.wait_ge(ct.sem(wsem), wo_t)
                wait_marks(t, ct, [fr] + att)
                for k in range(8):
                    t.matmul(pS[d % 2][:],
                             wstr[:, (d % 2) * 1024 + k * 128:
                                  (d % 2) * 1024 + k * 128 + 128],
                             attn[:, k * 2048 + jt * 512: k * 2048 + (jt + 1) * 512],
                             start=(k == 0), stop=(k == 7))
                t.drain().then_inc(ct.sem("qp"), 1)
            wo_mm[(jt, d)] = mk("qp")

            @blk.vector
            def _(v, d=d, jt=jt, xjt_t=xjt_t, m=wo_mm[(jt, d)]):
                v.wait_ge(ct.sem("uld"), xjt_t)
                wait_marks(v, ct, [m])
                v.scalar_tensor_tensor(out=xjt[:, d * 512:(d + 1) * 512],
                                       in0=pS[d % 2][:],
                                       scalar=biases[:, 24 + d:24 + d + 1],
                                       in1=xjt[:, d * 512:(d + 1) * 512],
                                       op0=ALU.add, op1=ALU.add)
                v.drain()
                v.tensor_copy(xb[:, d * 2048 + jt * 512: d * 2048 + (jt + 1) * 512],
                              xjt[:, d * 512:(d + 1) * 512])
                v.drain().then_inc(ct.sem("qv"), 1)
            m2 = mk("qv")
            stt_marks.append(m2)
            jt_stt.append(m2)

        @blk.sync
        def _(sy, jt=jt, ms=list(jt_stt)):
            wait_marks(sy, ct, ms)
            for c in range(8):
                sy.dma_start(out=u_dram[c * 128:(c + 1) * 128, jt * 512:(jt + 1) * 512],
                             in_=xjt[:, c * 512:(c + 1) * 512]).then_inc(ct.sem("ust"), 16)
                ct.bump("ust", 16)
        ust_t = ct.val("ust")
    u_all = ct.val("ust")

    sq_free = None
    sums_marks = []
    for jt in range(4):
        @blk.scalar
        def _(s, jt=jt, ms=list(stt_marks), fr=sq_free):
            wait_marks(s, ct, ms + ([fr] if fr else []))
            for c in range(8):
                sq_out = (dump[:, c * 512:(c + 1) * 512] if c < 4
                          else ptb[:, (c - 4) * 512:(c - 3) * 512])
                s.activation(out=sq_out,
                             in_=xb[:, c * 2048 + jt * 512: c * 2048 + (jt + 1) * 512],
                             func=AF.Square)
            s.drain().then_inc(ct.sem("qs"), 1)
        sq_mark = mk("qs")

        @blk.tensor
        def _(t, jt=jt, sq_mark=sq_mark,
              fr=(stt_marks[-1] if jt == 0 else None)):
            wait_marks(t, ct, [sq_mark, fr])
            for c in range(8):
                t.matmul(pS[0][0:1, :], cons["ones_bf"][:, 0:1],
                         xb[:, c * 2048 + jt * 512: c * 2048 + (jt + 1) * 512],
                         start=(c == 0), stop=(c == 7))
            for c in range(8):
                sq_in = (dump[:, c * 512:(c + 1) * 512] if c < 4
                         else ptb[:, (c - 4) * 512:(c - 3) * 512])
                t.matmul(pS[1][0:1, :], cons["ones_bf"][:, 0:1], sq_in,
                         start=(c == 0), stop=(c == 7))
            t.drain().then_inc(ct.sem("qp"), 1)
        sums_mm = mk("qp")

        @blk.vector
        def _(v, jt=jt, sums_mm=sums_mm):
            wait_marks(v, ct, [sums_mm])
            v.tensor_copy(abst[0:1, jt * 512:(jt + 1) * 512], pS[0][0:1, :])
            v.drain()
            v.tensor_copy(abst[0:1, 2048 + jt * 512: 2048 + (jt + 1) * 512],
                          pS[1][0:1, :])
            v.drain().then_inc(ct.sem("qv"), 1)
        sq_free = mk("qv")
        sums_marks.append(sq_free)

    @blk.vector
    def _(v, ms=list(sums_marks)):
        wait_marks(v, ct, ms)
        v.tensor_scalar_mul(abst[64:65, 2048:4096], abst[0:1, 0:2048], -1.0 / 1024.0)
        v.tensor_scalar_mul(abst[64:65, 0:2048], abst[0:1, 2048:4096], 1.0 / 1024.0)
        v.drain()
        v.scalar_tensor_tensor(out=pA[0:1, 0:2048], in0=abst[64:65, 2048:4096],
                               scalar=-1.0, in1=abst[64:65, 2048:4096],
                               op0=ALU.mult, op1=ALU.mult)
        v.drain()
        v.tensor_tensor(out=abst[32:33, 0:2048], in0=abst[64:65, 0:2048],
                        in1=pA[0:1, 0:2048], op=ALU.add)
        v.drain()
        v.tensor_scalar_add(abst[32:33, 0:2048], abst[32:33, 0:2048], 1e-5)
        v.drain().then_inc(ct.sem("qv"), 1)
    var_mark = mk("qv")

    @blk.scalar
    def _(s, var_mark=var_mark):
        wait_marks(s, ct, [var_mark])
        s.activation(out=abst[64:65, 0:2048], in_=abst[32:33, 0:2048], func=AF.Sqrt)
        s.drain().then_inc(ct.sem("qs"), 1)
    sqrt_mark = mk("qs")

    @blk.vector
    def _(v, sqrt_mark=sqrt_mark):
        wait_marks(v, ct, [sqrt_mark])
        v.reciprocal(abst[64:65, 0:2048], abst[64:65, 0:2048])
        v.drain()
        v.tensor_tensor(out=abst[64:65, 2048:4096], in0=abst[64:65, 2048:4096],
                        in1=abst[64:65, 0:2048], op=ALU.mult)
        v.drain().then_inc(ct.sem("qv"), 1)
    ab_mark = mk("qv")

    ap_mark = None
    st_t = ct.val("x3s")
    for jt in range(4):
        @blk.tensor
        def _(t, jt=jt, ab_mark=ab_mark, fr=ap_mark,
              fr2=(sums_marks[-1] if jt == 0 else None)):
            wait_marks(t, ct, [ab_mark, fr, fr2])
            t.matmul(pS[0][:], cons["ones_f"][64:65, :],
                     abst[64:65, jt * 512:(jt + 1) * 512], start=True, stop=True)
            t.matmul(pS[1][:], cons["ones_f"][64:65, :],
                     abst[64:65, 2048 + jt * 512:2048 + (jt + 1) * 512],
                     start=True, stop=True)
            t.drain().then_inc(ct.sem("qp"), 1)
        bc_mark = mk("qp")

        @blk.sync
        def _(sy, jt=jt, st_t=st_t, u_all=u_all):
            sy.wait_ge(ct.sem("ust"), u_all)
            sy.wait_ge(ct.sem("x3s"), st_t)
            for c in range(8):
                sy.dma_start(out=xjt[:, c * 512:(c + 1) * 512],
                             in_=u_dram[c * 128:(c + 1) * 128, jt * 512:(jt + 1) * 512]
                             ).then_inc(ct.sem("uld"), 16)
                ct.bump("uld", 16)
        u_t = ct.val("uld")

        @blk.vector
        def _(v, jt=jt, bc_mark=bc_mark, u_t=u_t):
            wait_marks(v, ct, [bc_mark])
            v.wait_ge(ct.sem("uld"), u_t)
            for c in range(8):
                v.tensor_tensor(out=xjt[:, c * 512:(c + 1) * 512],
                                in0=xjt[:, c * 512:(c + 1) * 512],
                                in1=pS[0][:], op=ALU.mult)
            v.drain()
            for c in range(8):
                v.tensor_tensor(out=xjt[:, c * 512:(c + 1) * 512],
                                in0=xjt[:, c * 512:(c + 1) * 512],
                                in1=pS[1][:], op=ALU.add)
            v.drain().then_inc(ct.sem("qv"), 1)
        ap_mark = mk("qv")

        @blk.sync
        def _(sy, jt=jt, ap_mark=ap_mark):
            wait_marks(sy, ct, [ap_mark])
            for c in range(8):
                sy.dma_start(out=x3d[c * 128:(c + 1) * 128, jt * 512:(jt + 1) * 512],
                             in_=xjt[:, c * 512:(c + 1) * 512]).then_inc(ct.sem("x3s"), 16)
                ct.bump("x3s", 16)
        st_t = ct.val("x3s")
    return ct.val("x3s")


def emit_f1(ct, blk, io, cons, x3d, y_ap, u_dram, wpq_dram, xb, pA, pB, x3_all, es):
    mk = ct.mk
    wfc = ct.sb("f1_wfc", [128, 8 * 4096], BF16, es=es)
    hbuf = ct.sb("f1_hbuf", [128, 32 * 512], BF16, es=es)
    wpt = ct.sb("f1_wpt", [128, 2 * 4096], BF16, es=es)
    slab = ct.sb("f1_slab", [128, 2 * 1024], F32, es=es)
    qtmp = ct.sb("f1_qtmp", [128, 1024], BF16, es=es)
    sgn = ct.sb("f1_sgn", [128, 1024], BF16, es=es)
    absf = ct.sb("f1_absf", [128, 1024], F32, es=es)
    xjt = ct.sb("f1_xjt", [128, 4096], F32, es=es)
    tmp5 = ct.sb("f1_tmp5", [128, 512], F32, es=es)
    dump = ct.sb("f1_dump", [128, 1024], BF16, es=es)
    abst = ct.sb("f1_abst", [65, 4096], F32, es=es)
    gcol = ct.sb("f1_gcol", [128, 4], F32, es=es)
    g11 = ct.sb("f1_g11", [1, 4], F32, es=es)
    lnc = io["f1_lnc_sb"]
    wfcT, wprojT = io["f1_wfcT"], io["f1_wprojT"]

    cast_mark = None
    for p in range(4):
        @blk.sync
        def _(sy, p=p, cast_mark=cast_mark, pre=all_marks(ct) if p == 0 else []):
            sy.wait_ge(ct.sem("x3s"), x3_all)
            if cast_mark is not None:
                wait_marks(sy, ct, [cast_mark])
            else:
                wait_marks(sy, ct, pre)
            for cc in range(2):
                c = 2 * p + cc
                sy.dma_start(out=xjt[:, cc * 2048:(cc + 1) * 2048],
                             in_=x3d[c * 128:(c + 1) * 128, :]).then_inc(ct.sem("uld"), 16)
                ct.bump("uld", 16)
        lds_t = ct.val("uld")

        @blk.vector
        def _(v, p=p, lds_t=lds_t):
            v.wait_ge(ct.sem("uld"), lds_t)
            for cc in range(2):
                c = 2 * p + cc
                v.tensor_copy(xb[:, c * 2048:(c + 1) * 2048],
                              xjt[:, cc * 2048:(cc + 1) * 2048])
            v.drain().then_inc(ct.sem("qv"), 1)
        cast_mark = mk("qv")
    xb_done = cast_mark

    def slab_src(mat, i):
        if mat == "fc":
            r, cpc = i // 4, i % 4
            return wfcT[r * 128:(r + 1) * 128, cpc * 1024:(cpc + 1) * 1024]
        return wprojT[i * 128:(i + 1) * 128, :]

    @blk.vector
    def _(v):
        v.memset(gcol[:, 0:2], 0.0)
        v.drain().then_inc(ct.sem("qv"), 1)
    ct.bump("qv")

    gm_marks = []
    slab_carry = [None, None]
    for mat, col, sems in (("fc", 0, ("wfa", "wfb")), ("wp", 1, ("wpa", "wpb"))):
        sl_done = {-2: slab_carry[0], -1: slab_carry[1]}
        for i in range(32):
            sem_nm = sems[i % 2]

            @blk.gpsimd
            def _(g, mat=mat, i=i, sem_nm=sem_nm, prev=sl_done.get(i - 2),
                  pre=all_marks(ct, ("qg",)) if (mat == "fc" and i < 2) else []):
                wait_marks(g, ct, ([prev] if prev else []) + pre)
                g.dma_start(out=slab[:, (i % 2) * 1024:(i % 2) * 1024 + 1024],
                            in_=slab_src(mat, i)).then_inc(ct.sem(sem_nm), 16)
                ct.bump(sem_nm, 16)
            dma_t = ct.val(sem_nm)

            @blk.scalar
            def _(s, i=i, sem_nm=sem_nm, dma_t=dma_t, prev=sl_done.get(i - 1)):
                s.wait_ge(ct.sem(sem_nm), dma_t)
                wait_marks(s, ct, [prev] if prev else [])
                s.activation(out=dump[:, 0:1024],
                             in_=slab[:, (i % 2) * 1024:(i % 2) * 1024 + 1024],
                             func=AF.Abs, accum_out=gcol[:, 2:3])
                s.drain().then_inc(ct.sem("qs"), 1)
            ab_mark = mk("qs")

            @blk.vector
            def _(v, col=col, ab_mark=ab_mark):
                wait_marks(v, ct, [ab_mark])
                v.tensor_tensor(out=gcol[:, col:col + 1], in0=gcol[:, col:col + 1],
                                in1=gcol[:, 2:3], op=ALU.add)
                v.drain().then_inc(ct.sem("qv"), 1)
            sl_done[i] = mk("qv")
        gm_marks.append(sl_done[31])
        slab_carry = [sl_done[30], sl_done[31]]

    @blk.tensor
    def _(t, ms=list(gm_marks), pre=all_marks(ct, ("qp",))):
        wait_marks(t, ct, ms + pre)
        t.matmul(pA[0:1, 0:1], gcol[:, 0:1], cons["ones_cf"][:], start=True, stop=True)
        t.matmul(pA[0:1, 1:2], gcol[:, 1:2], cons["ones_cf"][:], start=True, stop=True)
        t.drain().then_inc(ct.sem("qp"), 1)
    gsum_mark = mk("qp")

    @blk.vector
    def _(v, gsum_mark=gsum_mark):
        wait_marks(v, ct, [gsum_mark])
        v.tensor_scalar(out=g11[0:1, 0:2], in0=pA[0:1, 0:2],
                        scalar1=1.0 / NW, scalar2=1e-6, op0=ALU.mult, op1=ALU.add)
        v.drain()
        v.tensor_scalar_mul(g11[0:1, 0:2], g11[0:1, 0:2], 0.5)
        v.drain().then_inc(ct.sem("qv"), 1)
    ghalf_mark = mk("qv")

    gbc_marks = []
    for c in range(2):
        @blk.tensor
        def _(t, c=c, ghalf_mark=ghalf_mark, fr=(gbc_marks[-1] if gbc_marks else None)):
            wait_marks(t, ct, [ghalf_mark, fr])
            t.matmul(pB[:, 0:1], cons["ones_f"][0:1, :], g11[0:1, c:c + 1],
                     start=True, stop=True)
            t.drain().then_inc(ct.sem("qp"), 1)
        m = mk("qp")

        @blk.vector
        def _(v, c=c, m=m):
            wait_marks(v, ct, [m])
            v.tensor_copy(gcol[:, 2 + c:3 + c], pB[:, 0:1])
            v.drain().then_inc(ct.sem("qv"), 1)
        gbc_marks.append(mk("qv"))
    gbc_mark = gbc_marks[-1]

    slab_carry2 = list(slab_carry)

    def emit_quant(mat, col, sems, out_cb):
        sl_done = {-2: slab_carry2[0], -1: slab_carry2[1]}
        for i in range(32):
            sem_nm = sems[i % 2]

            @blk.gpsimd
            def _(g, mat=mat, i=i, sem_nm=sem_nm, prev=sl_done.get(i - 2),
                  gbc_mark=gbc_mark):
                wait_marks(g, ct, ([prev] if prev else []) + [gbc_mark])
                g.dma_start(out=slab[:, (i % 2) * 1024:(i % 2) * 1024 + 1024],
                            in_=slab_src(mat, i)).then_inc(ct.sem(sem_nm), 16)
                ct.bump(sem_nm, 16)
            dma_t = ct.val(sem_nm)

            @blk.vector
            def _(v, i=i, col=col, sem_nm=sem_nm, dma_t=dma_t,
                  prevq=sl_done.get(i - 1)):
                v.wait_ge(ct.sem(sem_nm), dma_t)
                wait_marks(v, ct, [prevq] if prevq else [])
                sl_ = slab[:, (i % 2) * 1024:(i % 2) * 1024 + 1024]
                v.tensor_scalar(out=sgn[:], in0=sl_, scalar1=0.0, scalar2=None,
                                op0=ALU.is_gt)
                v.tensor_scalar(out=qtmp[:], in0=sl_, scalar1=0.0, scalar2=None,
                                op0=ALU.is_lt)
                v.drain()
                v.tensor_tensor(out=sgn[:], in0=sgn[:], in1=qtmp[:],
                                op=ALU.subtract)
                v.drain()
                v.tensor_tensor(out=absf[:], in0=sl_, in1=sgn[:], op=ALU.mult)
                v.drain()
                v.tensor_scalar(out=qtmp[:], in0=absf[:],
                                scalar1=gcol[:, 2 + col:3 + col], scalar2=None,
                                op0=ALU.is_gt)
                v.drain()
                v.tensor_tensor(out=qtmp[:], in0=qtmp[:], in1=sgn[:], op=ALU.mult)
                v.drain().then_inc(ct.sem("qv"), 1)
            q_mark = mk("qv")
            sl_done[i] = out_cb(i, q_mark)
        slab_carry2[0], slab_carry2[1] = sl_done[30], sl_done[31]

    def fc_out(i, q_mark):
        r, cpc = i // 4, i % 4

        @blk.vector
        def _(v, r=r, cpc=cpc):
            v.tensor_copy(wfc[:, r * 4096 + cpc * 1024: r * 4096 + (cpc + 1) * 1024],
                          qtmp[:])
            v.drain().then_inc(ct.sem("qv"), 1)
        return ("qv", ct.bump("qv"))

    def wp_out(i, q_mark):
        @blk.sync
        def _(sy, i=i, q_mark=q_mark):
            wait_marks(sy, ct, [q_mark])
            sy.dma_start(out=wpq_dram[i * 128:(i + 1) * 128, :],
                         in_=qtmp[:]).then_inc(ct.sem("wps"), 16)
        ct.bump("wps", 16)
        u_t = ct.val("wps")

        @blk.vector
        def _(v, u_t=u_t):
            v.wait_ge(ct.sem("wps"), u_t)
            v.drain().then_inc(ct.sem("qv"), 1)
        return ("qv", ct.bump("qv"))

    emit_quant("fc", 0, ("wfa", "wfb"), fc_out)
    emit_quant("wp", 1, ("wpa", "wpb"), wp_out)
    wpq_all = ct.val("wps")
    quant_done = ("qv", ct.val("qv"))

    ust_y = ct.val("ust")
    stt_marks = []
    wp_mm = {}
    for tg in range(4):
        @blk.sync
        def _(sy, tg=tg, fr=(stt_marks[-1] if stt_marks else xb_done), u_prev=ust_y):
            wait_marks(sy, ct, [fr])
            if tg >= 1:
                sy.wait_ge(ct.sem("ust"), u_prev)
            for c in range(8):
                sy.dma_start(out=xjt[:, c * 512:(c + 1) * 512],
                             in_=x3d[c * 128:(c + 1) * 128, tg * 512:(tg + 1) * 512]
                             ).then_inc(ct.sem("uld"), 16)
                ct.bump("uld", 16)
        xjt_t = ct.val("uld")

        ge_marks = {}
        prev_tg_mark = stt_marks[-1] if stt_marks else None
        for hc in range(32):
            q = hc % 4

            @blk.tensor
            def _(t, tg=tg, hc=hc, q=q,
                  fr=ge_marks.get(hc - 4) or prev_tg_mark,
                  qd=quant_done, xb_done=xb_done):
                wait_marks(t, ct, [fr, qd, xb_done])
                for k in range(8):
                    t.matmul(pA[:, q * 512:(q + 1) * 512],
                             wfc[:, k * 4096 + hc * 128: k * 4096 + hc * 128 + 128],
                             xb[:, k * 2048 + tg * 512: k * 2048 + (tg + 1) * 512],
                             start=(k == 0), stop=(k == 7))
                t.drain().then_inc(ct.sem("qp"), 1)
            mmk = mk("qp")

            @blk.scalar
            def _(s, hc=hc, q=q, mmk=mmk):
                wait_marks(s, ct, [mmk])
                s.activation(out=hbuf[:, hc * 512:(hc + 1) * 512],
                             in_=pA[:, q * 512:(q + 1) * 512], func=AF.Gelu)
                s.drain().then_inc(ct.sem("qs"), 1)
            ge_marks[hc] = mk("qs")

        @blk.tensor
        def _(t, tg=tg, ms=[ge_marks[31]]):
            t.wait_ge(ct.sem("ldg"), 80)
            wait_marks(t, ct, ms)
            for hc in range(32):
                t.matmul(pB[0:1, 0:512], cons["ones_bf"][:, 0:1],
                         hbuf[:, hc * 512:(hc + 1) * 512],
                         start=(hc == 0), stop=(hc == 31))
            t.drain().then_inc(ct.sem("qp"), 1)
        sum_mark = mk("qp")

        sq_marks = {}
        for hc in range(32):
            @blk.vector
            def _(v, hc=hc, fr=sq_marks.get(hc - 2), ms=[ge_marks[31]]):
                wait_marks(v, ct, [fr] if fr else ms)
                v.tensor_tensor(out=dump[:, (hc % 2) * 512:(hc % 2) * 512 + 512],
                                in0=hbuf[:, hc * 512:(hc + 1) * 512],
                                in1=hbuf[:, hc * 512:(hc + 1) * 512], op=ALU.mult)
                v.drain().then_inc(ct.sem("qv"), 1)
            sqm = mk("qv")

            @blk.tensor
            def _(t, hc=hc, sqm=sqm):
                wait_marks(t, ct, [sqm])
                t.matmul(pB[0:1, 512:1024], cons["ones_bf"][:, 0:1],
                         dump[:, (hc % 2) * 512:(hc % 2) * 512 + 512],
                         start=(hc == 0), stop=(hc == 31))
                t.drain().then_inc(ct.sem("qp"), 1)
            sq_marks[hc] = mk("qp")
        sumsq_mark = sq_marks[31]

        @blk.vector
        def _(v, sum_mark=sum_mark, sumsq_mark=sumsq_mark):
            wait_marks(v, ct, [sum_mark, sumsq_mark])
            v.tensor_scalar_mul(abst[64:65, 512:1024], pB[0:1, 0:512], -1.0 / 4096.0)
            v.tensor_scalar_mul(abst[64:65, 0:512], pB[0:1, 512:1024], 1.0 / 4096.0)
            v.drain()
            v.scalar_tensor_tensor(out=pA[0:1, 0:512], in0=abst[64:65, 512:1024],
                                   scalar=-1.0, in1=abst[64:65, 512:1024],
                                   op0=ALU.mult, op1=ALU.mult)
            v.drain()
            v.tensor_tensor(out=abst[32:33, 0:512], in0=abst[64:65, 0:512],
                            in1=pA[0:1, 0:512], op=ALU.add)
            v.drain()
            v.tensor_scalar_add(abst[32:33, 0:512], abst[32:33, 0:512], 1e-5)
            v.drain().then_inc(ct.sem("qv"), 1)
        var_mark = mk("qv")

        @blk.scalar
        def _(s, var_mark=var_mark):
            wait_marks(s, ct, [var_mark])
            s.activation(out=abst[64:65, 0:512], in_=abst[32:33, 0:512], func=AF.Sqrt)
            s.drain().then_inc(ct.sem("qs"), 1)
        sqrt_mark = mk("qs")

        @blk.vector
        def _(v, sqrt_mark=sqrt_mark):
            wait_marks(v, ct, [sqrt_mark])
            v.reciprocal(abst[64:65, 0:512], abst[64:65, 0:512])
            v.drain()
            v.tensor_tensor(out=abst[64:65, 512:1024], in0=abst[64:65, 512:1024],
                            in1=abst[64:65, 0:512], op=ALU.mult)
            v.drain().then_inc(ct.sem("qv"), 1)
        ab_mark = mk("qv")

        @blk.tensor
        def _(t, ab_mark=ab_mark):
            wait_marks(t, ct, [ab_mark])
            t.matmul(pA[:, 0:512], cons["ones_f"][64:65, :],
                     abst[64:65, 0:512], start=True, stop=True)
            t.matmul(pA[:, 512:1024], cons["ones_f"][64:65, :],
                     abst[64:65, 512:1024], start=True, stop=True)
            t.drain().then_inc(ct.sem("qp"), 1)
        bc_mark = mk("qp")

        apl_mark = None
        for hc in range(32):
            @blk.vector
            def _(v, hc=hc, bc_mark=bc_mark):
                wait_marks(v, ct, [bc_mark])
                v.tensor_tensor(out=tmp5[:], in0=hbuf[:, hc * 512:(hc + 1) * 512],
                                in1=pA[:, 0:512], op=ALU.mult)
                v.drain()
                v.tensor_tensor(out=tmp5[:], in0=tmp5[:], in1=pA[:, 512:1024],
                                op=ALU.add)
                v.drain()
                v.tensor_scalar(out=hbuf[:, hc * 512:(hc + 1) * 512], in0=tmp5[:],
                                scalar1=lnc[:, hc:hc + 1],
                                scalar2=lnc[:, 32 + hc:33 + hc],
                                op0=ALU.mult, op1=ALU.add)
                v.drain().then_inc(ct.sem("qv"), 1)
            apl_mark = mk("qv")

        jt_stt = []
        for d in range(8):
            sem_nm = "wta" if d % 2 == 0 else "wtb"

            @blk.sync
            def _(sy, d=d, tg=tg, sem_nm=sem_nm,
                  prev=wp_mm.get((tg, d - 2)) or wp_mm.get((tg - 1, d + 6)),
                  wpq_all=wpq_all):
                sy.wait_ge(ct.sem("wps"), wpq_all)
                wait_marks(sy, ct, [prev] if prev else [])
                src = wpq_dram[:, d * 128:(d + 1) * 128].rearrange("(k p) i -> p k i",
                                                                  p=128)
                sy.dma_start(out=wpt[:, (d % 2) * 4096:(d % 2) * 4096 + 4096]
                             .rearrange("p (k i) -> p k i", k=32),
                             in_=src).then_inc(ct.sem(sem_nm), 16)
                ct.bump(sem_nm, 16)
            wp_t = ct.val(sem_nm)

            @blk.tensor
            def _(t, d=d, tg=tg, wp_t=wp_t, sem_nm=sem_nm, apl_mark=apl_mark,
                  fr=(stt_marks[-2] if len(stt_marks) >= 2 else None)):
                t.wait_ge(ct.sem(sem_nm), wp_t)
                wait_marks(t, ct, [apl_mark, fr])
                for k in range(32):
                    t.matmul(pB[:, (d % 2) * 512:(d % 2) * 512 + 512],
                             wpt[:, (d % 2) * 4096 + k * 128:
                                 (d % 2) * 4096 + k * 128 + 128],
                             hbuf[:, k * 512:(k + 1) * 512],
                             start=(k == 0), stop=(k == 31))
                t.drain().then_inc(ct.sem("qp"), 1)
            wp_mm[(tg, d)] = mk("qp")

            @blk.vector
            def _(v, d=d, tg=tg, m=wp_mm[(tg, d)], xjt_t=xjt_t):
                v.wait_ge(ct.sem("uld"), xjt_t)
                wait_marks(v, ct, [m])
                v.tensor_tensor(out=xjt[:, d * 512:(d + 1) * 512],
                                in0=pB[:, (d % 2) * 512:(d % 2) * 512 + 512],
                                in1=xjt[:, d * 512:(d + 1) * 512], op=ALU.add)
                v.drain()
                v.tensor_copy(xb[:, d * 2048 + tg * 512: d * 2048 + (tg + 1) * 512],
                              xjt[:, d * 512:(d + 1) * 512])
                v.drain().then_inc(ct.sem("qv"), 1)
            m2 = mk("qv")
            stt_marks.append(m2)
            jt_stt.append(m2)

        @blk.sync
        def _(sy, tg=tg, ms=list(jt_stt)):
            wait_marks(sy, ct, ms)
            for c in range(8):
                sy.dma_start(out=u_dram[c * 128:(c + 1) * 128, tg * 512:(tg + 1) * 512],
                             in_=xjt[:, c * 512:(c + 1) * 512]).then_inc(ct.sem("ust"), 16)
                ct.bump("ust", 16)
        ust_y = ct.val("ust")
    u_all = ct.val("ust")

    sq_free = None
    sums_marks = []
    for jt in range(4):
        @blk.scalar
        def _(s, jt=jt, ms=list(stt_marks), fr=sq_free):
            wait_marks(s, ct, ms + ([fr] if fr else []))
            for c in range(8):
                s.activation(out=hbuf[:, c * 512:(c + 1) * 512],
                             in_=xb[:, c * 2048 + jt * 512: c * 2048 + (jt + 1) * 512],
                             func=AF.Square)
            s.drain().then_inc(ct.sem("qs"), 1)
        sq_mark = mk("qs")

        @blk.tensor
        def _(t, jt=jt, sq_mark=sq_mark,
              fr=(stt_marks[-1] if jt == 0 else None)):
            wait_marks(t, ct, [sq_mark, fr])
            for c in range(8):
                t.matmul(pA[0:1, 0:512], cons["ones_bf"][:, 0:1],
                         xb[:, c * 2048 + jt * 512: c * 2048 + (jt + 1) * 512],
                         start=(c == 0), stop=(c == 7))
            for c in range(8):
                t.matmul(pB[0:1, 0:512], cons["ones_bf"][:, 0:1],
                         hbuf[:, c * 512:(c + 1) * 512],
                         start=(c == 0), stop=(c == 7))
            t.drain().then_inc(ct.sem("qp"), 1)
        sums_mm = mk("qp")

        @blk.vector
        def _(v, jt=jt, sums_mm=sums_mm):
            wait_marks(v, ct, [sums_mm])
            v.tensor_copy(abst[0:1, jt * 512:(jt + 1) * 512], pA[0:1, 0:512])
            v.drain()
            v.tensor_copy(abst[0:1, 2048 + jt * 512: 2048 + (jt + 1) * 512],
                          pB[0:1, 0:512])
            v.drain().then_inc(ct.sem("qv"), 1)
        sq_free = mk("qv")
        sums_marks.append(sq_free)

    @blk.vector
    def _(v, ms=list(sums_marks)):
        wait_marks(v, ct, ms)
        v.tensor_scalar_mul(abst[64:65, 2048:4096], abst[0:1, 0:2048], -1.0 / 1024.0)
        v.tensor_scalar_mul(abst[64:65, 0:2048], abst[0:1, 2048:4096], 1.0 / 1024.0)
        v.drain()
        v.scalar_tensor_tensor(out=pA[0:1, 0:2048], in0=abst[64:65, 2048:4096],
                               scalar=-1.0, in1=abst[64:65, 2048:4096],
                               op0=ALU.mult, op1=ALU.mult)
        v.drain()
        v.tensor_tensor(out=abst[32:33, 0:2048], in0=abst[64:65, 0:2048],
                        in1=pA[0:1, 0:2048], op=ALU.add)
        v.drain()
        v.tensor_scalar_add(abst[32:33, 0:2048], abst[32:33, 0:2048], 1e-5)
        v.drain().then_inc(ct.sem("qv"), 1)
    var_mark = mk("qv")

    @blk.scalar
    def _(s, var_mark=var_mark):
        wait_marks(s, ct, [var_mark])
        s.activation(out=abst[64:65, 0:2048], in_=abst[32:33, 0:2048], func=AF.Sqrt)
        s.drain().then_inc(ct.sem("qs"), 1)
    sqrt_mark = mk("qs")

    @blk.vector
    def _(v, sqrt_mark=sqrt_mark):
        wait_marks(v, ct, [sqrt_mark])
        v.reciprocal(abst[64:65, 0:2048], abst[64:65, 0:2048])
        v.drain()
        v.tensor_tensor(out=abst[64:65, 2048:4096], in0=abst[64:65, 2048:4096],
                        in1=abst[64:65, 0:2048], op=ALU.mult)
        v.drain().then_inc(ct.sem("qv"), 1)
    ab_mark2 = mk("qv")

    ap_mark = None
    st_t = ct.val("yst")
    for jt in range(4):
        @blk.tensor
        def _(t, jt=jt, ab_mark2=ab_mark2, fr=ap_mark,
              fr2=(sums_marks[-1] if jt == 0 else None)):
            wait_marks(t, ct, [ab_mark2, fr, fr2])
            t.matmul(pA[:, 0:512], cons["ones_f"][64:65, :],
                     abst[64:65, jt * 512:(jt + 1) * 512], start=True, stop=True)
            t.matmul(pB[:, 0:512], cons["ones_f"][64:65, :],
                     abst[64:65, 2048 + jt * 512:2048 + (jt + 1) * 512],
                     start=True, stop=True)
            t.drain().then_inc(ct.sem("qp"), 1)
        bc2 = mk("qp")

        @blk.sync
        def _(sy, jt=jt, st_t=st_t, u_all=u_all):
            sy.wait_ge(ct.sem("ust"), u_all)
            sy.wait_ge(ct.sem("yst"), st_t)
            for c in range(8):
                sy.dma_start(out=xjt[:, c * 512:(c + 1) * 512],
                             in_=u_dram[c * 128:(c + 1) * 128, jt * 512:(jt + 1) * 512]
                             ).then_inc(ct.sem("uld"), 16)
                ct.bump("uld", 16)
        u_t = ct.val("uld")

        @blk.vector
        def _(v, jt=jt, bc2=bc2, u_t=u_t):
            wait_marks(v, ct, [bc2])
            v.wait_ge(ct.sem("uld"), u_t)
            for c in range(8):
                v.tensor_tensor(out=xjt[:, c * 512:(c + 1) * 512],
                                in0=xjt[:, c * 512:(c + 1) * 512],
                                in1=pA[:, 0:512], op=ALU.mult)
            v.drain()
            for c in range(8):
                v.tensor_tensor(out=xjt[:, c * 512:(c + 1) * 512],
                                in0=xjt[:, c * 512:(c + 1) * 512],
                                in1=pB[:, 0:512], op=ALU.add)
            v.drain().then_inc(ct.sem("qv"), 1)
        ap_mark = mk("qv")

        @blk.sync
        def _(sy, jt=jt, ap_mark=ap_mark):
            wait_marks(sy, ct, [ap_mark])
            for c in range(8):
                sy.dma_start(out=y_ap[c * 128:(c + 1) * 128, jt * 512:(jt + 1) * 512],
                             in_=xjt[:, c * 512:(c + 1) * 512]).then_inc(ct.sem("yst"), 16)
                ct.bump("yst", 16)
        st_t = ct.val("yst")

    @blk.vector
    def _(v):
        v.wait_ge(ct.sem("yst"), ct.val("yst"))
        v.drain().then_inc(ct.sem("qv"), 1)
    ct.bump("qv")


def build_fused():
    nc = bass.Bass("TRN2", target_bir_lowering=False)
    dr = lambda nm, shp, dt=F32, kind="ExternalInput": nc.dram_tensor(nm, shp, dt, kind=kind).ap()
    x_ap = dr("x", [2048, 1024])
    io = {}
    for nm, shp in (("a0_wq", [1024, 1024]), ("a0_wk", [1024, 1024]),
                    ("a0_wv", [1024, 1024]), ("a0_wo", [1024, 1024]),
                    ("a0_biases", [128, 32]),
                    ("f0_gate_wT", [1024, 8]), ("f0_wg", [8192, 1408]),
                    ("f0_wu", [8192, 1408]), ("f0_wd", [11264, 1024]),
                    ("f0_swg", [1024, 2816]), ("f0_swu", [1024, 2816]),
                    ("f0_swd", [2816, 1024]),
                    ("a1_wq", [1024, 1024]), ("a1_wk", [1024, 1024]),
                    ("a1_wv", [1024, 1024]), ("a1_wo", [1024, 1024]),
                    ("a1_biases", [128, 32]), ("a1_bvrow", [1, 1024]),
                    ("f1_wfcT", [1024, 4096]), ("f1_wprojT", [4096, 1024]),
                    ("f1_lncols", [128, 64])):
        io[nm] = dr(nm, shp)
    cs = make_consts()
    for k, v in cs.items():
        io["c_" + k] = dr("c_" + k, list(v.shape))
    y_ap = dr("y_fm", [1024, 2048], F32, "ExternalOutput")
    x2d = dr("x2_scr", [1024, 2048], F32, "ExternalOutput")
    x3d = dr("x3_scr", [1024, 2048], F32, "ExternalOutput")
    u_dram = dr("u_scr", [1024, 2048], F32, "ExternalOutput")
    wpq_dram = dr("wpq_scr", [4096, 1024], BF16, "ExternalOutput")

    ct = Ctx(nc)
    with ct.es:
        xb = ct.sb("xb", [128, 16384], BF16)
        cons = dict(
            identf=ct.sb("s_identf", [128, 128], F32),
            identb=ct.sb("s_identb", [128, 128], BF16),
            ones_f=ct.sb("s_ones_f", [65, 128], F32),
            ones_bf=ct.sb("s_ones_bf", [128, 1], BF16),
            ones_cf=ct.sb("s_ones_cf", [128, 1], F32),
            ind_sc=ct.sb("s_ind_sc", [128, 8], BF16),
            ind_bc=ct.sb("s_ind_bc", [4, 256], BF16),
            sel8=ct.sb("s_sel8", [8, 1024], F32),
            mtri=ct.sb("s_mtri", [128, 128], F32),
            mtriT=ct.sb("s_mtriT", [128, 128], F32),
        )
        a0_bias = ct.sb("a0_bias_sb", [128, 32], F32)
        a1_bias = ct.sb("a1_bias_sb", [128, 32], F32)
        f1_lnc = ct.sb("f1_lnc_sb", [128, 64], F32)
        f0_gwf = ct.sb("f0_gwf_sb", [128, 64], F32)
        io["a1_biases_sb"] = a1_bias
        io["f1_lnc_sb"] = f1_lnc
        pA = ct.ps("pA", [128, 2048], F32)
        pB = ct.ps("pB", [128, 2048], F32)
        bank = [pA[:, i * 512:(i + 1) * 512] for i in range(4)] + \
               [pB[:, i * 512:(i + 1) * 512] for i in range(4)]
        blk = ct.es.enter_context(nc.Block())
        for s in ("lds", "ldg", "xld", "wqa", "wqb", "wda", "wdb", "aug",
                  "wqs", "woa", "wob", "spl", "x3s", "uld", "ust", "yst",
                  "wfa", "wfb", "wpa", "wpb", "wta", "wtb", "wps",
                  "qp", "qv", "qs", "qg"):
            ct.sem(s)

        @blk.sync
        def _(sy):
            for t_sb, src in ((cons["ones_f"], io["c_ones_f"]),
                              (cons["ones_cf"], io["c_ones_cf"]),
                              (cons["identf"], io["c_identf"]),
                              (cons["mtri"], io["c_mtri"]),
                              (cons["mtriT"], io["c_mtriT"]),
                              (cons["sel8"], io["c_sel8"]),
                              (a0_bias, io["a0_biases"]),
                              (a1_bias, io["a1_biases"]),
                              (f1_lnc, io["f1_lncols"])):
                sy.dma_start(out=t_sb[:], in_=src).then_inc(ct.sem("lds"), 16)
                ct.bump("lds", 16)

        @blk.gpsimd
        def _(g):
            for t_sb, src in ((cons["identb"], io["c_identf"]),
                              (cons["ones_bf"], io["c_ones_bf"]),
                              (cons["ind_sc"], io["c_ind_sc"]),
                              (cons["ind_bc"], io["c_ind_bc"])):
                g.dma_start(out=t_sb[:], in_=src).then_inc(ct.sem("ldg"), 16)
                ct.bump("ldg", 16)
            src = io["f0_gate_wT"].rearrange("(k p) e -> p k e", p=128)
            g.dma_start(out=f0_gwf[:].rearrange("p (k e) -> p k e", k=8),
                        in_=src).then_inc(ct.sem("ldg"), 16)
            ct.bump("ldg", 16)
        CONSTS_T = ct.val("lds")
        LDGC_T = ct.val("ldg")

        es_A = contextlib.ExitStack()
        with es_A:
            x_fm = ct.sb("x_fm", [128, 16384], F32, es=es_A)
            abst = ct.sb("ab_abst", [65, 4096], F32, es=es_A)
            es_a0 = contextlib.ExitStack()
            with es_a0:
                sqt0 = ct.sb("a0_sqt", [128, 4096], BF16, es=es_a0)
                m1 = emit_a0(ct, blk, x_ap, io, cons, x_fm, xb, bank, es_a0,
                             a0_bias, CONSTS_T, LDGC_T)
                ap0 = emit_postln(ct, blk, x_fm, xb, cons, pA, pB, sqt0, abst, m1)
                @blk.vector
                def _(v, ap0=ap0):
                    wait_marks(v, ct, [ap0])
                    for c in range(8):
                        v.tensor_copy(xb[:, c * 2048:(c + 1) * 2048],
                                      x_fm[:, c * 2048:(c + 1) * 2048])
                    v.drain().then_inc(ct.sem("qv"), 1)
                m2 = [ct.mk("qv")]
            emit_barrier_dma(ct, blk, [])
            m3, act_f0 = emit_f0(ct, blk, io, cons, x_fm, xb, pA, pB, m2, es_A,
                                 f0_gwf, CONSTS_T, LDGC_T)
            ap1 = emit_postln(ct, blk, x_fm, xb, cons, pA, pB,
                              act_f0[:, 0:4096], abst, m3)

            @blk.vector
            def _(v, ap1=ap1):
                wait_marks(v, ct, [ap1])
                for c in range(8):
                    v.tensor_copy(xb[:, c * 2048:(c + 1) * 2048],
                                  x_fm[:, c * 2048:(c + 1) * 2048])
                v.drain().then_inc(ct.sem("qv"), 1)
            m4 = [ct.mk("qv")]

            @blk.sync
            def _(sy, ap1=ap1):
                wait_marks(sy, ct, [ap1])
                for c in range(8):
                    sy.dma_start(out=x2d[c * 128:(c + 1) * 128, :],
                                 in_=x_fm[:, c * 2048:(c + 1) * 2048]
                                 ).then_inc(ct.sem("spl"), 16)
                    ct.bump("spl", 16)

        emit_barrier_dma(ct, blk, [("spl", ct.val("spl"))])

        es_B = contextlib.ExitStack()
        with es_B:
            x3_all = emit_a1(ct, blk, io, cons, x2d, x3d, u_dram, xb, pA, pB,
                             m4, es_B, CONSTS_T, LDGC_T)

        emit_barrier_dma(ct, blk, [("x3s", ct.val("x3s")), ("ust", ct.val("ust"))])

        es_C = contextlib.ExitStack()
        with es_C:
            emit_f1(ct, blk, io, cons, x3d, y_ap, u_dram, wpq_dram, xb, pA, pB,
                    ct.val("x3s"), es_C)
    return nc


def emit_barrier_dma(ct, blk, dma_marks):
    marks = {nm: ct.val(nm) for nm in ENG_SEMS}

    def waits(eng, own):
        for nm, v in marks.items():
            if nm != own and v > 0:
                eng.wait_ge(ct.sem(nm), v)
        for nm, v in dma_marks:
            if v > 0:
                eng.wait_ge(ct.sem(nm), v)

    @blk.tensor
    def _(t):
        waits(t, "qp")
        t.drain().then_inc(ct.sem("qp"), 1)
    ct.bump("qp")

    @blk.vector
    def _(v):
        waits(v, "qv")
        v.drain().then_inc(ct.sem("qv"), 1)
    ct.bump("qv")

    @blk.scalar
    def _(s):
        waits(s, "qs")
        s.drain().then_inc(ct.sem("qs"), 1)
    ct.bump("qs")

    @blk.gpsimd
    def _(g):
        waits(g, "qg")
        g.drain().then_inc(ct.sem("qg"), 1)
    ct.bump("qg")


def host_inputs(inp):
    cs = make_consts()
    perm = np.concatenate([np.arange(64) + h * 64 for r in range(4)
                           for h in (r, r + 4, r + 8, r + 12)])
    a0_biases = np.zeros((128, 32), np.float32)
    for j, nm in enumerate(("bq", "bk", "bv")):
        a0_biases[:, j * 8:(j + 1) * 8] = np.asarray(inp["L0_attn_" + nm])[perm].reshape(8, 128).T
    a0_biases[:, 24:32] = np.asarray(inp["L0_attn_bo"]).reshape(8, 128).T
    a1_biases = np.zeros((128, 32), np.float32)
    a1_biases[:, 0:8] = np.asarray(inp["L1_attn_bq"]).reshape(8, 128).T
    a1_biases[:, 8:16] = np.asarray(inp["L1_attn_bk"]).reshape(8, 128).T
    a1_biases[:, 24:32] = np.asarray(inp["L1_attn_bo"]).reshape(8, 128).T
    lncols = np.zeros((128, 64), np.float32)
    lncols[:, 0:32] = np.asarray(inp["L1_ffn_ln_w"]).reshape(32, 128).T
    lncols[:, 32:64] = np.asarray(inp["L1_ffn_ln_b"]).reshape(32, 128).T
    base = dict(
        a0_wq=np.ascontiguousarray(np.asarray(inp["L0_attn_Wq"])[:, perm]),
        a0_wk=np.ascontiguousarray(np.asarray(inp["L0_attn_Wk"])[:, perm]),
        a0_wv=np.ascontiguousarray(np.asarray(inp["L0_attn_Wv"])[:, perm]),
        a0_wo=np.ascontiguousarray(np.asarray(inp["L0_attn_Wo"])[perm, :]),
        a0_biases=a0_biases,
        f0_gate_wT=np.ascontiguousarray(np.asarray(inp["L0_ffn_gate_w"]).T),
        f0_wg=np.ascontiguousarray(np.asarray(inp["L0_ffn_wg"]).reshape(8192, 1408)),
        f0_wu=np.ascontiguousarray(np.asarray(inp["L0_ffn_wu"]).reshape(8192, 1408)),
        f0_wd=np.ascontiguousarray(np.asarray(inp["L0_ffn_wd"]).reshape(11264, 1024)),
        f0_swg=np.ascontiguousarray(np.asarray(inp["L0_ffn_swg"])),
        f0_swu=np.ascontiguousarray(np.asarray(inp["L0_ffn_swu"])),
        f0_swd=np.ascontiguousarray(np.asarray(inp["L0_ffn_swd"])),
        a1_wq=np.ascontiguousarray(np.asarray(inp["L1_attn_Wq"])),
        a1_wk=np.ascontiguousarray(np.asarray(inp["L1_attn_Wk"])),
        a1_wv=np.ascontiguousarray(np.asarray(inp["L1_attn_Wv"])),
        a1_wo=np.ascontiguousarray(np.asarray(inp["L1_attn_Wo"])),
        a1_biases=a1_biases,
        a1_bvrow=np.ascontiguousarray(np.asarray(inp["L1_attn_bv"]).reshape(1, 1024)),
        f1_wfcT=np.ascontiguousarray(np.asarray(inp["L1_ffn_W_fc"]).T),
        f1_wprojT=np.ascontiguousarray(np.asarray(inp["L1_ffn_W_proj"]).T),
        f1_lncols=lncols,
        **{"c_" + k: np.ascontiguousarray(v) for k, v in cs.items()})
    x = np.asarray(inp["L0_attn_x"])
    return [dict(base, x=np.ascontiguousarray(x[b])) for b in range(8)]




_PROG = []


def kernel(**inputs):
    inp = {k: np.asarray(v) for k, v in inputs.items()}
    if not _PROG:
        _PROG.append(build_fused())
    nc = _PROG[0]
    in_maps = host_inputs(inp)
    res = run_bass_kernel_spmd(nc, in_maps, core_ids=list(range(8)))
    return np.stack([r["y_fm"].T for r in res.results]).astype(np.float32)


# version 07 (final)
# speedup vs baseline: 1.3342x; 1.0808x over previous
import numpy as np
import ml_dtypes
import contextlib
import concourse.bass as bass
import concourse.mybir as mybir
from concourse.bass_utils import run_bass_kernel_spmd

F32 = mybir.dt.float32
BF16 = mybir.dt.bfloat16
AF = mybir.ActivationFunctionType
ALU = mybir.AluOpType
AX = mybir.AxisListType
S8 = float(1.0 / np.sqrt(8.0))
NW = 1024 * 4096


class Ctx:
    def __init__(self, nc):
        self.nc = nc
        self.es = contextlib.ExitStack()
        self._sem_counts = {}
        self._sems = {}

    def sb(self, name, shape, dtype=F32, es=None):
        return (es or self.es).enter_context(self.nc.sbuf_tensor(name, shape, dtype))

    def ps(self, name, shape, dtype=F32):
        return self.es.enter_context(self.nc.psum_tensor(name, shape, dtype))

    def sem(self, name):
        if name not in self._sems:
            self._sems[name] = self.es.enter_context(self.nc.semaphore(name))
            self._sem_counts[name] = 0
        return self._sems[name]

    def bump(self, name, k=1):
        self._sem_counts[name] += k
        return self._sem_counts[name]

    def val(self, name):
        return self._sem_counts.get(name, 0)

    def mk(self, nm):
        return (nm, self.bump(nm))


ENG_SEMS = ("qp", "qv", "qs", "qg")


def wait_marks(eng, ct, marks):
    best = {}
    for m in marks:
        if m is None:
            continue
        nm, v = m
        best[nm] = max(best.get(nm, 0), v)
    for nm, v in best.items():
        if v > 0:
            eng.wait_ge(ct.sem(nm), v)


def all_marks(ct, exclude=()):
    return [(nm, ct.val(nm)) for nm in ENG_SEMS
            if nm not in exclude and ct.val(nm) > 0]


def emit_barrier(ct, blk):
    marks = {nm: ct.val(nm) for nm in ENG_SEMS}

    @blk.tensor
    def _(t):
        wait_marks(t, ct, [(nm, v) for nm, v in marks.items() if nm != "qp"])
        t.drain().then_inc(ct.sem("qp"), 1)
    ct.bump("qp")

    @blk.vector
    def _(v):
        wait_marks(v, ct, [(nm, v2) for nm, v2 in marks.items() if nm != "qv"])
        v.drain().then_inc(ct.sem("qv"), 1)
    ct.bump("qv")

    @blk.scalar
    def _(s):
        wait_marks(s, ct, [(nm, v) for nm, v in marks.items() if nm != "qs"])
        s.drain().then_inc(ct.sem("qs"), 1)
    ct.bump("qs")

    @blk.gpsimd
    def _(g):
        wait_marks(g, ct, [(nm, v) for nm, v in marks.items() if nm != "qg"])
        g.drain().then_inc(ct.sem("qg"), 1)
    ct.bump("qg")


def hi_lo(v):
    hi = v.astype(ml_dtypes.bfloat16).astype(np.float32)
    lo = (v - hi).astype(ml_dtypes.bfloat16).astype(np.float32)
    return hi, lo


def make_consts():
    identf = np.eye(128, dtype=np.float32)
    ones_f = np.ones((65, 128), np.float32)
    ones_bf = np.ones((128, 1), np.float32)
    ones_cf = np.ones((128, 1), np.float32)
    ind_sc = np.zeros((128, 8), np.float32)
    ind_sc[0:64, 0] = 1; ind_sc[64:128, 1] = 1
    ind_sc[0:64, 6] = 1; ind_sc[64:128, 7] = 1
    ind_bc = np.zeros((4, 256), np.float32)
    ind_bc[0, 0:64] = 1; ind_bc[1, 64:128] = 1
    ind_bc[2, 128:192] = 1; ind_bc[3, 192:256] = 1
    sel8 = np.zeros((8, 1024), np.float32)
    for e in range(8):
        sel8[e, e * 128:(e + 1) * 128] = 1.0
    mtri = np.where(np.arange(128)[None, :] <= np.arange(128)[:, None], 0.0,
                    -1e30).astype(np.float32)
    mtriT = np.ascontiguousarray(mtri.T)
    k_ar = np.arange(2048, dtype=np.float32)
    qaug = np.zeros((16, 4, 2048), np.float32)
    kaug = np.zeros((16, 6, 2048), np.float32)
    for h in range(16):
        slope = 2.0 ** (-(h + 1) / 2.0)
        khi, klo = hi_lo(slope * k_ar)
        qhi, qlo = hi_lo(-slope * k_ar)
        qaug[h, 0] = 1.0; qaug[h, 1] = 1.0; qaug[h, 2] = qhi; qaug[h, 3] = qlo
        kaug[h, 0] = -1.0; kaug[h, 1] = 1.0; kaug[h, 2] = khi; kaug[h, 3] = klo
        kaug[h, 4] = 1.0; kaug[h, 5] = 1.0
    return dict(identf=identf, ones_f=ones_f, ones_bf=ones_bf, ones_cf=ones_cf,
                ind_sc=ind_sc, ind_bc=ind_bc, sel8=sel8, mtri=mtri, mtriT=mtriT,
                qaug=qaug.reshape(64, 2048), kaug=kaug.reshape(96, 2048))


def emit_postln(ct, blk, x_fm, xb, cons, pA, pB, sqt, abst, xb_marks):
    mk = ct.mk
    sq_free = None
    sums_marks = []
    for jt in range(4):
        @blk.scalar
        def _(s, jt=jt, ms=list(xb_marks), fr=sq_free):
            wait_marks(s, ct, ms + ([fr] if fr else []))
            for c in range(8):
                s.activation(out=sqt[:, c * 512:(c + 1) * 512],
                             in_=xb[:, c * 2048 + jt * 512: c * 2048 + (jt + 1) * 512],
                             func=AF.Square)
            s.drain().then_inc(ct.sem("qs"), 1)
        sq_mark = mk("qs")

        @blk.tensor
        def _(t, jt=jt, sq_mark=sq_mark, ms=list(xb_marks),
              pre=all_marks(ct, exclude=("qp",)) if jt == 0 else []):
            t.wait_ge(ct.sem("ldg"), 80)
            wait_marks(t, ct, [sq_mark] + ms + pre)
            for c in range(8):
                t.matmul(pA[0:1, 0:512], cons["ones_bf"][:, 0:1],
                         xb[:, c * 2048 + jt * 512: c * 2048 + (jt + 1) * 512],
                         start=(c == 0), stop=(c == 7))
            for c in range(8):
                t.matmul(pB[0:1, 0:512], cons["ones_bf"][:, 0:1],
                         sqt[:, c * 512:(c + 1) * 512],
                         start=(c == 0), stop=(c == 7))
            t.drain().then_inc(ct.sem("qp"), 1)
        sums_mm = mk("qp")

        @blk.vector
        def _(v, jt=jt, sums_mm=sums_mm):
            wait_marks(v, ct, [sums_mm])
            v.tensor_copy(abst[0:1, jt * 512:(jt + 1) * 512], pA[0:1, 0:512])
            v.drain()
            v.tensor_copy(abst[0:1, 2048 + jt * 512: 2048 + (jt + 1) * 512],
                          pB[0:1, 0:512])
            v.drain().then_inc(ct.sem("qv"), 1)
        sq_free = mk("qv")
        sums_marks.append(sq_free)

    @blk.vector
    def _(v, ms=list(sums_marks)):
        wait_marks(v, ct, ms)
        v.tensor_scalar_mul(abst[64:65, 2048:4096], abst[0:1, 0:2048], -1.0 / 1024.0)
        v.tensor_scalar_mul(abst[64:65, 0:2048], abst[0:1, 2048:4096], 1.0 / 1024.0)
        v.drain()
        v.scalar_tensor_tensor(out=pA[0:1, 0:2048], in0=abst[64:65, 2048:4096],
                               scalar=-1.0, in1=abst[64:65, 2048:4096],
                               op0=ALU.mult, op1=ALU.mult)
        v.drain()
        v.tensor_tensor(out=abst[32:33, 0:2048], in0=abst[64:65, 0:2048],
                        in1=pA[0:1, 0:2048], op=ALU.add)
        v.drain()
        v.tensor_scalar_add(abst[32:33, 0:2048], abst[32:33, 0:2048], 1e-5)
        v.drain().then_inc(ct.sem("qv"), 1)
    var_mark = mk("qv")

    @blk.scalar
    def _(s, var_mark=var_mark):
        wait_marks(s, ct, [var_mark])
        s.activation(out=abst[64:65, 0:2048], in_=abst[32:33, 0:2048], func=AF.Sqrt)
        s.drain().then_inc(ct.sem("qs"), 1)
    sqrt_mark = mk("qs")

    @blk.vector
    def _(v, sqrt_mark=sqrt_mark):
        wait_marks(v, ct, [sqrt_mark])
        v.reciprocal(abst[64:65, 0:2048], abst[64:65, 0:2048])
        v.drain()
        v.tensor_tensor(out=abst[64:65, 2048:4096], in0=abst[64:65, 2048:4096],
                        in1=abst[64:65, 0:2048], op=ALU.mult)
        v.drain().then_inc(ct.sem("qv"), 1)
    ab_mark = mk("qv")

    ap_mark = None
    for jt in range(4):
        @blk.tensor
        def _(t, jt=jt, ab_mark=ab_mark, fr=ap_mark,
              fr2=(sums_marks[-1] if jt == 0 else None)):
            wait_marks(t, ct, [ab_mark, fr, fr2])
            t.matmul(pA[:, 0:512], cons["ones_f"][64:65, :],
                     abst[64:65, jt * 512:(jt + 1) * 512], start=True, stop=True)
            t.matmul(pB[:, 0:512], cons["ones_f"][64:65, :],
                     abst[64:65, 2048 + jt * 512:2048 + (jt + 1) * 512],
                     start=True, stop=True)
            t.drain().then_inc(ct.sem("qp"), 1)
        bc_mark = mk("qp")

        @blk.vector
        def _(v, jt=jt, bc_mark=bc_mark):
            wait_marks(v, ct, [bc_mark])
            for c in range(8):
                sl = slice(c * 2048 + jt * 512, c * 2048 + (jt + 1) * 512)
                v.tensor_tensor(out=x_fm[:, sl], in0=x_fm[:, sl], in1=pA[:, 0:512],
                                op=ALU.mult)
            v.drain()
            for c in range(8):
                sl = slice(c * 2048 + jt * 512, c * 2048 + (jt + 1) * 512)
                v.tensor_tensor(out=x_fm[:, sl], in0=x_fm[:, sl], in1=pB[:, 0:512],
                                op=ALU.add)
            v.drain().then_inc(ct.sem("qv"), 1)
        ap_mark = mk("qv")
    return ap_mark


def emit_a0(ct, blk, x_ap, io, cons, x_fm, xb, bank, es, bias, CONSTS_T, LDGC_T):
    nc = ct.nc
    mk = ct.mk
    xtok = ct.sb("a0_xtok", [128, 2048], F32, es=es)
    wq = ct.sb("a0_wq_sb", [128, 2 * 2048], BF16, es=es)
    wk = ct.sb("a0_wk_sb", [128, 2 * 2048], BF16, es=es)
    wv = ct.sb("a0_wv_sb", [128, 2 * 2048], BF16, es=es)
    wo = ct.sb("a0_wo_sb", [128, 2 * 2048], BF16, es=es)
    qr = ct.sb("a0_qr", [128, 1024], BF16, es=es)
    kr = ct.sb("a0_kr", [128, 1024], BF16, es=es)
    vr = ct.sb("a0_vr", [128, 1024], BF16, es=es)
    prod = ct.sb("a0_prod", [128, 1024], BF16, es=es)
    probs = ct.sb("a0_probs", [4, 512], BF16, es=es)
    mrow = ct.sb("a0_mrow", [4, 4], F32, es=es)
    lrow = ct.sb("a0_lrow", [4, 4], F32, es=es)
    linv = ct.sb("a0_linv", [4, 4], F32, es=es)
    wvs = ct.sb("a0_wvs", [128, 1024], BF16, es=es)

    @blk.sync
    def _(sy):
        for i in range(16):
            if i >= 2:
                wait_marks(sy, ct, [("qp", i - 1)])
            sy.dma_start(out=xtok[:, (i % 2) * 1024:(i % 2) * 1024 + 1024],
                         in_=x_ap[i * 128:(i + 1) * 128, :]).then_inc(ct.sem("xld"), 16)
            ct.bump("xld", 16)

    evv = {}
    evs = {}
    for i in range(16):
        @blk.tensor
        def _(t, i=i, ev=evv.get(i - 1), es_=evs.get(i - 1)):
            t.wait_ge(ct.sem("xld"), min(16 * (i + 2), 256))
            t.wait_ge(ct.sem("lds"), CONSTS_T)
            wait_marks(t, ct, [ev, es_])
            for g4 in range(2):
                for c4 in range(4):
                    c = g4 * 4 + c4
                    t.transpose(bank[6 + g4][:, c4 * 128:(c4 + 1) * 128],
                                xtok[:, (i % 2) * 1024 + c * 128:
                                     (i % 2) * 1024 + (c + 1) * 128],
                                cons["identf"][:])
            t.drain().then_inc(ct.sem("qp"), 1)
        td = mk("qp")

        @blk.vector
        def _(v, i=i, td=td):
            wait_marks(v, ct, [td])
            dst = x_fm.rearrange("p (c t) -> p c t", c=8)[:, 0:4, i * 128:(i + 1) * 128]
            v.tensor_copy(dst, bank[6][:].rearrange("p (c q) -> p c q", c=4))
            v.drain()
            dst = x_fm.rearrange("p (c t) -> p c t", c=8)[:, 4:8, i * 128:(i + 1) * 128]
            v.tensor_copy(dst, bank[7][:].rearrange("p (c q) -> p c q", c=4))
            v.drain().then_inc(ct.sem("qv"), 1)
        evv[i] = mk("qv")

        @blk.scalar
        def _(s, i=i, td=td):
            wait_marks(s, ct, [td])
            for g4 in range(2):
                dst = xb.rearrange("p (c t) -> p c t", c=8)[:, g4 * 4:(g4 + 1) * 4,
                                                            i * 128:(i + 1) * 128]
                s.copy(out=dst, in_=bank[6 + g4][:].rearrange("p (c q) -> p c q", c=4))
            s.drain().then_inc(ct.sem("qs"), 1)
        evs[i] = mk("qs")
    x_ready = [evv[15], evs[15]]

    cls_t = {}
    cls_compute_done = {}

    def emit_class_dma(g, r):
        half = (r % 2) * 2048
        sem_nm = "wqa" if r % 2 == 0 else "wqb"
        for w_sb, nm in ((wq, "a0_wq"), (wk, "a0_wk"), (wv, "a0_wv")):
            src = io[nm][:, r * 256:(r + 1) * 256].rearrange("(k p) i -> p k i", p=128)
            dst = w_sb[:, half:half + 2048].rearrange("p (k i) -> p k i", k=8)
            g.dma_start(out=dst, in_=src).then_inc(ct.sem(sem_nm), 16)
            ct.bump(sem_nm, 16)
        src = io["a0_wo"][r * 256:(r + 1) * 256, :].rearrange("(k p) i -> p k i", p=128)
        dst = wo[:, half:half + 2048].rearrange("p (k i) -> p k i", k=2)
        g.dma_start(out=dst, in_=src).then_inc(ct.sem(sem_nm), 16)
        ct.bump(sem_nm, 16)
        cls_t[r] = (sem_nm, ct.val(sem_nm))

    @blk.gpsimd
    def _(g):
        emit_class_dma(g, 0)

    @blk.gpsimd
    def _(g):
        emit_class_dma(g, 1)

    for r in range(4):
        half = (r % 2) * 2048

        @blk.tensor
        def _(t, r=r, half=half, xr=list(x_ready), pre=all_marks(ct, ("qp",)) if r == 0 else []):
            t.wait_ge(ct.sem(cls_t[r][0]), cls_t[r][1])
            t.wait_ge(ct.sem("lds"), CONSTS_T)
            t.wait_ge(ct.sem("ldg"), LDGC_T)
            wait_marks(t, ct, xr + pre + ([cls_compute_done[r - 1]] if r > 0 else []))
            for dst_banks, w_sb in (((0, 1), wq), ((2, 3), wk), ((4, 5), wv)):
                for ic in range(2):
                    for k in range(8):
                        t.matmul(bank[dst_banks[ic]][:],
                                 w_sb[:, half + k * 256 + ic * 128:
                                      half + k * 256 + ic * 128 + 128],
                                 xb[:, k * 2048 + r: (k + 1) * 2048: 4],
                                 start=(k == 0), stop=(k == 7))
            t.drain().then_inc(ct.sem("qp"), 1)
        qkv_mm = mk("qp")

        @blk.vector
        def _(v, r=r, qkv_mm=qkv_mm):
            wait_marks(v, ct, [qkv_mm])
            for ic in range(2):
                v.tensor_scalar(out=qr[:, ic * 512:(ic + 1) * 512], in0=bank[ic][:],
                                scalar1=bias[:, r * 2 + ic: r * 2 + ic + 1], scalar2=None,
                                op0=ALU.add)
                v.tensor_scalar(out=kr[:, ic * 512:(ic + 1) * 512], in0=bank[2 + ic][:],
                                scalar1=bias[:, 8 + r * 2 + ic: 8 + r * 2 + ic + 1],
                                scalar2=None, op0=ALU.add)
                v.tensor_scalar(out=vr[:, ic * 512:(ic + 1) * 512], in0=bank[4 + ic][:],
                                scalar1=bias[:, 16 + r * 2 + ic: 16 + r * 2 + ic + 1],
                                scalar2=None, op0=ALU.add)
            v.drain()
            v.tensor_tensor(out=prod[:], in0=qr[:], in1=kr[:], op=ALU.mult)
            v.drain().then_inc(ct.sem("qv"), 1)
        prod_mark = mk("qv")

        @blk.tensor
        def _(t, prod_mark=prod_mark):
            wait_marks(t, ct, [prod_mark])
            t.matmul(bank[6][0:4, :], cons["ind_sc"][:, 0:4], prod[:, 0:512],
                     start=True, stop=False)
            t.matmul(bank[6][0:4, :], cons["ind_sc"][:, 4:8], prod[:, 512:1024],
                     start=False, stop=True)
            t.drain().then_inc(ct.sem("qp"), 1)
        sc_mark = mk("qp")

        @blk.vector
        def _(v, sc_mark=sc_mark):
            wait_marks(v, ct, [sc_mark])
            for n in range(4):
                v.reduce_max(mrow[:, n:n + 1], bank[6][0:4, n * 128:(n + 1) * 128],
                             axis=AX.X, negate=True)
            v.drain()
            v.tensor_scalar_mul(mrow[:], mrow[:], 0.125)
            v.drain().then_inc(ct.sem("qv"), 1)
        m_mark = mk("qv")

        @blk.scalar
        def _(s, m_mark=m_mark):
            wait_marks(s, ct, [m_mark])
            for n in range(4):
                s.activation(out=probs[:, n * 128:(n + 1) * 128],
                             in_=bank[6][0:4, n * 128:(n + 1) * 128],
                             func=AF.Exp, scale=0.125, bias=mrow[:, n:n + 1],
                             accum_out=lrow[:, n:n + 1])
            s.drain().then_inc(ct.sem("qs"), 1)
        e_mark = mk("qs")

        @blk.vector
        def _(v, e_mark=e_mark):
            wait_marks(v, ct, [e_mark])
            v.reciprocal(linv[:], lrow[:])
            v.drain()
            for n in range(4):
                v.tensor_scalar_mul(probs[:, n * 128:(n + 1) * 128],
                                    probs[:, n * 128:(n + 1) * 128], linv[:, n:n + 1])
            v.drain().then_inc(ct.sem("qv"), 1)
        p_mark = mk("qv")

        @blk.tensor
        def _(t, p_mark=p_mark):
            wait_marks(t, ct, [p_mark])
            t.matmul(bank[0][:], cons["ind_bc"][0:4, 0:128], probs[:], start=True, stop=True)
            t.matmul(bank[1][:], cons["ind_bc"][0:4, 128:256], probs[:], start=True, stop=True)
            t.drain().then_inc(ct.sem("qp"), 1)
        bc_mark = mk("qp")

        @blk.vector
        def _(v, bc_mark=bc_mark):
            wait_marks(v, ct, [bc_mark])
            v.tensor_tensor(out=wvs[:, 0:512], in0=bank[0][:], in1=vr[:, 0:512], op=ALU.mult)
            v.tensor_tensor(out=wvs[:, 512:1024], in0=bank[1][:], in1=vr[:, 512:1024],
                            op=ALU.mult)
            v.drain().then_inc(ct.sem("qv"), 1)
        wv_mark = mk("qv")

        ev_mark = None
        for dpair in range(4):
            d0, d1 = 2 * dpair, 2 * dpair + 1

            @blk.tensor
            def _(t, half=half, d0=d0, d1=d1, wv_mark=wv_mark, fr=ev_mark):
                wait_marks(t, ct, [wv_mark, fr])
                for dd, bk_i in ((d0, 2), (d1, 3)):
                    for k in range(2):
                        t.matmul(bank[bk_i][:],
                                 wo[:, half + k * 1024 + dd * 128:
                                     half + k * 1024 + dd * 128 + 128],
                                 wvs[:, k * 512:(k + 1) * 512],
                                 start=(k == 0), stop=(k == 1))
                t.drain().then_inc(ct.sem("qp"), 1)
            y_mark = mk("qp")

            @blk.vector
            def _(v, r=r, d0=d0, d1=d1, y_mark=y_mark):
                wait_marks(v, ct, [y_mark])
                for dd, bk_i in ((d0, 2), (d1, 3)):
                    sl = slice(dd * 2048 + r, (dd + 1) * 2048, 4)
                    v.scalar_tensor_tensor(out=x_fm[:, sl], in0=bank[bk_i][:],
                                           scalar=bias[:, 24 + dd: 24 + dd + 1],
                                           in1=x_fm[:, sl], op0=ALU.add, op1=ALU.add)
                v.drain().then_inc(ct.sem("qv"), 1)
            ev_mark = mk("qv")
        cls_compute_done[r] = ev_mark

        if r < 2:
            @blk.gpsimd
            def _(g, r=r, gate=ev_mark):
                wait_marks(g, ct, [gate])
                emit_class_dma(g, r + 2)

    @blk.vector
    def _(v, gate=cls_compute_done[3]):
        wait_marks(v, ct, [gate])
        for c in range(8):
            v.tensor_copy(xb[:, c * 2048:(c + 1) * 2048], x_fm[:, c * 2048:(c + 1) * 2048])
        v.drain().then_inc(ct.sem("qv"), 1)
    return [mk("qv")]


def emit_f0(ct, blk, io, cons, x_fm, xb, pA, pB, xb_marks, es, gwf, CONSTS_T, LDGC_T):
    mk = ct.mk
    act = ct.sb("f0_act", [128, 11 * 2048], BF16, es=es)
    xlo = act[:, 0:16384]
    ghi = ct.sb("f0_ghi", [128, 64], BF16, es=es)
    glo = ct.sb("f0_glo", [128, 64], BF16, es=es)
    lgt = ct.sb("f0_lgt", [128, 8], F32, es=es)
    gA = ct.sb("f0_gA", [128, 8], F32, es=es)
    gB = ct.sb("f0_gB", [128, 8], F32, es=es)
    m1 = ct.sb("f0_m1", [128, 1], F32, es=es)
    m2 = ct.sb("f0_m2", [128, 1], F32, es=es)
    dn = ct.sb("f0_dn", [128, 1], F32, es=es)
    wfm = ct.sb("f0_wfm", [8, 2048], F32, es=es)
    wbc = ct.sb("f0_wbc", [128, 2048], BF16, es=es)
    sg = ct.sb("f0_sg", [128, 2048], F32, es=es)
    lgf = sg[0:8, 0:2048]
    wgu = ct.sb("f0_wgu", [128, 4 * 2048], BF16, es=es)
    wdb = ct.sb("f0_wdb", [128, 2 * 1408], BF16, es=es)

    @blk.vector
    def _(v, ms=list(xb_marks)):
        v.wait_ge(ct.sem("ldg"), LDGC_T)
        wait_marks(v, ct, ms)
        v.tensor_copy(ghi[:], gwf[:])
        v.drain()
        v.tensor_tensor(out=glo[:], in0=gwf[:], in1=ghi[:], op=ALU.subtract)
        for c in range(8):
            v.tensor_tensor(out=xlo[:, c * 2048:(c + 1) * 2048],
                            in0=x_fm[:, c * 2048:(c + 1) * 2048],
                            in1=xb[:, c * 2048:(c + 1) * 2048], op=ALU.subtract)
        v.drain().then_inc(ct.sem("qv"), 1)
    gate_prep = mk("qv")

    lg_ev = {}
    for jt in range(4):
        @blk.tensor
        def _(t, jt=jt, gate_prep=gate_prep, pre=all_marks(ct, ("qp",)) if jt == 0 else []):
            wait_marks(t, ct, [gate_prep] + pre + ([lg_ev[jt - 1]] if jt > 0 else []))
            first = True
            for xsrc in (xb, xlo):
                for gsrc in (ghi, glo):
                    for k in range(8):
                        t.matmul(pA[0:8, jt * 512:(jt + 1) * 512],
                                 gsrc[:, k * 8:(k + 1) * 8],
                                 xsrc[:, k * 2048 + jt * 512: k * 2048 + (jt + 1) * 512],
                                 start=first, stop=(xsrc is xlo and gsrc is glo and k == 7))
                        first = False
            t.drain().then_inc(ct.sem("qp"), 1)
        lg_mm = mk("qp")

        @blk.vector
        def _(v, jt=jt, lg_mm=lg_mm):
            wait_marks(v, ct, [lg_mm])
            v.tensor_copy(lgf[:, jt * 512:(jt + 1) * 512], pA[0:8, jt * 512:(jt + 1) * 512])
            v.drain().then_inc(ct.sem("qv"), 1)
        lg_ev[jt] = mk("qv")

    wfm_marks = []
    tp_free = None
    for tt in range(16):
        @blk.tensor
        def _(t, tt=tt, gate=lg_ev[3], fr=tp_free):
            wait_marks(t, ct, [gate, fr])
            t.transpose(pB[0:128, 0:8], lgf[:, tt * 128:(tt + 1) * 128],
                        cons["identf"][0:8, 0:8])
            t.drain().then_inc(ct.sem("qp"), 1)
        tpm = mk("qp")

        @blk.vector
        def _(v, tt=tt, tpm=tpm):
            wait_marks(v, ct, [tpm])
            v.tensor_copy(lgt[:], pB[0:128, 0:8])
            v.drain()
            v.reduce_max(m1[:], lgt[:], axis=AX.X, negate=True)
            v.drain()
            v.tensor_scalar(out=gA[:], in0=lgt[:], scalar1=m1[:], scalar2=0.0,
                            op0=ALU.add, op1=ALU.is_equal)
            v.drain()
            v.scalar_tensor_tensor(out=gB[:], in0=gA[:], scalar=-1e30, in1=lgt[:],
                                   op0=ALU.mult, op1=ALU.add)
            v.drain()
            v.reduce_max(m2[:], gB[:], axis=AX.X, negate=True)
            v.drain().then_inc(ct.sem("qv"), 1)
        m_mark = mk("qv")

        @blk.scalar
        def _(s, m_mark=m_mark):
            wait_marks(s, ct, [m_mark])
            s.activation(out=gA[:], in_=lgt[:], func=AF.Exp, bias=m1[:])
            s.drain().then_inc(ct.sem("qs"), 1)
        e_mark = mk("qs")

        @blk.vector
        def _(v, e_mark=e_mark):
            wait_marks(v, ct, [e_mark])
            v.tensor_scalar(out=gB[:], in0=lgt[:], scalar1=m2[:], scalar2=0.0,
                            op0=ALU.add, op1=ALU.is_ge)
            v.drain()
            v.tensor_tensor(out=gA[:], in0=gA[:], in1=gB[:], op=ALU.mult)
            v.drain()
            v.reduce_sum(dn[:], gA[:], axis=AX.X)
            v.drain()
            v.reciprocal(dn[:], dn[:])
            v.drain()
            v.tensor_scalar_mul(gA[:], gA[:], dn[:])
            v.drain().then_inc(ct.sem("qv"), 1)
        w_mark = mk("qv")

        @blk.tensor
        def _(t, w_mark=w_mark):
            wait_marks(t, ct, [w_mark])
            t.transpose(pB[0:8, 0:128], gA[:], cons["identf"][:])
            t.drain().then_inc(ct.sem("qp"), 1)
        wtm = mk("qp")

        @blk.vector
        def _(v, tt=tt, wtm=wtm):
            wait_marks(v, ct, [wtm])
            v.tensor_copy(wfm[:, tt * 128:(tt + 1) * 128], pB[0:8, 0:128])
            v.drain().then_inc(ct.sem("qv"), 1)
        tp_free = mk("qv")
        wfm_marks.append(tp_free)
    wfm_mark = wfm_marks[-1]

    def pair_list(n):
        out = []
        i = 0
        while i < n:
            out.append((i, min(2, n - i)))
            i += 2
        return out

    slab_gate = [wfm_mark]

    def emit_hg_hu_act(n_ic, wg_name, wu_name, wrow0, scale, wcol0=0):
        pairs = pair_list(n_ic)
        dma_t = {}
        ic_done = {}

        def emit_wdma(g, pi, gates):
            ic0, cnt = pairs[pi]
            sem_nm = "wqa" if pi % 2 == 0 else "wqb"
            wait_marks(g, ct, gates)
            for j, nm in ((0, wg_name), (1, wu_name)):
                slot = (pi % 2) + 2 * j
                for o in range(cnt):
                    ic = ic0 + o
                    src = io[nm][wrow0:wrow0 + 1024, wcol0 + ic * 128:wcol0 + (ic + 1) * 128]
                    src = src.rearrange("(k p) i -> p k i", p=128)
                    dst = wgu[:, slot * 2048 + o * 1024: slot * 2048 + (o + 1) * 1024]
                    dst = dst.rearrange("p (k i) -> p k i", k=8)
                    g.dma_start(out=dst, in_=src).then_inc(ct.sem(sem_nm), 16)
                    ct.bump(sem_nm, 16)
            dma_t[pi] = (sem_nm, ct.val(sem_nm))

        phase_gate = list(slab_gate)

        @blk.gpsimd
        def _(g):
            emit_wdma(g, 0, phase_gate)

        @blk.gpsimd
        def _(g):
            emit_wdma(g, 1, phase_gate)

        sil_mark = None
        act_mark = None
        for ic in range(n_ic):
            pi, o = ic // 2, ic % 2
            slot_g, slot_u = (pi % 2), 2 + (pi % 2)

            @blk.tensor
            def _(t, ic=ic, pi=pi, o=o, slot_g=slot_g, slot_u=slot_u,
                  fr_sil=sil_mark, fr_act=act_mark):
                t.wait_ge(ct.sem(dma_t[pi][0]), dma_t[pi][1])
                wait_marks(t, ct, [fr_sil, fr_act])
                for ps, slot in ((pA, slot_g), (pB, slot_u)):
                    for k in range(8):
                        for jt in range(4):
                            t.matmul(ps[:, jt * 512:(jt + 1) * 512],
                                     wgu[:, slot * 2048 + o * 1024 + k * 128:
                                         slot * 2048 + o * 1024 + k * 128 + 128],
                                     xb[:, k * 2048 + jt * 512: k * 2048 + (jt + 1) * 512],
                                     start=(k == 0), stop=(k == 7))
                t.drain().then_inc(ct.sem("qp"), 1)
            mm_mark = mk("qp")

            @blk.scalar
            def _(s, mm_mark=mm_mark):
                wait_marks(s, ct, [mm_mark])
                s.activation(out=sg[:], in_=pA[:], func=AF.Silu)
                s.drain().then_inc(ct.sem("qs"), 1)
            sil_mark = mk("qs")

            @blk.vector
            def _(v, ic=ic, sil_mark=sil_mark, scale=scale):
                wait_marks(v, ct, [sil_mark])
                aslot = act[:, ic * 2048:(ic + 1) * 2048]
                v.tensor_tensor(out=aslot, in0=sg[:], in1=pB[:], op=ALU.mult)
                if scale:
                    v.drain()
                    v.tensor_tensor(out=aslot, in0=aslot, in1=wbc[:], op=ALU.mult)
                v.drain().then_inc(ct.sem("qv"), 1)
            act_mark = mk("qv")
            ic_done[ic] = act_mark

            if pi + 2 < len(pairs) and (o == pairs[pi][1] - 1):
                @blk.gpsimd
                def _(g, pi=pi, gate=ic_done[ic]):
                    emit_wdma(g, pi + 2, [gate])
        slab_gate[0] = ic_done[n_ic - 1]
        return ic_done

    wd_gate = [wfm_mark]

    def emit_y_phase(n_ic, wd_name, wdrow0, act_ready):
        dma_t = {}
        ev = {}
        phase_gate = list(wd_gate)

        def emit_wd_dma(g, d, gates):
            sem_nm = "wda" if d % 2 == 0 else "wdb"
            wait_marks(g, ct, gates)
            src = io[wd_name][wdrow0:wdrow0 + n_ic * 128, d * 128:(d + 1) * 128]
            src = src.rearrange("(k p) i -> p k i", p=128)
            dst = wdb[:, (d % 2) * 1408: (d % 2) * 1408 + n_ic * 128]
            dst = dst.rearrange("p (k i) -> p k i", k=n_ic)
            g.dma_start(out=dst, in_=src).then_inc(ct.sem(sem_nm), 16)
            ct.bump(sem_nm, 16)
            dma_t[d] = (sem_nm, ct.val(sem_nm))

        @blk.gpsimd
        def _(g):
            emit_wd_dma(g, 0, phase_gate)

        @blk.gpsimd
        def _(g):
            emit_wd_dma(g, 1, phase_gate)

        for d in range(8):
            ps = pA if d % 2 == 0 else pB

            @blk.tensor
            def _(t, d=d, ps=ps, act_ready=act_ready, fr=ev.get(d - 2)):
                t.wait_ge(ct.sem(dma_t[d][0]), dma_t[d][1])
                wait_marks(t, ct, [act_ready, fr])
                for k in range(n_ic):
                    for jt in range(4):
                        t.matmul(ps[:, jt * 512:(jt + 1) * 512],
                                 wdb[:, (d % 2) * 1408 + k * 128:
                                     (d % 2) * 1408 + k * 128 + 128],
                                 act[:, k * 2048 + jt * 512: k * 2048 + (jt + 1) * 512],
                                 start=(k == 0), stop=(k == n_ic - 1))
                t.drain().then_inc(ct.sem("qp"), 1)
            y_mark = mk("qp")

            @blk.vector
            def _(v, d=d, ps=ps, y_mark=y_mark):
                wait_marks(v, ct, [y_mark])
                v.tensor_tensor(out=x_fm[:, d * 2048:(d + 1) * 2048],
                                in0=ps[:], in1=x_fm[:, d * 2048:(d + 1) * 2048],
                                op=ALU.add)
                v.drain().then_inc(ct.sem("qv"), 1)
            ev[d] = mk("qv")

            if d + 2 < 8:
                @blk.gpsimd
                def _(g, d=d, gate=ev[d]):
                    emit_wd_dma(g, d + 2, [gate])
        wd_gate[0] = ev[7]
        slab_gate[0] = ev[7]
        return ev[7]

    for grp in range(2):
        ic_done = emit_hg_hu_act(11, "f0_swg", "f0_swu", 0, False, wcol0=grp * 1408)
        emit_y_phase(11, "f0_swd", grp * 1408, ic_done[10])

    wbc_ev = None
    for e in range(8):
        @blk.tensor
        def _(t, e=e, gate=wd_gate[0], wfm_mark=wfm_mark, fr=wbc_ev):
            wait_marks(t, ct, [gate, wfm_mark, fr])
            for jt in range(4):
                t.matmul(pA[:, jt * 512:(jt + 1) * 512],
                         cons["sel8"][:, e * 128:(e + 1) * 128],
                         wfm[0:8, jt * 512:(jt + 1) * 512], start=True, stop=True)
            t.drain().then_inc(ct.sem("qp"), 1)
        wbc_mm = mk("qp")

        @blk.vector
        def _(v, wbc_mm=wbc_mm):
            wait_marks(v, ct, [wbc_mm])
            v.tensor_copy(wbc[:], pA[:])
            v.drain().then_inc(ct.sem("qv"), 1)
        wbc_ev = mk("qv")
        slab_gate[0] = wbc_ev

        ic_done = emit_hg_hu_act(11, "f0_wg", "f0_wu", e * 1024, True)
        wbc_ev = emit_y_phase(11, "f0_wd", e * 1408, ic_done[10])

    @blk.vector
    def _(v, gate=wd_gate[0]):
        wait_marks(v, ct, [gate])
        for c in range(8):
            v.tensor_copy(xb[:, c * 2048:(c + 1) * 2048], x_fm[:, c * 2048:(c + 1) * 2048])
        v.drain().then_inc(ct.sem("qv"), 1)
    return [mk("qv")], act


def emit_a1(ct, blk, io, cons, x2d, x3d, u_dram, xb, pA, pB, xb_marks, es,
            CONSTS_T, LDGC_T):
    mk = ct.mk
    QA = ct.sb("a1_QA", [70, 16384], BF16, es=es)
    KA = ct.sb("a1_KA", [70, 16384], BF16, es=es)
    V = ct.sb("a1_V", [128, 8320], BF16, es=es)
    attn = ct.sb("a1_attn", [128, 16384], BF16, es=es)
    xjt = ct.sb("a1_xjt", [128, 4096], F32, es=es)
    wstr = ct.sb("a1_wstr", [128, 4096], BF16, es=es)
    dump = ct.sb("a1_dump", [128, 2048], BF16, es=es)
    ptb = ct.sb("a1_ptb", [128, 3072], BF16, es=es)
    mcols = ct.sb("a1_mcols", [128, 32], F32, es=es)
    abst = ct.sb("a1_abst", [65, 4096], F32, es=es)
    bias8 = ct.sb("a1_bias8", [128, 16], F32, es=es)
    bvrow = ct.sb("a1_bvrow_sb", [1, 512], F32, es=es)
    pS = [pB[:, 0:512], pB[:, 512:1024]]
    pACC = pB[:, 1024:1536]
    pTf = pB[:, 1536:2048]
    biases = io["a1_biases_sb"]



    @blk.vector
    def _(v, ms=list(xb_marks)):
        v.wait_ge(ct.sem("lds"), CONSTS_T)
        wait_marks(v, ct, ms)
        v.tensor_scalar_mul(bias8[:], biases[:, 0:16], S8)
        v.memset(V.rearrange("p (k h c) -> p k h c", h=8, c=65)[:, :, :, 64:65], 1.0)
        v.drain().then_inc(ct.sem("qv"), 1)
    bias8_done = mk("qv")

    attn_done = []
    for g in range(2):
        gc = g * 512

        @blk.gpsimd
        def _(gp, g=g, pre=all_marks(ct, ("qg",))):
            wait_marks(gp, ct, pre)
            qa = io["c_qaug"].rearrange("(h r) n -> h r n", r=4)
            ka = io["c_kaug"].rearrange("(h r) n -> h r n", r=6)
            for hh in range(8):
                h = g * 8 + hh
                gp.dma_start(out=QA[66:70, hh * 2048:(hh + 1) * 2048],
                             in_=qa[h]).then_inc(ct.sem("aug"), 16)
                gp.dma_start(out=KA[64:70, hh * 2048:(hh + 1) * 2048],
                             in_=ka[h]).then_inc(ct.sem("aug"), 16)
                ct.bump("aug", 32)
            gp.memset(QA[64:66, :], 0.0)
            gp.drain().then_inc(ct.sem("qg"), 1)
        aug_dma = ct.val("aug")
        aug_mark = mk("qg")

        @blk.sync
        def _(sy, gc=gc, pre=all_marks(ct)):
            wait_marks(sy, ct, pre)
            sy.dma_start(out=bvrow[:], in_=io["a1_bvrow"][0:1, gc:gc + 512]
                         ).then_inc(ct.sem("lds"), 16)
            ct.bump("lds", 16)
        BV_T = ct.val("lds")

        ps_free = {0: None, 1: None}
        for wnm, dst, bcol0 in (("a1_wq", QA, 0), ("a1_wk", KA, 8)):
            @blk.gpsimd
            def _(gp, wnm=wnm, gc=gc, pre=all_marks(ct, ("qg",))):
                wait_marks(gp, ct, pre)
                src = io[wnm][:, gc:gc + 512].rearrange("(k p) i -> p k i", p=128)
                gp.dma_start(out=wstr[:].rearrange("p (k i) -> p k i", k=8),
                             in_=src).then_inc(ct.sem("wqs"), 16)
                ct.bump("wqs", 16)
            w_t = ct.val("wqs")

            pa_ev = ps_free.get("pa")
            for ic in range(4):
                @blk.tensor
                def _(t, ic=ic, w_t=w_t, fr=pa_ev):
                    t.wait_ge(ct.sem("wqs"), w_t)
                    wait_marks(t, ct, [fr])
                    for k in range(8):
                        for jt in range(4):
                            t.matmul(pA[:, jt * 512:(jt + 1) * 512],
                                     wstr[:, k * 512 + ic * 128: k * 512 + ic * 128 + 128],
                                     xb[:, k * 2048 + jt * 512: k * 2048 + (jt + 1) * 512],
                                     start=(k == 0), stop=(k == 7))
                    t.drain().then_inc(ct.sem("qp"), 1)
                mm_mark = mk("qp")

                @blk.vector
                def _(v, ic=ic, mm_mark=mm_mark, dst=dst,
                      bcol0=bcol0, g=g, bias8_done=bias8_done):
                    wait_marks(v, ct, [mm_mark, bias8_done])
                    bcol = bcol0 + g * 4 + ic
                    for jt in range(4):
                        for half in range(2):
                            hh = 2 * ic + half
                            v.tensor_scalar(
                                out=dst[0:64, hh * 2048 + jt * 512:
                                        hh * 2048 + (jt + 1) * 512],
                                in0=pA[half * 64:(half + 1) * 64, jt * 512:(jt + 1) * 512],
                                scalar1=S8,
                                scalar2=bias8[half * 64:(half + 1) * 64, bcol:bcol + 1],
                                op0=ALU.mult, op1=ALU.add)
                    v.drain().then_inc(ct.sem("qv"), 1)
                pa_ev = mk("qv")
            ps_free["pa"] = pa_ev

        mb_pe = None
        mb_sqrt = None
        for hh in range(8):
            hc = hh * 2048

            @blk.scalar
            def _(s, hc=hc, pa_ev=pa_ev, fr=mb_pe):
                wait_marks(s, ct, [pa_ev, fr])
                s.activation(out=ptb[0:64, 0:2048], in_=QA[0:64, hc:hc + 2048],
                             func=AF.Square)
                s.activation(out=ptb[64:128, 0:2048], in_=KA[0:64, hc:hc + 2048],
                             func=AF.Square)
                s.drain().then_inc(ct.sem("qs"), 1)
            sqm = mk("qs")

            @blk.tensor
            def _(t, sqm=sqm, fr=mb_sqrt):
                wait_marks(t, ct, [sqm, fr])
                for jt in range(4):
                    t.matmul(pA[0:1, jt * 512:(jt + 1) * 512],
                             cons["ones_bf"][64:128, 0:1],
                             ptb[64:128, jt * 512:(jt + 1) * 512],
                             start=True, stop=True)
                t.drain().then_inc(ct.sem("qp"), 1)
            kn_pe = mk("qp")

            @blk.vector
            def _(v, hh=hh, m=kn_pe):
                wait_marks(v, ct, [m])
                v.reduce_max(mcols[0:1, hh:hh + 1], pA[0:1, 0:2048], axis=AX.X)
                v.drain().then_inc(ct.sem("qv"), 1)
            kmax_dve = mk("qv")

            @blk.tensor
            def _(t, sqm=sqm, kd=kmax_dve):
                wait_marks(t, ct, [sqm, kd])
                for jt in range(4):
                    t.matmul(pA[0:1, jt * 512:(jt + 1) * 512],
                             cons["ones_bf"][0:64, 0:1],
                             ptb[0:64, jt * 512:(jt + 1) * 512],
                             start=True, stop=True)
                t.drain().then_inc(ct.sem("qp"), 1)
            mb_pe = mk("qp")

            @blk.scalar
            def _(s, hh=hh, m=mb_pe, mbs_need=ct.val("mbs")):
                s.wait_ge(ct.sem("mbs"), mbs_need)
                wait_marks(s, ct, [m])
                s.activation(out=dump[0:1, 0:2048], in_=pA[0:1, 0:2048],
                             func=AF.Sqrt, scale=mcols[0:1, hh:hh + 1])
                s.drain().then_inc(ct.sem("qs"), 1)
            mb_sqrt = mk("qs")

            @blk.sync
            def _(sy, hc=hc, m=mb_sqrt, am=aug_mark):
                wait_marks(sy, ct, [m, am])
                sy.dma_start(out=QA[64:65, hc:hc + 2048], in_=dump[0:1, 0:2048]
                             ).then_inc(ct.sem("mbs"), 16)
                ct.bump("mbs", 16)
        MBS_G = ct.val("mbs")

        @blk.gpsimd
        def _(gp, gc=gc, pre=all_marks(ct, ("qg",))):
            wait_marks(gp, ct, pre)
            src = io["a1_wv"][:, gc:gc + 512].rearrange("(k p) i -> p k i", p=128)
            gp.dma_start(out=wstr[:].rearrange("p (k i) -> p k i", k=8),
                         in_=src).then_inc(ct.sem("wqs"), 16)
            ct.bump("wqs", 16)
        wv_t = ct.val("wqs")

        for kt in range(16):
            sl = kt % 2

            @blk.tensor
            def _(t, kt=kt, sl=sl, wv_t=wv_t, gc=gc, fr=ps_free[sl], BV_T=BV_T):
                t.wait_ge(ct.sem("lds"), BV_T)
                t.wait_ge(ct.sem("wqs"), wv_t)
                wait_marks(t, ct, [fr])
                for k in range(8):
                    t.matmul(pS[sl][:],
                             xb[:, k * 2048 + kt * 128: k * 2048 + kt * 128 + 128],
                             wstr[:, k * 512:(k + 1) * 512],
                             start=(k == 0), stop=False)
                t.matmul(pS[sl][:], cons["ones_f"][0:1, :],
                         bvrow[0:1, 0:512], start=False, stop=True)
                t.drain().then_inc(ct.sem("qp"), 1)
            vm_mark = mk("qp")

            @blk.vector
            def _(v, kt=kt, sl=sl, vm_mark=vm_mark):
                wait_marks(v, ct, [vm_mark])
                v.tensor_copy(
                    V[:, kt * 520:kt * 520 + 520].rearrange(
                        "p (h c) -> p h c", c=65)[:, :, 0:64],
                    pS[sl][:].rearrange("p (h c) -> p h c", c=64))
                v.drain().then_inc(ct.sem("qv"), 1)
            ps_free[sl] = mk("qv")

        pairs_done = attn_done[-1] if attn_done else None
        first_of_group = True
        pair_evacs = []
        for jp in range(4):
            hA, hB = g * 8 + 2 * jp, g * 8 + 2 * jp + 1
            heads = [(hA, (2 * jp) * 2048, pA, 0), (hB, (2 * jp + 1) * 2048, pB, 1536)]

            def head_groups(ps_):
                pacc = ps_[:, 1536:2048]
                slots = [ps_[:, 0:512], ps_[:, 512:1024], ps_[:, 1024:1536]]
                out = []
                for qt in range(4):
                    nkc = 4 * (qt + 1)
                    kcs = list(range(nkc))
                    gsets = [kcs[i:i + 3] for i in range(0, nkc, 3)]
                    for gi, kset in enumerate(gsets):
                        out.append((qt, kset, nkc, kset[-1] == nkc - 1))
                return out, pacc, slots

            steps, paccA, slotsA = head_groups(pA)
            _, paccB, slotsB = head_groups(pB)
            per_head = {hA: (heads[0][1], paccA, slotsA, 0),
                        hB: (heads[1][1], paccB, slotsB, 1536)}

            prev_exp = None
            prev_pv = None
            last_ev = None
            for step_i, (qt, kset, nkc, qt_end) in enumerate(steps):
                @blk.tensor
                def _(t, qt=qt, kset=kset, fr=prev_exp, ev=last_ev,
                      first=first_of_group, aug_dma=aug_dma, MBS_G=MBS_G,
                      pre=all_marks(ct, ("qp",)) if first_of_group else [],
                      pd=pairs_done, hds=list(heads), ph=dict(per_head)):
                    t.wait_ge(ct.sem("aug"), aug_dma)
                    t.wait_ge(ct.sem("mbs"), MBS_G)
                    if first:
                        wait_marks(t, ct, pre)
                    else:
                        wait_marks(t, ct, [fr, ev, pd])
                    for h_, hc, ps, ptb0 in hds:
                        slots = ph[h_][2]
                        for i, kc in enumerate(kset):
                            kc_rel = kc - 4 * qt
                            j0 = max(0, kc_rel) * 128
                            t.matmul(slots[i][:, j0:512],
                                     KA[0:70, hc + kc * 128: hc + (kc + 1) * 128],
                                     QA[0:70, hc + qt * 512 + j0: hc + (qt + 1) * 512],
                                     start=True, stop=True)
                    t.drain().then_inc(ct.sem("qp"), 1)
                smm = mk("qp")
                first_of_group = False

                ndiag = sum(1 for kc in kset if kc - 4 * qt >= 0)
                if ndiag:
                    @blk.vector
                    def _(v, qt=qt, kset=kset, smm=smm, hds=list(heads),
                          ph=dict(per_head)):
                        wait_marks(v, ct, [smm])
                        for h_, hc, ps, ptb0 in hds:
                            slots = ph[h_][2]
                            for i, kc in enumerate(kset):
                                kc_rel = kc - 4 * qt
                                if kc_rel < 0:
                                    continue
                                j0 = kc_rel * 128
                                v.tensor_tensor(out=slots[i][:, j0:j0 + 128],
                                                in0=slots[i][:, j0:j0 + 128],
                                                in1=cons["mtriT"][:], op=ALU.add)
                                if j0 > 0:
                                    v.memset(slots[i][:, 0:j0], -1e30)
                        v.drain().then_inc(ct.sem("qv"), 1)
                    gate = mk("qv")
                else:
                    gate = smm

                @blk.scalar
                def _(s, kset=kset, gate=gate, pv2=prev_pv, hds=list(heads)):
                    wait_marks(s, ct, [gate, pv2])
                    n = len(kset)
                    for h_, hc, ps, ptb0 in hds:
                        s.activation(out=ptb[:, ptb0:ptb0 + n * 512],
                                     in_=ps[:, 0:n * 512], func=AF.Exp)
                    s.drain().then_inc(ct.sem("qs"), 1)
                em = mk("qs")
                prev_exp = em

                @blk.tensor
                def _(t, qt=qt, kset=kset, em=em, nkc=nkc, hds=list(heads),
                      ph=dict(per_head)):
                    wait_marks(t, ct, [em])
                    for h_, hc, ps, ptb0 in hds:
                        pacc = ph[h_][1]
                        hh2 = h_ % 8
                        for i, kc in enumerate(kset):
                            t.matmul(pacc[0:65, :],
                                     V[:, kc * 520 + hh2 * 65: kc * 520 + hh2 * 65 + 65],
                                     ptb[:, ptb0 + i * 512: ptb0 + (i + 1) * 512],
                                     start=(kc == 0), stop=(kc == nkc - 1))
                    t.drain().then_inc(ct.sem("qp"), 1)
                prev_pv = mk("qp")

                if qt_end:
                    @blk.vector
                    def _(v, m=prev_pv, hds=list(heads), ph=dict(per_head)):
                        wait_marks(v, ct, [m])
                        for h_, hc, ps, ptb0 in hds:
                            pacc = ph[h_][1]
                            v.reciprocal(abst[64:65, ptb0 // 3: ptb0 // 3 + 512],
                                         pacc[64:65, :])
                        v.drain().then_inc(ct.sem("qv"), 1)
                    rcp = mk("qv")

                    @blk.tensor
                    def _(t, m=rcp, hds=list(heads), ph=dict(per_head)):
                        wait_marks(t, ct, [m])
                        for h_, hc, ps, ptb0 in hds:
                            slots = ph[h_][2]
                            t.matmul(slots[2][:, 0:512],
                                     cons["ones_f"][64:65, 0:128],
                                     abst[64:65, ptb0 // 3: ptb0 // 3 + 512],
                                     start=True, stop=True)
                        t.drain().then_inc(ct.sem("qp"), 1)
                    bc = mk("qp")

                    @blk.vector
                    def _(v, qt=qt, m=bc, hds=list(heads), ph=dict(per_head)):
                        wait_marks(v, ct, [m])
                        for h_, hc, ps, ptb0 in hds:
                            pacc = ph[h_][1]
                            chunk, half = h_ // 2, h_ % 2
                            v.tensor_copy(attn[half * 64:half * 64 + 64,
                                               chunk * 2048 + qt * 512:
                                               chunk * 2048 + (qt + 1) * 512],
                                          pacc[0:64, :])
                        v.drain()
                        for h_, hc, ps, ptb0 in hds:
                            slots = ph[h_][2]
                            chunk, half = h_ // 2, h_ % 2
                            dst = attn[half * 64:half * 64 + 64,
                                       chunk * 2048 + qt * 512:
                                       chunk * 2048 + (qt + 1) * 512]
                            v.tensor_tensor(out=dst, in0=slots[2][0:64, 0:512],
                                            in1=dst, op=ALU.mult)
                        v.drain().then_inc(ct.sem("qv"), 1)
                    last_ev = mk("qv")
            pairs_done = last_ev
            pair_evacs.append(last_ev)
        pacc_free = pairs_done
        attn_done.append(pair_evacs[-1])

    wo_mm = {}
    stt_marks = []
    ust_t = ct.val("ust")
    for jt in range(4):
        @blk.sync
        def _(sy, jt=jt, ust_t=ust_t, pre=all_marks(ct) if jt == 0 else []):
            if jt >= 1:
                sy.wait_ge(ct.sem("ust"), ust_t)
            else:
                sy.wait_ge(ct.sem("spl"), ct.val("spl"))
                wait_marks(sy, ct, pre)
            for c in range(8):
                sy.dma_start(out=xjt[:, c * 512:(c + 1) * 512],
                             in_=x2d[c * 128:(c + 1) * 128, jt * 512:(jt + 1) * 512]
                             ).then_inc(ct.sem("uld"), 16)
                ct.bump("uld", 16)
        xjt_t = ct.val("uld")

        jt_stt = []
        for d in range(8):
            wsem = "woa" if d % 2 == 0 else "wob"

            @blk.gpsimd
            def _(gp, d=d, jt=jt, wsem=wsem,
                  prev=wo_mm.get((jt, d - 2)) or wo_mm.get((jt - 1, d + 6)),
                  att=list(attn_done)):
                wait_marks(gp, ct, [prev] + (att if (jt == 0 and d < 2) else []))
                src = io["a1_wo"][:, d * 128:(d + 1) * 128].rearrange("(k p) i -> p k i",
                                                                     p=128)
                gp.dma_start(out=wstr[:, (d % 2) * 1024:(d % 2) * 1024 + 1024]
                             .rearrange("p (k i) -> p k i", k=8),
                             in_=src).then_inc(ct.sem(wsem), 16)
                ct.bump(wsem, 16)
            wo_t = ct.val(wsem)

            @blk.tensor
            def _(t, d=d, jt=jt, wo_t=wo_t, wsem=wsem,
                  fr=(stt_marks[-2] if len(stt_marks) >= 2 else None),
                  att=list(attn_done)):
                t.wait_ge(ct.sem(wsem), wo_t)
                wait_marks(t, ct, [fr] + att)
                for k in range(8):
                    t.matmul(pS[d % 2][:],
                             wstr[:, (d % 2) * 1024 + k * 128:
                                  (d % 2) * 1024 + k * 128 + 128],
                             attn[:, k * 2048 + jt * 512: k * 2048 + (jt + 1) * 512],
                             start=(k == 0), stop=(k == 7))
                t.drain().then_inc(ct.sem("qp"), 1)
            wo_mm[(jt, d)] = mk("qp")

            @blk.vector
            def _(v, d=d, jt=jt, xjt_t=xjt_t, m=wo_mm[(jt, d)]):
                v.wait_ge(ct.sem("uld"), xjt_t)
                wait_marks(v, ct, [m])
                v.scalar_tensor_tensor(out=xjt[:, d * 512:(d + 1) * 512],
                                       in0=pS[d % 2][:],
                                       scalar=biases[:, 24 + d:24 + d + 1],
                                       in1=xjt[:, d * 512:(d + 1) * 512],
                                       op0=ALU.add, op1=ALU.add)
                v.drain()
                v.tensor_copy(xb[:, d * 2048 + jt * 512: d * 2048 + (jt + 1) * 512],
                              xjt[:, d * 512:(d + 1) * 512])
                v.drain().then_inc(ct.sem("qv"), 1)
            m2 = mk("qv")
            stt_marks.append(m2)
            jt_stt.append(m2)

        @blk.sync
        def _(sy, jt=jt, ms=list(jt_stt)):
            wait_marks(sy, ct, ms)
            for c in range(8):
                sy.dma_start(out=u_dram[c * 128:(c + 1) * 128, jt * 512:(jt + 1) * 512],
                             in_=xjt[:, c * 512:(c + 1) * 512]).then_inc(ct.sem("ust"), 16)
                ct.bump("ust", 16)
        ust_t = ct.val("ust")
    u_all = ct.val("ust")

    sq_free = None
    sums_marks = []
    for jt in range(4):
        @blk.scalar
        def _(s, jt=jt, ms=list(stt_marks), fr=sq_free):
            wait_marks(s, ct, ms + ([fr] if fr else []))
            for c in range(8):
                sq_out = (dump[:, c * 512:(c + 1) * 512] if c < 4
                          else ptb[:, (c - 4) * 512:(c - 3) * 512])
                s.activation(out=sq_out,
                             in_=xb[:, c * 2048 + jt * 512: c * 2048 + (jt + 1) * 512],
                             func=AF.Square)
            s.drain().then_inc(ct.sem("qs"), 1)
        sq_mark = mk("qs")

        @blk.tensor
        def _(t, jt=jt, sq_mark=sq_mark,
              fr=(stt_marks[-1] if jt == 0 else None)):
            wait_marks(t, ct, [sq_mark, fr])
            for c in range(8):
                t.matmul(pS[0][0:1, :], cons["ones_bf"][:, 0:1],
                         xb[:, c * 2048 + jt * 512: c * 2048 + (jt + 1) * 512],
                         start=(c == 0), stop=(c == 7))
            for c in range(8):
                sq_in = (dump[:, c * 512:(c + 1) * 512] if c < 4
                         else ptb[:, (c - 4) * 512:(c - 3) * 512])
                t.matmul(pS[1][0:1, :], cons["ones_bf"][:, 0:1], sq_in,
                         start=(c == 0), stop=(c == 7))
            t.drain().then_inc(ct.sem("qp"), 1)
        sums_mm = mk("qp")

        @blk.vector
        def _(v, jt=jt, sums_mm=sums_mm):
            wait_marks(v, ct, [sums_mm])
            v.tensor_copy(abst[0:1, jt * 512:(jt + 1) * 512], pS[0][0:1, :])
            v.drain()
            v.tensor_copy(abst[0:1, 2048 + jt * 512: 2048 + (jt + 1) * 512],
                          pS[1][0:1, :])
            v.drain().then_inc(ct.sem("qv"), 1)
        sq_free = mk("qv")
        sums_marks.append(sq_free)

    @blk.vector
    def _(v, ms=list(sums_marks)):
        wait_marks(v, ct, ms)
        v.tensor_scalar_mul(abst[64:65, 2048:4096], abst[0:1, 0:2048], -1.0 / 1024.0)
        v.tensor_scalar_mul(abst[64:65, 0:2048], abst[0:1, 2048:4096], 1.0 / 1024.0)
        v.drain()
        v.scalar_tensor_tensor(out=pA[0:1, 0:2048], in0=abst[64:65, 2048:4096],
                               scalar=-1.0, in1=abst[64:65, 2048:4096],
                               op0=ALU.mult, op1=ALU.mult)
        v.drain()
        v.tensor_tensor(out=abst[32:33, 0:2048], in0=abst[64:65, 0:2048],
                        in1=pA[0:1, 0:2048], op=ALU.add)
        v.drain()
        v.tensor_scalar_add(abst[32:33, 0:2048], abst[32:33, 0:2048], 1e-5)
        v.drain().then_inc(ct.sem("qv"), 1)
    var_mark = mk("qv")

    @blk.scalar
    def _(s, var_mark=var_mark):
        wait_marks(s, ct, [var_mark])
        s.activation(out=abst[64:65, 0:2048], in_=abst[32:33, 0:2048], func=AF.Sqrt)
        s.drain().then_inc(ct.sem("qs"), 1)
    sqrt_mark = mk("qs")

    @blk.vector
    def _(v, sqrt_mark=sqrt_mark):
        wait_marks(v, ct, [sqrt_mark])
        v.reciprocal(abst[64:65, 0:2048], abst[64:65, 0:2048])
        v.drain()
        v.tensor_tensor(out=abst[64:65, 2048:4096], in0=abst[64:65, 2048:4096],
                        in1=abst[64:65, 0:2048], op=ALU.mult)
        v.drain().then_inc(ct.sem("qv"), 1)
    ab_mark = mk("qv")

    ap_mark = None
    st_t = ct.val("x3s")
    for jt in range(4):
        @blk.tensor
        def _(t, jt=jt, ab_mark=ab_mark, fr=ap_mark,
              fr2=(sums_marks[-1] if jt == 0 else None)):
            wait_marks(t, ct, [ab_mark, fr, fr2])
            t.matmul(pS[0][:], cons["ones_f"][64:65, :],
                     abst[64:65, jt * 512:(jt + 1) * 512], start=True, stop=True)
            t.matmul(pS[1][:], cons["ones_f"][64:65, :],
                     abst[64:65, 2048 + jt * 512:2048 + (jt + 1) * 512],
                     start=True, stop=True)
            t.drain().then_inc(ct.sem("qp"), 1)
        bc_mark = mk("qp")

        @blk.sync
        def _(sy, jt=jt, st_t=st_t, u_all=u_all):
            sy.wait_ge(ct.sem("ust"), u_all)
            sy.wait_ge(ct.sem("x3s"), st_t)
            for c in range(8):
                sy.dma_start(out=xjt[:, c * 512:(c + 1) * 512],
                             in_=u_dram[c * 128:(c + 1) * 128, jt * 512:(jt + 1) * 512]
                             ).then_inc(ct.sem("uld"), 16)
                ct.bump("uld", 16)
        u_t = ct.val("uld")

        @blk.vector
        def _(v, jt=jt, bc_mark=bc_mark, u_t=u_t):
            wait_marks(v, ct, [bc_mark])
            v.wait_ge(ct.sem("uld"), u_t)
            for c in range(8):
                v.tensor_tensor(out=xjt[:, c * 512:(c + 1) * 512],
                                in0=xjt[:, c * 512:(c + 1) * 512],
                                in1=pS[0][:], op=ALU.mult)
            v.drain()
            for c in range(8):
                v.tensor_tensor(out=xjt[:, c * 512:(c + 1) * 512],
                                in0=xjt[:, c * 512:(c + 1) * 512],
                                in1=pS[1][:], op=ALU.add)
            v.drain().then_inc(ct.sem("qv"), 1)
        ap_mark = mk("qv")

        @blk.sync
        def _(sy, jt=jt, ap_mark=ap_mark):
            wait_marks(sy, ct, [ap_mark])
            for c in range(8):
                sy.dma_start(out=x3d[c * 128:(c + 1) * 128, jt * 512:(jt + 1) * 512],
                             in_=xjt[:, c * 512:(c + 1) * 512]).then_inc(ct.sem("x3s"), 16)
                ct.bump("x3s", 16)
        st_t = ct.val("x3s")
    return ct.val("x3s")


def emit_f1(ct, blk, io, cons, x3d, y_ap, u_dram, wpq_dram, xb, pA, pB, x3_all, es):
    mk = ct.mk
    wfc = ct.sb("f1_wfc", [128, 8 * 4096], BF16, es=es)
    hbuf = ct.sb("f1_hbuf", [128, 32 * 512], BF16, es=es)
    wpt = ct.sb("f1_wpt", [128, 2 * 4096], BF16, es=es)
    slab = ct.sb("f1_slab", [128, 2 * 1024], F32, es=es)
    qtmp = ct.sb("f1_qtmp", [128, 1024], BF16, es=es)
    sgn = ct.sb("f1_sgn", [128, 1024], BF16, es=es)
    absf = ct.sb("f1_absf", [128, 1024], F32, es=es)
    xjt = ct.sb("f1_xjt", [128, 4096], F32, es=es)
    tmp5 = ct.sb("f1_tmp5", [128, 512], F32, es=es)
    dump = ct.sb("f1_dump", [128, 1024], BF16, es=es)
    abst = ct.sb("f1_abst", [65, 4096], F32, es=es)
    gcol = ct.sb("f1_gcol", [128, 4], F32, es=es)
    g11 = ct.sb("f1_g11", [1, 4], F32, es=es)
    lnc = io["f1_lnc_sb"]
    wfcT, wprojT = io["f1_wfcT"], io["f1_wprojT"]

    cast_mark = None
    for p in range(4):
        @blk.sync
        def _(sy, p=p, cast_mark=cast_mark, pre=all_marks(ct) if p == 0 else []):
            sy.wait_ge(ct.sem("x3s"), x3_all)
            if cast_mark is not None:
                wait_marks(sy, ct, [cast_mark])
            else:
                wait_marks(sy, ct, pre)
            for cc in range(2):
                c = 2 * p + cc
                sy.dma_start(out=xjt[:, cc * 2048:(cc + 1) * 2048],
                             in_=x3d[c * 128:(c + 1) * 128, :]).then_inc(ct.sem("uld"), 16)
                ct.bump("uld", 16)
        lds_t = ct.val("uld")

        @blk.vector
        def _(v, p=p, lds_t=lds_t):
            v.wait_ge(ct.sem("uld"), lds_t)
            for cc in range(2):
                c = 2 * p + cc
                v.tensor_copy(xb[:, c * 2048:(c + 1) * 2048],
                              xjt[:, cc * 2048:(cc + 1) * 2048])
            v.drain().then_inc(ct.sem("qv"), 1)
        cast_mark = mk("qv")
    xb_done = cast_mark

    def slab_src(mat, i):
        if mat == "fc":
            r, cpc = i // 4, i % 4
            return wfcT[r * 128:(r + 1) * 128, cpc * 1024:(cpc + 1) * 1024]
        return wprojT[i * 128:(i + 1) * 128, :]

    @blk.vector
    def _(v):
        v.memset(gcol[:, 0:2], 0.0)
        v.drain().then_inc(ct.sem("qv"), 1)
    ct.bump("qv")

    gm_marks = []
    slab_carry = [None, None]
    for mat, col, sems in (("fc", 0, ("wfa", "wfb")), ("wp", 1, ("wpa", "wpb"))):
        sl_done = {-2: slab_carry[0], -1: slab_carry[1]}
        for i in range(32):
            sem_nm = sems[i % 2]

            @blk.gpsimd
            def _(g, mat=mat, i=i, sem_nm=sem_nm, prev=sl_done.get(i - 2),
                  pre=all_marks(ct, ("qg",)) if (mat == "fc" and i < 2) else []):
                wait_marks(g, ct, ([prev] if prev else []) + pre)
                g.dma_start(out=slab[:, (i % 2) * 1024:(i % 2) * 1024 + 1024],
                            in_=slab_src(mat, i)).then_inc(ct.sem(sem_nm), 16)
                ct.bump(sem_nm, 16)
            dma_t = ct.val(sem_nm)

            @blk.scalar
            def _(s, i=i, sem_nm=sem_nm, dma_t=dma_t, prev=sl_done.get(i - 1)):
                s.wait_ge(ct.sem(sem_nm), dma_t)
                wait_marks(s, ct, [prev] if prev else [])
                s.activation(out=dump[:, 0:1024],
                             in_=slab[:, (i % 2) * 1024:(i % 2) * 1024 + 1024],
                             func=AF.Abs, accum_out=gcol[:, 2:3])
                s.drain().then_inc(ct.sem("qs"), 1)
            ab_mark = mk("qs")

            @blk.vector
            def _(v, col=col, ab_mark=ab_mark):
                wait_marks(v, ct, [ab_mark])
                v.tensor_tensor(out=gcol[:, col:col + 1], in0=gcol[:, col:col + 1],
                                in1=gcol[:, 2:3], op=ALU.add)
                v.drain().then_inc(ct.sem("qv"), 1)
            sl_done[i] = mk("qv")
        gm_marks.append(sl_done[31])
        slab_carry = [sl_done[30], sl_done[31]]

    @blk.tensor
    def _(t, ms=list(gm_marks), pre=all_marks(ct, ("qp",))):
        wait_marks(t, ct, ms + pre)
        t.matmul(pA[0:1, 0:1], gcol[:, 0:1], cons["ones_cf"][:], start=True, stop=True)
        t.matmul(pA[0:1, 1:2], gcol[:, 1:2], cons["ones_cf"][:], start=True, stop=True)
        t.drain().then_inc(ct.sem("qp"), 1)
    gsum_mark = mk("qp")

    @blk.vector
    def _(v, gsum_mark=gsum_mark):
        wait_marks(v, ct, [gsum_mark])
        v.tensor_scalar(out=g11[0:1, 0:2], in0=pA[0:1, 0:2],
                        scalar1=1.0 / NW, scalar2=1e-6, op0=ALU.mult, op1=ALU.add)
        v.drain()
        v.tensor_scalar_mul(g11[0:1, 0:2], g11[0:1, 0:2], 0.5)
        v.drain().then_inc(ct.sem("qv"), 1)
    ghalf_mark = mk("qv")

    gbc_marks = []
    for c in range(2):
        @blk.tensor
        def _(t, c=c, ghalf_mark=ghalf_mark, fr=(gbc_marks[-1] if gbc_marks else None)):
            wait_marks(t, ct, [ghalf_mark, fr])
            t.matmul(pB[:, 0:1], cons["ones_f"][0:1, :], g11[0:1, c:c + 1],
                     start=True, stop=True)
            t.drain().then_inc(ct.sem("qp"), 1)
        m = mk("qp")

        @blk.vector
        def _(v, c=c, m=m):
            wait_marks(v, ct, [m])
            v.tensor_copy(gcol[:, 2 + c:3 + c], pB[:, 0:1])
            v.drain().then_inc(ct.sem("qv"), 1)
        gbc_marks.append(mk("qv"))
    gbc_mark = gbc_marks[-1]

    slab_carry2 = list(slab_carry)

    def emit_quant(mat, col, sems, out_cb):
        sl_done = {-2: slab_carry2[0], -1: slab_carry2[1]}
        for i in range(32):
            sem_nm = sems[i % 2]

            @blk.gpsimd
            def _(g, mat=mat, i=i, sem_nm=sem_nm, prev=sl_done.get(i - 2),
                  gbc_mark=gbc_mark):
                wait_marks(g, ct, ([prev] if prev else []) + [gbc_mark])
                g.dma_start(out=slab[:, (i % 2) * 1024:(i % 2) * 1024 + 1024],
                            in_=slab_src(mat, i)).then_inc(ct.sem(sem_nm), 16)
                ct.bump(sem_nm, 16)
            dma_t = ct.val(sem_nm)

            @blk.vector
            def _(v, i=i, col=col, sem_nm=sem_nm, dma_t=dma_t,
                  prevq=sl_done.get(i - 1)):
                v.wait_ge(ct.sem(sem_nm), dma_t)
                wait_marks(v, ct, [prevq] if prevq else [])
                sl_ = slab[:, (i % 2) * 1024:(i % 2) * 1024 + 1024]
                v.tensor_scalar(out=sgn[:], in0=sl_, scalar1=0.0, scalar2=None,
                                op0=ALU.is_gt)
                v.tensor_scalar(out=qtmp[:], in0=sl_, scalar1=0.0, scalar2=None,
                                op0=ALU.is_lt)
                v.drain()
                v.tensor_tensor(out=sgn[:], in0=sgn[:], in1=qtmp[:],
                                op=ALU.subtract)
                v.drain()
                v.tensor_tensor(out=absf[:], in0=sl_, in1=sgn[:], op=ALU.mult)
                v.drain()
                v.tensor_scalar(out=qtmp[:], in0=absf[:],
                                scalar1=gcol[:, 2 + col:3 + col], scalar2=None,
                                op0=ALU.is_gt)
                v.drain()
                v.tensor_tensor(out=qtmp[:], in0=qtmp[:], in1=sgn[:], op=ALU.mult)
                v.drain().then_inc(ct.sem("qv"), 1)
            q_mark = mk("qv")
            sl_done[i] = out_cb(i, q_mark)
        slab_carry2[0], slab_carry2[1] = sl_done[30], sl_done[31]

    def fc_out(i, q_mark):
        r, cpc = i // 4, i % 4

        @blk.vector
        def _(v, r=r, cpc=cpc):
            v.tensor_copy(wfc[:, r * 4096 + cpc * 1024: r * 4096 + (cpc + 1) * 1024],
                          qtmp[:])
            v.drain().then_inc(ct.sem("qv"), 1)
        return ("qv", ct.bump("qv"))

    def wp_out(i, q_mark):
        @blk.sync
        def _(sy, i=i, q_mark=q_mark):
            wait_marks(sy, ct, [q_mark])
            sy.dma_start(out=wpq_dram[i * 128:(i + 1) * 128, :],
                         in_=qtmp[:]).then_inc(ct.sem("wps"), 16)
        ct.bump("wps", 16)
        u_t = ct.val("wps")

        @blk.vector
        def _(v, u_t=u_t):
            v.wait_ge(ct.sem("wps"), u_t)
            v.drain().then_inc(ct.sem("qv"), 1)
        return ("qv", ct.bump("qv"))

    emit_quant("fc", 0, ("wfa", "wfb"), fc_out)
    emit_quant("wp", 1, ("wpa", "wpb"), wp_out)
    wpq_all = ct.val("wps")
    quant_done = ("qv", ct.val("qv"))

    ust_y = ct.val("ust")
    stt_marks = []
    wp_mm = {}
    for tg in range(4):
        @blk.sync
        def _(sy, tg=tg, fr=(stt_marks[-1] if stt_marks else xb_done), u_prev=ust_y):
            wait_marks(sy, ct, [fr])
            if tg >= 1:
                sy.wait_ge(ct.sem("ust"), u_prev)
            for c in range(8):
                sy.dma_start(out=xjt[:, c * 512:(c + 1) * 512],
                             in_=x3d[c * 128:(c + 1) * 128, tg * 512:(tg + 1) * 512]
                             ).then_inc(ct.sem("uld"), 16)
                ct.bump("uld", 16)
        xjt_t = ct.val("uld")

        ge_marks = {}
        prev_tg_mark = stt_marks[-1] if stt_marks else None
        for hc in range(32):
            q = hc % 4

            @blk.tensor
            def _(t, tg=tg, hc=hc, q=q,
                  fr=ge_marks.get(hc - 4) or prev_tg_mark,
                  qd=quant_done, xb_done=xb_done):
                wait_marks(t, ct, [fr, qd, xb_done])
                for k in range(8):
                    t.matmul(pA[:, q * 512:(q + 1) * 512],
                             wfc[:, k * 4096 + hc * 128: k * 4096 + hc * 128 + 128],
                             xb[:, k * 2048 + tg * 512: k * 2048 + (tg + 1) * 512],
                             start=(k == 0), stop=(k == 7))
                t.drain().then_inc(ct.sem("qp"), 1)
            mmk = mk("qp")

            @blk.scalar
            def _(s, hc=hc, q=q, mmk=mmk):
                wait_marks(s, ct, [mmk])
                s.activation(out=hbuf[:, hc * 512:(hc + 1) * 512],
                             in_=pA[:, q * 512:(q + 1) * 512], func=AF.Gelu)
                s.drain().then_inc(ct.sem("qs"), 1)
            ge_marks[hc] = mk("qs")

        @blk.tensor
        def _(t, tg=tg, ms=[ge_marks[31]]):
            t.wait_ge(ct.sem("ldg"), 80)
            wait_marks(t, ct, ms)
            for hc in range(32):
                t.matmul(pB[0:1, 0:512], cons["ones_bf"][:, 0:1],
                         hbuf[:, hc * 512:(hc + 1) * 512],
                         start=(hc == 0), stop=(hc == 31))
            t.drain().then_inc(ct.sem("qp"), 1)
        sum_mark = mk("qp")

        sq_marks = {}
        for hc in range(32):
            @blk.vector
            def _(v, hc=hc, fr=sq_marks.get(hc - 2), ms=[ge_marks[31]]):
                wait_marks(v, ct, [fr] if fr else ms)
                v.tensor_tensor(out=dump[:, (hc % 2) * 512:(hc % 2) * 512 + 512],
                                in0=hbuf[:, hc * 512:(hc + 1) * 512],
                                in1=hbuf[:, hc * 512:(hc + 1) * 512], op=ALU.mult)
                v.drain().then_inc(ct.sem("qv"), 1)
            sqm = mk("qv")

            @blk.tensor
            def _(t, hc=hc, sqm=sqm):
                wait_marks(t, ct, [sqm])
                t.matmul(pB[0:1, 512:1024], cons["ones_bf"][:, 0:1],
                         dump[:, (hc % 2) * 512:(hc % 2) * 512 + 512],
                         start=(hc == 0), stop=(hc == 31))
                t.drain().then_inc(ct.sem("qp"), 1)
            sq_marks[hc] = mk("qp")
        sumsq_mark = sq_marks[31]

        @blk.vector
        def _(v, sum_mark=sum_mark, sumsq_mark=sumsq_mark):
            wait_marks(v, ct, [sum_mark, sumsq_mark])
            v.tensor_scalar_mul(abst[64:65, 512:1024], pB[0:1, 0:512], -1.0 / 4096.0)
            v.tensor_scalar_mul(abst[64:65, 0:512], pB[0:1, 512:1024], 1.0 / 4096.0)
            v.drain()
            v.scalar_tensor_tensor(out=pA[0:1, 0:512], in0=abst[64:65, 512:1024],
                                   scalar=-1.0, in1=abst[64:65, 512:1024],
                                   op0=ALU.mult, op1=ALU.mult)
            v.drain()
            v.tensor_tensor(out=abst[32:33, 0:512], in0=abst[64:65, 0:512],
                            in1=pA[0:1, 0:512], op=ALU.add)
            v.drain()
            v.tensor_scalar_add(abst[32:33, 0:512], abst[32:33, 0:512], 1e-5)
            v.drain().then_inc(ct.sem("qv"), 1)
        var_mark = mk("qv")

        @blk.scalar
        def _(s, var_mark=var_mark):
            wait_marks(s, ct, [var_mark])
            s.activation(out=abst[64:65, 0:512], in_=abst[32:33, 0:512], func=AF.Sqrt)
            s.drain().then_inc(ct.sem("qs"), 1)
        sqrt_mark = mk("qs")

        @blk.vector
        def _(v, sqrt_mark=sqrt_mark):
            wait_marks(v, ct, [sqrt_mark])
            v.reciprocal(abst[64:65, 0:512], abst[64:65, 0:512])
            v.drain()
            v.tensor_tensor(out=abst[64:65, 512:1024], in0=abst[64:65, 512:1024],
                            in1=abst[64:65, 0:512], op=ALU.mult)
            v.drain().then_inc(ct.sem("qv"), 1)
        ab_mark = mk("qv")

        @blk.tensor
        def _(t, ab_mark=ab_mark):
            wait_marks(t, ct, [ab_mark])
            t.matmul(pA[:, 0:512], cons["ones_f"][64:65, :],
                     abst[64:65, 0:512], start=True, stop=True)
            t.matmul(pA[:, 512:1024], cons["ones_f"][64:65, :],
                     abst[64:65, 512:1024], start=True, stop=True)
            t.drain().then_inc(ct.sem("qp"), 1)
        bc_mark = mk("qp")

        apl_mark = None
        for hc in range(32):
            @blk.vector
            def _(v, hc=hc, bc_mark=bc_mark):
                wait_marks(v, ct, [bc_mark])
                v.tensor_tensor(out=tmp5[:], in0=hbuf[:, hc * 512:(hc + 1) * 512],
                                in1=pA[:, 0:512], op=ALU.mult)
                v.drain()
                v.tensor_tensor(out=tmp5[:], in0=tmp5[:], in1=pA[:, 512:1024],
                                op=ALU.add)
                v.drain()
                v.tensor_scalar(out=hbuf[:, hc * 512:(hc + 1) * 512], in0=tmp5[:],
                                scalar1=lnc[:, hc:hc + 1],
                                scalar2=lnc[:, 32 + hc:33 + hc],
                                op0=ALU.mult, op1=ALU.add)
                v.drain().then_inc(ct.sem("qv"), 1)
            apl_mark = mk("qv")

        jt_stt = []
        for d in range(8):
            sem_nm = "wta" if d % 2 == 0 else "wtb"

            @blk.sync
            def _(sy, d=d, tg=tg, sem_nm=sem_nm,
                  prev=wp_mm.get((tg, d - 2)) or wp_mm.get((tg - 1, d + 6)),
                  wpq_all=wpq_all):
                sy.wait_ge(ct.sem("wps"), wpq_all)
                wait_marks(sy, ct, [prev] if prev else [])
                src = wpq_dram[:, d * 128:(d + 1) * 128].rearrange("(k p) i -> p k i",
                                                                  p=128)
                sy.dma_start(out=wpt[:, (d % 2) * 4096:(d % 2) * 4096 + 4096]
                             .rearrange("p (k i) -> p k i", k=32),
                             in_=src).then_inc(ct.sem(sem_nm), 16)
                ct.bump(sem_nm, 16)
            wp_t = ct.val(sem_nm)

            @blk.tensor
            def _(t, d=d, tg=tg, wp_t=wp_t, sem_nm=sem_nm, apl_mark=apl_mark,
                  fr=(stt_marks[-2] if len(stt_marks) >= 2 else None)):
                t.wait_ge(ct.sem(sem_nm), wp_t)
                wait_marks(t, ct, [apl_mark, fr])
                for k in range(32):
                    t.matmul(pB[:, (d % 2) * 512:(d % 2) * 512 + 512],
                             wpt[:, (d % 2) * 4096 + k * 128:
                                 (d % 2) * 4096 + k * 128 + 128],
                             hbuf[:, k * 512:(k + 1) * 512],
                             start=(k == 0), stop=(k == 31))
                t.drain().then_inc(ct.sem("qp"), 1)
            wp_mm[(tg, d)] = mk("qp")

            @blk.vector
            def _(v, d=d, tg=tg, m=wp_mm[(tg, d)], xjt_t=xjt_t):
                v.wait_ge(ct.sem("uld"), xjt_t)
                wait_marks(v, ct, [m])
                v.tensor_tensor(out=xjt[:, d * 512:(d + 1) * 512],
                                in0=pB[:, (d % 2) * 512:(d % 2) * 512 + 512],
                                in1=xjt[:, d * 512:(d + 1) * 512], op=ALU.add)
                v.drain()
                v.tensor_copy(xb[:, d * 2048 + tg * 512: d * 2048 + (tg + 1) * 512],
                              xjt[:, d * 512:(d + 1) * 512])
                v.drain().then_inc(ct.sem("qv"), 1)
            m2 = mk("qv")
            stt_marks.append(m2)
            jt_stt.append(m2)

        @blk.sync
        def _(sy, tg=tg, ms=list(jt_stt)):
            wait_marks(sy, ct, ms)
            for c in range(8):
                sy.dma_start(out=u_dram[c * 128:(c + 1) * 128, tg * 512:(tg + 1) * 512],
                             in_=xjt[:, c * 512:(c + 1) * 512]).then_inc(ct.sem("ust"), 16)
                ct.bump("ust", 16)
        ust_y = ct.val("ust")
    u_all = ct.val("ust")

    sq_free = None
    sums_marks = []
    for jt in range(4):
        @blk.scalar
        def _(s, jt=jt, ms=list(stt_marks), fr=sq_free):
            wait_marks(s, ct, ms + ([fr] if fr else []))
            for c in range(8):
                s.activation(out=hbuf[:, c * 512:(c + 1) * 512],
                             in_=xb[:, c * 2048 + jt * 512: c * 2048 + (jt + 1) * 512],
                             func=AF.Square)
            s.drain().then_inc(ct.sem("qs"), 1)
        sq_mark = mk("qs")

        @blk.tensor
        def _(t, jt=jt, sq_mark=sq_mark,
              fr=(stt_marks[-1] if jt == 0 else None)):
            wait_marks(t, ct, [sq_mark, fr])
            for c in range(8):
                t.matmul(pA[0:1, 0:512], cons["ones_bf"][:, 0:1],
                         xb[:, c * 2048 + jt * 512: c * 2048 + (jt + 1) * 512],
                         start=(c == 0), stop=(c == 7))
            for c in range(8):
                t.matmul(pB[0:1, 0:512], cons["ones_bf"][:, 0:1],
                         hbuf[:, c * 512:(c + 1) * 512],
                         start=(c == 0), stop=(c == 7))
            t.drain().then_inc(ct.sem("qp"), 1)
        sums_mm = mk("qp")

        @blk.vector
        def _(v, jt=jt, sums_mm=sums_mm):
            wait_marks(v, ct, [sums_mm])
            v.tensor_copy(abst[0:1, jt * 512:(jt + 1) * 512], pA[0:1, 0:512])
            v.drain()
            v.tensor_copy(abst[0:1, 2048 + jt * 512: 2048 + (jt + 1) * 512],
                          pB[0:1, 0:512])
            v.drain().then_inc(ct.sem("qv"), 1)
        sq_free = mk("qv")
        sums_marks.append(sq_free)

    @blk.vector
    def _(v, ms=list(sums_marks)):
        wait_marks(v, ct, ms)
        v.tensor_scalar_mul(abst[64:65, 2048:4096], abst[0:1, 0:2048], -1.0 / 1024.0)
        v.tensor_scalar_mul(abst[64:65, 0:2048], abst[0:1, 2048:4096], 1.0 / 1024.0)
        v.drain()
        v.scalar_tensor_tensor(out=pA[0:1, 0:2048], in0=abst[64:65, 2048:4096],
                               scalar=-1.0, in1=abst[64:65, 2048:4096],
                               op0=ALU.mult, op1=ALU.mult)
        v.drain()
        v.tensor_tensor(out=abst[32:33, 0:2048], in0=abst[64:65, 0:2048],
                        in1=pA[0:1, 0:2048], op=ALU.add)
        v.drain()
        v.tensor_scalar_add(abst[32:33, 0:2048], abst[32:33, 0:2048], 1e-5)
        v.drain().then_inc(ct.sem("qv"), 1)
    var_mark = mk("qv")

    @blk.scalar
    def _(s, var_mark=var_mark):
        wait_marks(s, ct, [var_mark])
        s.activation(out=abst[64:65, 0:2048], in_=abst[32:33, 0:2048], func=AF.Sqrt)
        s.drain().then_inc(ct.sem("qs"), 1)
    sqrt_mark = mk("qs")

    @blk.vector
    def _(v, sqrt_mark=sqrt_mark):
        wait_marks(v, ct, [sqrt_mark])
        v.reciprocal(abst[64:65, 0:2048], abst[64:65, 0:2048])
        v.drain()
        v.tensor_tensor(out=abst[64:65, 2048:4096], in0=abst[64:65, 2048:4096],
                        in1=abst[64:65, 0:2048], op=ALU.mult)
        v.drain().then_inc(ct.sem("qv"), 1)
    ab_mark2 = mk("qv")

    ap_mark = None
    st_t = ct.val("yst")
    for jt in range(4):
        @blk.tensor
        def _(t, jt=jt, ab_mark2=ab_mark2, fr=ap_mark,
              fr2=(sums_marks[-1] if jt == 0 else None)):
            wait_marks(t, ct, [ab_mark2, fr, fr2])
            t.matmul(pA[:, 0:512], cons["ones_f"][64:65, :],
                     abst[64:65, jt * 512:(jt + 1) * 512], start=True, stop=True)
            t.matmul(pB[:, 0:512], cons["ones_f"][64:65, :],
                     abst[64:65, 2048 + jt * 512:2048 + (jt + 1) * 512],
                     start=True, stop=True)
            t.drain().then_inc(ct.sem("qp"), 1)
        bc2 = mk("qp")

        @blk.sync
        def _(sy, jt=jt, st_t=st_t, u_all=u_all):
            sy.wait_ge(ct.sem("ust"), u_all)
            sy.wait_ge(ct.sem("yst"), st_t)
            for c in range(8):
                sy.dma_start(out=xjt[:, c * 512:(c + 1) * 512],
                             in_=u_dram[c * 128:(c + 1) * 128, jt * 512:(jt + 1) * 512]
                             ).then_inc(ct.sem("uld"), 16)
                ct.bump("uld", 16)
        u_t = ct.val("uld")

        @blk.vector
        def _(v, jt=jt, bc2=bc2, u_t=u_t):
            wait_marks(v, ct, [bc2])
            v.wait_ge(ct.sem("uld"), u_t)
            for c in range(8):
                v.tensor_tensor(out=xjt[:, c * 512:(c + 1) * 512],
                                in0=xjt[:, c * 512:(c + 1) * 512],
                                in1=pA[:, 0:512], op=ALU.mult)
            v.drain()
            for c in range(8):
                v.tensor_tensor(out=xjt[:, c * 512:(c + 1) * 512],
                                in0=xjt[:, c * 512:(c + 1) * 512],
                                in1=pB[:, 0:512], op=ALU.add)
            v.drain().then_inc(ct.sem("qv"), 1)
        ap_mark = mk("qv")

        @blk.sync
        def _(sy, jt=jt, ap_mark=ap_mark):
            wait_marks(sy, ct, [ap_mark])
            for c in range(8):
                sy.dma_start(out=y_ap[c * 128:(c + 1) * 128, jt * 512:(jt + 1) * 512],
                             in_=xjt[:, c * 512:(c + 1) * 512]).then_inc(ct.sem("yst"), 16)
                ct.bump("yst", 16)
        st_t = ct.val("yst")

    @blk.vector
    def _(v):
        v.wait_ge(ct.sem("yst"), ct.val("yst"))
        v.drain().then_inc(ct.sem("qv"), 1)
    ct.bump("qv")


def build_fused():
    nc = bass.Bass("TRN2", target_bir_lowering=False)
    dr = lambda nm, shp, dt=F32, kind="ExternalInput": nc.dram_tensor(nm, shp, dt, kind=kind).ap()
    x_ap = dr("x", [2048, 1024])
    io = {}
    for nm, shp in (("a0_wq", [1024, 1024]), ("a0_wk", [1024, 1024]),
                    ("a0_wv", [1024, 1024]), ("a0_wo", [1024, 1024]),
                    ("a0_biases", [128, 32]),
                    ("f0_gate_wT", [1024, 8]), ("f0_wg", [8192, 1408]),
                    ("f0_wu", [8192, 1408]), ("f0_wd", [11264, 1024]),
                    ("f0_swg", [1024, 2816]), ("f0_swu", [1024, 2816]),
                    ("f0_swd", [2816, 1024]),
                    ("a1_wq", [1024, 1024]), ("a1_wk", [1024, 1024]),
                    ("a1_wv", [1024, 1024]), ("a1_wo", [1024, 1024]),
                    ("a1_biases", [128, 32]), ("a1_bvrow", [1, 1024]),
                    ("f1_wfcT", [1024, 4096]), ("f1_wprojT", [4096, 1024]),
                    ("f1_lncols", [128, 64])):
        io[nm] = dr(nm, shp)
    cs = make_consts()
    for k, v in cs.items():
        io["c_" + k] = dr("c_" + k, list(v.shape))
    y_ap = dr("y_fm", [1024, 2048], F32, "ExternalOutput")
    x2d = dr("x2_scr", [1024, 2048], F32, "ExternalOutput")
    x3d = dr("x3_scr", [1024, 2048], F32, "ExternalOutput")
    u_dram = dr("u_scr", [1024, 2048], F32, "ExternalOutput")
    wpq_dram = dr("wpq_scr", [4096, 1024], BF16, "ExternalOutput")

    ct = Ctx(nc)
    with ct.es:
        xb = ct.sb("xb", [128, 16384], BF16)
        cons = dict(
            identf=ct.sb("s_identf", [128, 128], F32),
            identb=ct.sb("s_identb", [128, 128], BF16),
            ones_f=ct.sb("s_ones_f", [65, 128], F32),
            ones_bf=ct.sb("s_ones_bf", [128, 1], BF16),
            ones_cf=ct.sb("s_ones_cf", [128, 1], F32),
            ind_sc=ct.sb("s_ind_sc", [128, 8], BF16),
            ind_bc=ct.sb("s_ind_bc", [4, 256], BF16),
            sel8=ct.sb("s_sel8", [8, 1024], F32),
            mtri=ct.sb("s_mtri", [128, 128], F32),
            mtriT=ct.sb("s_mtriT", [128, 128], F32),
        )
        a0_bias = ct.sb("a0_bias_sb", [128, 32], F32)
        a1_bias = ct.sb("a1_bias_sb", [128, 32], F32)
        f1_lnc = ct.sb("f1_lnc_sb", [128, 64], F32)
        f0_gwf = ct.sb("f0_gwf_sb", [128, 64], F32)
        io["a1_biases_sb"] = a1_bias
        io["f1_lnc_sb"] = f1_lnc
        pA = ct.ps("pA", [128, 2048], F32)
        pB = ct.ps("pB", [128, 2048], F32)
        bank = [pA[:, i * 512:(i + 1) * 512] for i in range(4)] + \
               [pB[:, i * 512:(i + 1) * 512] for i in range(4)]
        blk = ct.es.enter_context(nc.Block())
        for s in ("lds", "ldg", "xld", "wqa", "wqb", "wda", "wdb", "aug",
                  "wqs", "woa", "wob", "spl", "x3s", "uld", "ust", "yst",
                  "wfa", "wfb", "wpa", "wpb", "wta", "wtb", "wps",
                  "qp", "qv", "qs", "qg"):
            ct.sem(s)

        @blk.sync
        def _(sy):
            for t_sb, src in ((cons["ones_f"], io["c_ones_f"]),
                              (cons["ones_cf"], io["c_ones_cf"]),
                              (cons["identf"], io["c_identf"]),
                              (cons["mtri"], io["c_mtri"]),
                              (cons["mtriT"], io["c_mtriT"]),
                              (cons["sel8"], io["c_sel8"]),
                              (a0_bias, io["a0_biases"]),
                              (a1_bias, io["a1_biases"]),
                              (f1_lnc, io["f1_lncols"])):
                sy.dma_start(out=t_sb[:], in_=src).then_inc(ct.sem("lds"), 16)
                ct.bump("lds", 16)

        @blk.gpsimd
        def _(g):
            for t_sb, src in ((cons["identb"], io["c_identf"]),
                              (cons["ones_bf"], io["c_ones_bf"]),
                              (cons["ind_sc"], io["c_ind_sc"]),
                              (cons["ind_bc"], io["c_ind_bc"])):
                g.dma_start(out=t_sb[:], in_=src).then_inc(ct.sem("ldg"), 16)
                ct.bump("ldg", 16)
            src = io["f0_gate_wT"].rearrange("(k p) e -> p k e", p=128)
            g.dma_start(out=f0_gwf[:].rearrange("p (k e) -> p k e", k=8),
                        in_=src).then_inc(ct.sem("ldg"), 16)
            ct.bump("ldg", 16)
        CONSTS_T = ct.val("lds")
        LDGC_T = ct.val("ldg")

        es_A = contextlib.ExitStack()
        with es_A:
            x_fm = ct.sb("x_fm", [128, 16384], F32, es=es_A)
            abst = ct.sb("ab_abst", [65, 4096], F32, es=es_A)
            es_a0 = contextlib.ExitStack()
            with es_a0:
                sqt0 = ct.sb("a0_sqt", [128, 4096], BF16, es=es_a0)
                m1 = emit_a0(ct, blk, x_ap, io, cons, x_fm, xb, bank, es_a0,
                             a0_bias, CONSTS_T, LDGC_T)
                ap0 = emit_postln(ct, blk, x_fm, xb, cons, pA, pB, sqt0, abst, m1)
                @blk.vector
                def _(v, ap0=ap0):
                    wait_marks(v, ct, [ap0])
                    for c in range(8):
                        v.tensor_copy(xb[:, c * 2048:(c + 1) * 2048],
                                      x_fm[:, c * 2048:(c + 1) * 2048])
                    v.drain().then_inc(ct.sem("qv"), 1)
                m2 = [ct.mk("qv")]
            emit_barrier_dma(ct, blk, [])
            m3, act_f0 = emit_f0(ct, blk, io, cons, x_fm, xb, pA, pB, m2, es_A,
                                 f0_gwf, CONSTS_T, LDGC_T)
            ap1 = emit_postln(ct, blk, x_fm, xb, cons, pA, pB,
                              act_f0[:, 0:4096], abst, m3)

            @blk.vector
            def _(v, ap1=ap1):
                wait_marks(v, ct, [ap1])
                for c in range(8):
                    v.tensor_copy(xb[:, c * 2048:(c + 1) * 2048],
                                  x_fm[:, c * 2048:(c + 1) * 2048])
                v.drain().then_inc(ct.sem("qv"), 1)
            m4 = [ct.mk("qv")]

            @blk.sync
            def _(sy, ap1=ap1):
                wait_marks(sy, ct, [ap1])
                for c in range(8):
                    sy.dma_start(out=x2d[c * 128:(c + 1) * 128, :],
                                 in_=x_fm[:, c * 2048:(c + 1) * 2048]
                                 ).then_inc(ct.sem("spl"), 16)
                    ct.bump("spl", 16)

        emit_barrier_dma(ct, blk, [("spl", ct.val("spl"))])

        es_B = contextlib.ExitStack()
        with es_B:
            x3_all = emit_a1(ct, blk, io, cons, x2d, x3d, u_dram, xb, pA, pB,
                             m4, es_B, CONSTS_T, LDGC_T)

        emit_barrier_dma(ct, blk, [("x3s", ct.val("x3s")), ("ust", ct.val("ust"))])

        es_C = contextlib.ExitStack()
        with es_C:
            emit_f1(ct, blk, io, cons, x3d, y_ap, u_dram, wpq_dram, xb, pA, pB,
                    ct.val("x3s"), es_C)
    return nc


def emit_barrier_dma(ct, blk, dma_marks):
    marks = {nm: ct.val(nm) for nm in ENG_SEMS}

    def waits(eng, own):
        for nm, v in marks.items():
            if nm != own and v > 0:
                eng.wait_ge(ct.sem(nm), v)
        for nm, v in dma_marks:
            if v > 0:
                eng.wait_ge(ct.sem(nm), v)

    @blk.tensor
    def _(t):
        waits(t, "qp")
        t.drain().then_inc(ct.sem("qp"), 1)
    ct.bump("qp")

    @blk.vector
    def _(v):
        waits(v, "qv")
        v.drain().then_inc(ct.sem("qv"), 1)
    ct.bump("qv")

    @blk.scalar
    def _(s):
        waits(s, "qs")
        s.drain().then_inc(ct.sem("qs"), 1)
    ct.bump("qs")

    @blk.gpsimd
    def _(g):
        waits(g, "qg")
        g.drain().then_inc(ct.sem("qg"), 1)
    ct.bump("qg")


def host_inputs(inp):
    cs = make_consts()
    perm = np.concatenate([np.arange(64) + h * 64 for r in range(4)
                           for h in (r, r + 4, r + 8, r + 12)])
    a0_biases = np.zeros((128, 32), np.float32)
    for j, nm in enumerate(("bq", "bk", "bv")):
        a0_biases[:, j * 8:(j + 1) * 8] = np.asarray(inp["L0_attn_" + nm])[perm].reshape(8, 128).T
    a0_biases[:, 24:32] = np.asarray(inp["L0_attn_bo"]).reshape(8, 128).T
    a1_biases = np.zeros((128, 32), np.float32)
    a1_biases[:, 0:8] = np.asarray(inp["L1_attn_bq"]).reshape(8, 128).T
    a1_biases[:, 8:16] = np.asarray(inp["L1_attn_bk"]).reshape(8, 128).T
    a1_biases[:, 24:32] = np.asarray(inp["L1_attn_bo"]).reshape(8, 128).T
    lncols = np.zeros((128, 64), np.float32)
    lncols[:, 0:32] = np.asarray(inp["L1_ffn_ln_w"]).reshape(32, 128).T
    lncols[:, 32:64] = np.asarray(inp["L1_ffn_ln_b"]).reshape(32, 128).T
    base = dict(
        a0_wq=np.ascontiguousarray(np.asarray(inp["L0_attn_Wq"])[:, perm]),
        a0_wk=np.ascontiguousarray(np.asarray(inp["L0_attn_Wk"])[:, perm]),
        a0_wv=np.ascontiguousarray(np.asarray(inp["L0_attn_Wv"])[:, perm]),
        a0_wo=np.ascontiguousarray(np.asarray(inp["L0_attn_Wo"])[perm, :]),
        a0_biases=a0_biases,
        f0_gate_wT=np.ascontiguousarray(np.asarray(inp["L0_ffn_gate_w"]).T),
        f0_wg=np.ascontiguousarray(np.asarray(inp["L0_ffn_wg"]).reshape(8192, 1408)),
        f0_wu=np.ascontiguousarray(np.asarray(inp["L0_ffn_wu"]).reshape(8192, 1408)),
        f0_wd=np.ascontiguousarray(np.asarray(inp["L0_ffn_wd"]).reshape(11264, 1024)),
        f0_swg=np.ascontiguousarray(np.asarray(inp["L0_ffn_swg"])),
        f0_swu=np.ascontiguousarray(np.asarray(inp["L0_ffn_swu"])),
        f0_swd=np.ascontiguousarray(np.asarray(inp["L0_ffn_swd"])),
        a1_wq=np.ascontiguousarray(np.asarray(inp["L1_attn_Wq"])),
        a1_wk=np.ascontiguousarray(np.asarray(inp["L1_attn_Wk"])),
        a1_wv=np.ascontiguousarray(np.asarray(inp["L1_attn_Wv"])),
        a1_wo=np.ascontiguousarray(np.asarray(inp["L1_attn_Wo"])),
        a1_biases=a1_biases,
        a1_bvrow=np.ascontiguousarray(np.asarray(inp["L1_attn_bv"]).reshape(1, 1024)),
        f1_wfcT=np.ascontiguousarray(np.asarray(inp["L1_ffn_W_fc"]).T),
        f1_wprojT=np.ascontiguousarray(np.asarray(inp["L1_ffn_W_proj"]).T),
        f1_lncols=lncols,
        **{"c_" + k: np.ascontiguousarray(v) for k, v in cs.items()})
    x = np.asarray(inp["L0_attn_x"])
    return [dict(base, x=np.ascontiguousarray(x[b])) for b in range(8)]




_PROG = []


def kernel(**inputs):
    inp = {k: np.asarray(v) for k, v in inputs.items()}
    if not _PROG:
        _PROG.append(build_fused())
    nc = _PROG[0]
    in_maps = host_inputs(inp)
    res = run_bass_kernel_spmd(nc, in_maps, core_ids=list(range(8)))
    return np.stack([r["y_fm"].T for r in res.results]).astype(np.float32)
